# version 02
# baseline (speedup 1.0000x reference)
import numpy as np
import ml_dtypes
import concourse.bass as bass
import concourse.mybir as mybir
from concourse.bass_utils import run_bass_kernel_spmd

F32 = mybir.dt.float32
BF16 = mybir.dt.bfloat16
AF = mybir.ActivationFunctionType
ALU = mybir.AluOpType
AX = mybir.AxisListType
bfd = ml_dtypes.bfloat16

H = 16
D = 64
DM = 1024
FF = 4096
NE = 8
CT = 512
NGC = CT // 8
NQT = 4
NHALF = 2
JT_HALF = 8
ITERS_HALF = NE * NQT


class Evt:
    def __init__(self, stack, nc, name, step=1):
        self.h = stack.enter_context(nc.semaphore(name))
        self.step = step
        self.n = 0

    def inc(self, instr):
        instr.then_inc(self.h, self.step)
        self.n += 1
        return instr

    def wait(self, eng, i):
        assert 0 <= i < self.n or True
        eng.wait_ge(self.h, self.step * (i + 1))


def build():
    nc = bass.Bass("TRN2", target_bir_lowering=False,
                   detect_race_conditions=False)
    x_own = nc.dram_tensor("x_own", [2048, DM], F32, kind="ExternalInput").ap()
    x_res = nc.dram_tensor("x_res", [16, 128, DM], F32, kind="ExternalInput").ap()
    Wq = nc.dram_tensor("Wq", [DM, DM], F32, kind="ExternalInput").ap()
    Wk = nc.dram_tensor("Wk", [DM, DM], F32, kind="ExternalInput").ap()
    Wv = nc.dram_tensor("Wv", [DM, DM], F32, kind="ExternalInput").ap()
    Wo = nc.dram_tensor("Wo", [DM, DM], F32, kind="ExternalInput").ap()
    bq = nc.dram_tensor("bq", [DM], F32, kind="ExternalInput").ap()
    bk = nc.dram_tensor("bk", [DM], F32, kind="ExternalInput").ap()
    bv = nc.dram_tensor("bv", [DM], F32, kind="ExternalInput").ap()
    bo = nc.dram_tensor("bo", [DM], F32, kind="ExternalInput").ap()
    W1 = nc.dram_tensor("W1", [NE, DM, FF], F32, kind="ExternalInput").ap()
    b1 = nc.dram_tensor("b1", [NE, FF], F32, kind="ExternalInput").ap()
    W2 = nc.dram_tensor("W2", [NE, FF, DM], F32, kind="ExternalInput").ap()
    b2 = nc.dram_tensor("b2", [NE, DM], F32, kind="ExternalInput").ap()
    Wr = nc.dram_tensor("Wr", [DM, NE], F32, kind="ExternalInput").ap()
    br = nc.dram_tensor("br", [NE], F32, kind="ExternalInput").ap()
    ident_hbm = nc.dram_tensor("ident_hbm", [128, 128], BF16, kind="ExternalInput").ap()
    mask_hbm = nc.dram_tensor("mask_hbm", [128, 128], F32, kind="ExternalInput").ap()
    out = nc.dram_tensor("out", [16, 128, DM], F32, kind="ExternalOutput").ap()

    from contextlib import ExitStack
    st = ExitStack()
    with st:
        sbt = lambda *a: st.enter_context(nc.sbuf_tensor(*a))
        pst = lambda *a: st.enter_context(nc.psum_tensor(*a))

        ident = sbt("ident", [128, 128], BF16)
        mask8 = sbt("mask8", [128, 128], F32)
        bqkv_sb = sbt("bqkv_sb", [64, 3, H], F32)
        bo_rep = sbt("bo_rep", [128, DM], F32)
        br_rep = sbt("br_rep", [128, NE], F32)
        Wr_sb = sbt("Wr_sb", [128, 8, NE], BF16)
        b1_sb = sbt("b1_sb", [128, NE, 32], F32)
        b2_sb = sbt("b2_sb", [NE, DM], BF16)
        eps_t = sbt("eps_t", [128, 1], F32)
        ln_s = sbt("ln_s", [128, 2, 4], F32)
        sqd = sbt("sqd", [128, DM], F32)
        acc = sbt("acc", [128, JT_HALF, DM], F32)
        WA = [sbt(f"wa{i}", [128, 8, DM], BF16) for i in range(4)]
        x_sb = sbt("x_sb", [128, 2, DM], F32)
        xc_sb = sbt("xc_sb", [128, 2, DM], F32)
        xln_sb = sbt("xln_sb", [128, 2, DM], BF16)
        xlnT = sbt("xlnT", [128, 8, CT], BF16)
        preT2 = sbt("preT2", [128, 4, 8, 128], BF16)
        T_sb = sbt("T_sb", [128, 2, 128], F32)
        P_sb = sbt("P_sb", [128, 2, 128], F32)
        Pn_sb = sbt("Pn_sb", [128, 2, 128], BF16)
        PT_sb = sbt("PT_sb", [128, 2, 128], BF16)
        Vg_sb = sbt("Vg_sb", [128, 2, D], BF16)
        smax = sbt("smax", [128, 2, 1], F32)
        ssum = sbt("ssum", [128, 2, 1], F32)
        srec = sbt("srec", [128, 2, 1], F32)
        S_gates = sbt("S_gates", [128, JT_HALF, NE], F32)
        rlog = sbt("rlog", [128, NE], F32)
        rmsk = sbt("rmsk", [128, NE], F32)
        rneg = sbt("rneg", [128, 1], F32)
        rsum = sbt("rsum", [128, 1], F32)
        rgat = sbt("rgat", [128, 1], F32)
        S_bf = sbt("S_bf", [128, NE], BF16)
        ST_sb = sbt("ST_sb", [NE, 128], BF16)
        qkv_region = sbt("qkv_region", [128, 48 * 1024 // 4], F32)
        base = nc.lookup_mloc(qkv_region).addr
        q_all = nc.alloc_sbuf_tensor_at("q_all", [64, NGC, H, 8], BF16, offset=base)
        k_all = nc.alloc_sbuf_tensor_at("k_all", [64, NGC, H, 8], BF16, offset=base + 16 * 1024)
        v_all = nc.alloc_sbuf_tensor_at("v_all", [64, NGC, H, 8], BF16, offset=base + 32 * 1024)
        xfT = nc.alloc_sbuf_tensor_at("xfT", [128, 8, 1024], BF16, offset=base)
        h_sb = nc.alloc_sbuf_tensor_at("h_sb", [128, 8, 1024], BF16, offset=base + 16 * 1024)

        PP = [pst("pp0", [128, 512], F32), pst("pp1", [128, 512], F32)]
        SB_ps = [pst("S0", [128, 512], F32), pst("S1", [128, 512], F32)]
        tp_ps = pst("tp", [128, 256], BF16)
        PT_ps = pst("PT0", [128, 128], BF16)
        VG_ps = pst("VG0", [128, D], BF16)
        OT_ps = pst("OT0", [64, 128], F32)

        ev_names = [
            "loads", "wload", "store", "init", "bq8",
            "xload", "xresload", "negsum", "negmean", "ssq", "std", "rstd",
            "xln", "xcw", "tp", "tpevac", "qkvfill", "qkvevac",
            "Smm", "VGmm", "PTmm", "OTmm", "Tmax", "P", "Psum", "Prec", "Pn",
            "grpcopies", "OTevac", "wofill", "woevac", "boadd",
            "rfill", "rlogv", "rmax", "rexp", "rsum_", "rgate", "rmask",
            "rsg", "rsbf", "stT", "stTevac", "b2f", "b2acc",
            "hfill", "hevac", "yfill", "yacc",
        ]
        EV = {n: Evt(st, nc, n, 16 if n in ("loads", "wload", "store", "xload", "xresload") else 1)
              for n in ev_names}

        block = st.enter_context(nc.Block())

        @block.sync
        def _(sync):
            EV["loads"].inc(sync.dma_start(out=ident[:], in_=ident_hbm))
            EV["loads"].inc(sync.dma_start(out=mask8[:], in_=mask_hbm))
            EV["loads"].inc(sync.dma_start(out=bo_rep[:], in_=bo[None, :].broadcast_to([128, DM])))
            EV["loads"].inc(sync.dma_start(out=br_rep[:], in_=br[None, :].broadcast_to([128, NE])))
            for i, bsrc in enumerate((bq, bk, bv)):
                EV["loads"].inc(sync.dma_start(
                    out=bqkv_sb[:, i, :], in_=bsrc.rearrange("(h d) -> d h", d=64),
                    allow_slow_non_contiguous=True))
            EV["loads"].inc(sync.dma_start(
                out=b1_sb[:], in_=b1.rearrange("e (mq p) -> p e mq", p=128),
                allow_slow_non_contiguous=True))
        N_CONST_LOADS = EV["loads"].n

        @block.gpsimd
        def _(g):
            EV["wload"].inc(g.dma_start(out=Wr_sb[:], in_=Wr.rearrange("(fc p) e -> p fc e", p=128)))
            EV["wload"].inc(g.dma_start(out=b2_sb[:], in_=b2))
        N_CONST_WLOADS = EV["wload"].n

        @block.vector
        def _(v):
            v.memset(T_sb[:], 0.0)
            EV["init"].inc(v.memset(eps_t[:], 1e-5))

        @block.scalar
        def _(s):
            EV["loads"].wait(s, N_CONST_LOADS - 1)
            EV["init"].wait(s, 0)
            EV["bq8"].inc(s.mul(bqkv_sb[:, 0, :], bqkv_sb[:, 0, :], 0.125))

        WQKVO = (Wq, Wk, Wv, Wo)
        W_re = lambda W: W.rearrange("(fc p) o -> p fc o", p=128)

        ln_count = [0]
        tp_count = [0]
        xcw_of_xtile = {}

        def emit_ln_tile(src_ap):
            i = ln_count[0]
            ln_count[0] += 1
            p = i % 2

            @block.vector
            def _(v):
                EV["negsum"].inc(v.tensor_reduce(
                    ln_s[:, p, 0:1], src_ap, axis=AX.X, op=ALU.add, negate=True))

            @block.scalar
            def _(s):
                EV["negsum"].wait(s, i)
                EV["negmean"].inc(s.mul(ln_s[:, p, 1:2], ln_s[:, p, 0:1], 1.0 / DM))
                EV["negmean"].wait(s, i)
                EV["xcw"].inc(s.activation(xc_sb[:, p], src_ap, AF.Identity,
                                           bias=ln_s[:, p, 1:2]))
                EV["ssq"].inc(s.activation(sqd[:], xc_sb[:, p], AF.Square,
                                           accum_out=ln_s[:, p, 2:3]))
                EV["ssq"].wait(s, i)
                EV["std"].inc(s.activation(ln_s[:, p, 3:4], ln_s[:, p, 2:3],
                                           AF.Sqrt, scale=1.0 / DM, bias=eps_t[:]))

            @block.vector
            def _(v):
                EV["std"].wait(v, i)
                EV["rstd"].inc(v.reciprocal(ln_s[:, p, 1:2], ln_s[:, p, 3:4]))

            @block.scalar
            def _(s):
                EV["rstd"].wait(s, i)
                if tp_count[0] >= 8:
                    EV["tp"].wait(s, tp_count[0] - 5)
                EV["xln"].inc(s.activation(xln_sb[:, p], xc_sb[:, p],
                                           AF.Identity, scale=ln_s[:, p, 1:2]))
            return i

        def emit_transposes(dstT, dst_col0, xln_i):
            p = xln_i % 2
            for half4 in range(4):
                m = tp_count[0]
                tp_count[0] += 1

                @block.tensor
                def _(t, half4=half4, m=m):
                    if half4 == 0:
                        EV["xln"].wait(t, xln_i)
                    if m > 0:
                        EV["tpevac"].wait(t, m - 1)
                    for i2 in range(2):
                        fc = half4 * 2 + i2
                        ins = t.matmul(tp_ps[:, i2 * 128:(i2 + 1) * 128],
                                       xln_sb[:, p, fc * 128:(fc + 1) * 128],
                                       ident[:], is_transpose=True,
                                       start=(i2 == 0), stop=(i2 == 1))
                    EV["tp"].inc(ins)

                @block.vector
                def _(v, half4=half4, m=m):
                    EV["tp"].wait(v, m)
                    EV["tpevac"].inc(v.tensor_copy(
                        dstT[:, half4 * 2:(half4 + 1) * 2, dst_col0:dst_col0 + 128],
                        tp_ps[:]))

        grp_count = [0]
        qkv_count = [0]
        wo_count = [0]
        xl_count = [0]
        it_count = [0]
        h_count = [0]
        y_count = [0]
        rt_count = [0]

        def gslice(t_all, g):
            return t_all[:, g].rearrange("d h e -> d (h e)")

        for hf in range(NHALF):
            @block.sync
            def _(sync, hf=hf):
                EV["xresload"].inc(sync.dma_start(
                    out=acc[:],
                    in_=x_res.rearrange("j p d -> p j d")[:, hf * 8:(hf + 1) * 8, :]))

            @block.vector
            def _(v, hf=hf):
                EV["loads"].wait(v, N_CONST_LOADS - 1)
                EV["xresload"].wait(v, hf)
                for j in range(JT_HALF):
                    ins = v.tensor_tensor(acc[:, j], acc[:, j], bo_rep[:], ALU.add)
                EV["boadd"].inc(ins)

            @block.gpsimd
            def _(g, hf=hf):
                if hf > 0:
                    EV["yacc"].wait(g, y_count[0] - 1)
                for i, W in enumerate(WQKVO):
                    EV["wload"].inc(g.dma_start(out=WA[i][:], in_=W_re(W)))
            wl_attn_last = EV["wload"].n - 1

            for ch in range(2):
                for n in range(4):
                    tile = (hf * 2 + ch) * 4 + n
                    xi = xl_count[0]
                    xl_count[0] += 1

                    @block.sync
                    def _(sync, tile=tile, xi=xi):
                        if xi >= 2:
                            EV["xcw"].wait(sync, xcw_of_xtile[xi - 2])
                        EV["xload"].inc(sync.dma_start(
                            out=x_sb[:, xi % 2],
                            in_=x_own.rearrange("(t p) d -> t p d", p=128)[tile]))

                    @block.vector
                    def _(v, xi=xi):
                        EV["xload"].wait(v, xi)

                    ln_i = emit_ln_tile(x_sb[:, xi % 2])
                    xcw_of_xtile[xi] = ln_i
                    emit_transposes(xlnT, n * 128, ln_i)

                @block.tensor
                def _(t, hf=hf):
                    EV["tpevac"].wait(t, tp_count[0] - 1)
                    EV["bq8"].wait(t, 0)
                    EV["wload"].wait(t, wl_attn_last)
                    if hf > 0 and qkv_count[0] == hf * 96:
                        EV["yacc"].wait(t, y_count[0] - 1)
                    if wo_count[0] > 0:
                        EV["woevac"].wait(t, wo_count[0] - 1)
                    for idx in range(3 * H):
                        pi, h = divmod(idx, H)
                        gi = qkv_count[0] + idx
                        if gi >= 2:
                            EV["qkvevac"].wait(t, gi - 2)
                        ps = PP[gi % 2]
                        for fc in range(8):
                            ins = t.matmul(ps[0:64, :], WA[pi][:, fc, h * 64:(h + 1) * 64],
                                           xlnT[:, fc, :], start=(fc == 0), stop=(fc == 7))
                        EV["qkvfill"].inc(ins)

                @block.scalar
                def _(s):
                    dst_all = (q_all, k_all, v_all)
                    for idx in range(3 * H):
                        pi, h = divmod(idx, H)
                        gi = qkv_count[0] + idx
                        EV["qkvfill"].wait(s, gi)
                        EV["qkvevac"].inc(s.activation(
                            dst_all[pi][:, :, h, :], PP[gi % 2][0:64, :], AF.Identity,
                            bias=bqkv_sb[:, pi, h:h + 1],
                            scale=(0.125 if pi == 0 else 1.0)))
                qkv_count[0] += 3 * H

                g0 = grp_count[0]

                @block.tensor
                def _(t, g0=g0):
                    EV["qkvevac"].wait(t, qkv_count[0] - 1)
                    for gl in range(NGC):
                        g = g0 + gl
                        pr = g % 2
                        if g >= 2:
                            EV["grpcopies"].wait(t, g - 2)
                        EV["Smm"].inc(t.matmul(SB_ps[pr][:, 0:128], gslice(q_all, gl),
                                               gslice(k_all, gl), start=True, stop=True))
                        if g >= 1:
                            EV["grpcopies"].wait(t, g - 1)
                        EV["VGmm"].inc(t.matmul(VG_ps[:], gslice(v_all, gl),
                                                ident[0:64, 0:64],
                                                is_transpose=True, start=True, stop=True))
                        EV["Pn"].wait(t, g)
                        EV["PTmm"].inc(t.matmul(PT_ps[:], Pn_sb[:, pr], ident[:],
                                                is_transpose=True, start=True, stop=True))
                        EV["grpcopies"].wait(t, g)
                        if g >= 1:
                            EV["OTevac"].wait(t, g - 1)
                        EV["OTmm"].inc(t.matmul(OT_ps[:], Vg_sb[:, pr], PT_sb[:, pr],
                                                start=True, stop=True))

                @block.vector
                def _(v, g0=g0):
                    def emit_group_evac(g):
                        gl = g - g0
                        EV["OTmm"].wait(v, g)
                        j = gl // 16
                        rr = (gl % 16) // 2
                        cc0 = 4 * (gl % 2)
                        ins = None
                        for sp in range(2):
                            srcap = OT_ps[:, sp::2].rearrange("d (h u) -> d h u", h=H)
                            dst = (preT2.ap()
                                   .rearrange("p j c (h r) -> p j c h r", h=H)
                                   [sp * 64:(sp + 1) * 64, j, cc0:cc0 + 4, :, rr]
                                   .rearrange("p c h -> p h c"))
                            ins = v.tensor_copy(dst, srcap)
                        EV["OTevac"].inc(ins)

                    for gl in range(NGC):
                        g = g0 + gl
                        pr = g % 2
                        EV["Smm"].wait(v, g)
                        v.tensor_tensor(T_sb[:, pr], SB_ps[pr][:, 0:128], mask8[:], ALU.add)
                        EV["Tmax"].inc(v.tensor_reduce(
                            smax[:, pr], T_sb[:, pr], axis=AX.X, op=ALU.max, negate=True))
                        EV["P"].wait(v, g)
                        EV["Psum"].inc(v.tensor_reduce(
                            ssum[:, pr], P_sb[:, pr], axis=AX.X, op=ALU.add))
                        EV["Psum"].wait(v, g)
                        EV["Prec"].inc(v.reciprocal(srec[:, pr], ssum[:, pr]))
                        EV["PTmm"].wait(v, g)
                        v.tensor_copy(PT_sb[:, pr], PT_ps[:])
                        EV["grpcopies"].inc(v.tensor_copy(Vg_sb[:, pr], VG_ps[:]))
                        if g > g0:
                            emit_group_evac(g - 1)
                    emit_group_evac(g0 + NGC - 1)

                @block.scalar
                def _(s, g0=g0):
                    for gl in range(NGC):
                        g = g0 + gl
                        pr = g % 2
                        EV["Tmax"].wait(s, g)
                        EV["P"].inc(s.activation(P_sb[:, pr], T_sb[:, pr], AF.Exp,
                                                 bias=smax[:, pr]))
                        EV["Prec"].wait(s, g)
                        EV["Pn"].inc(s.activation(Pn_sb[:, pr], P_sb[:, pr],
                                                  AF.Identity, scale=srec[:, pr]))
                grp_count[0] += NGC

                @block.tensor
                def _(t):
                    EV["OTevac"].wait(t, grp_count[0] - 1)
                    for m in range(8):
                        jj, seg = divmod(m, 2)
                        wi = wo_count[0] + m
                        if wi >= 2:
                            EV["woevac"].wait(t, wi - 2)
                        ps = PP[wi % 2]
                        for cc in range(8):
                            ins = t.matmul(ps[:], preT2[:, jj, cc, :],
                                           WA[3][:, cc, seg * 512:(seg + 1) * 512],
                                           start=(cc == 0), stop=(cc == 7))
                        EV["wofill"].inc(ins)

                @block.vector
                def _(v, ch=ch, hf=hf):
                    EV["boadd"].wait(v, hf)
                    for m in range(8):
                        jj, seg = divmod(m, 2)
                        wi = wo_count[0] + m
                        EV["wofill"].wait(v, wi)
                        jl = ch * 4 + jj
                        sl = slice(seg * 512, (seg + 1) * 512)
                        EV["woevac"].inc(v.tensor_tensor(
                            acc[:, jl, sl], acc[:, jl, sl], PP[wi % 2][:], ALU.add))
                wo_count[0] += 8

            for jl in range(JT_HALF):
                ln_i = emit_ln_tile(acc[:, jl])
                emit_transposes(xfT, jl * 128, ln_i)

            for jl in range(JT_HALF):
                ri = rt_count[0]
                rt_count[0] += 1

                @block.tensor
                def _(t, jl=jl, ri=ri):
                    EV["tpevac"].wait(t, tp_count[0] - (JT_HALF - jl - 1) * 4 - 1)
                    if ri > 0:
                        EV["rlogv"].wait(t, ri - 1)
                    EV["wload"].wait(t, N_CONST_WLOADS - 1)
                    for fc in range(8):
                        ins = t.matmul(SB_ps[0][:, 256:256 + NE],
                                       xfT[:, fc, jl * 128:(jl + 1) * 128],
                                       Wr_sb[:, fc, :], start=(fc == 0), stop=(fc == 7))
                    EV["rfill"].inc(ins)

                @block.vector
                def _(v, ri=ri):
                    EV["rfill"].wait(v, ri)
                    EV["rlogv"].inc(v.tensor_tensor(rlog[:], SB_ps[0][:, 256:256 + NE],
                                                    br_rep[:], ALU.add))
                    EV["rlogv"].wait(v, ri)
                    EV["rmax"].inc(v.tensor_reduce(rneg[:], rlog[:], axis=AX.X,
                                                   op=ALU.max, negate=True))

                @block.scalar
                def _(s, ri=ri):
                    EV["rmax"].wait(s, ri)
                    EV["rexp"].inc(s.activation(rmsk[:], rlog[:], AF.Exp, bias=rneg[:]))

                @block.vector
                def _(v, jl=jl, ri=ri):
                    EV["rexp"].wait(v, ri)
                    EV["rsum_"].inc(v.tensor_reduce(rsum[:], rmsk[:], axis=AX.X, op=ALU.add))
                    EV["rsum_"].wait(v, ri)
                    EV["rgate"].inc(v.reciprocal(rgat[:], rsum[:]))
                    EV["rmask"].inc(v.tensor_scalar(out=rmsk[:], in0=rlog[:],
                                                    scalar1=rneg[:], scalar2=0.0,
                                                    op0=ALU.add, op1=ALU.is_ge))
                    EV["rgate"].wait(v, ri)
                    EV["rmask"].wait(v, ri)
                    EV["rsg"].inc(v.tensor_scalar(out=S_gates[:, jl, :], in0=rmsk[:],
                                                  scalar1=rgat[:], scalar2=None,
                                                  op0=ALU.mult))
                    EV["rsg"].wait(v, ri)
                    EV["rsbf"].inc(v.tensor_copy(S_bf[:], S_gates[:, jl, :]))

                @block.tensor
                def _(t, ri=ri):
                    EV["rsbf"].wait(t, ri)
                    if ri > 0:
                        EV["stTevac"].wait(t, ri - 1)
                    EV["stT"].inc(t.matmul(PT_ps[0:NE, :], S_bf[:], ident[:],
                                           is_transpose=True, start=True, stop=True))

                @block.vector
                def _(v, ri=ri):
                    EV["stT"].wait(v, ri)
                    EV["stTevac"].inc(v.tensor_copy(ST_sb[:], PT_ps[0:NE, :]))

                @block.tensor
                def _(t, ri=ri):
                    EV["stTevac"].wait(t, ri)
                    if ri > 0:
                        EV["b2acc"].wait(t, 2 * ri - 1)
                    for seg in range(2):
                        EV["b2f"].inc(t.matmul(PP[seg][:], ST_sb[:],
                                               b2_sb[:, seg * 512:(seg + 1) * 512],
                                               start=True, stop=True))

                @block.vector
                def _(v, jl=jl, ri=ri):
                    for seg in range(2):
                        EV["b2f"].wait(v, 2 * ri + seg)
                        sl = slice(seg * 512, (seg + 1) * 512)
                        EV["b2acc"].inc(v.tensor_tensor(
                            acc[:, jl, sl], acc[:, jl, sl], PP[seg][:], ALU.add))

            for e in range(NE):
                for qt in range(NQT):
                    it = it_count[0]
                    it_count[0] += 1
                    buf = it % 2
                    W1q = WA[buf * 2]
                    W2q = WA[buf * 2 + 1]
                    w1_ord = N_CONST_WLOADS + (hf + 1) * 4 + it * 2
                    w2_ord = w1_ord + 1

                    @block.gpsimd
                    def _(g, e=e, qt=qt, it=it, hf=hf, W1q=W1q, W2q=W2q):
                        if it == hf * ITERS_HALF:
                            EV["qkvfill"].wait(g, qkv_count[0] - 1)
                            EV["wofill"].wait(g, wo_count[0] - 1)
                        if it >= hf * ITERS_HALF + 2:
                            EV["yfill"].wait(g, (it - 1) * 16 - 1)
                        EV["wload"].inc(g.dma_start(
                            out=W1q[:],
                            in_=W1[e, :, qt * 1024:(qt + 1) * 1024]
                                .rearrange("(fc p) o -> p fc o", p=128)))
                        EV["wload"].inc(g.dma_start(
                            out=W2q[:],
                            in_=W2[e, qt * 1024:(qt + 1) * 1024, :]
                                .rearrange("(fc p) o -> p fc o", p=128)))

                    @block.tensor
                    def _(t, it=it, hf=hf, W1q=W1q, W2q=W2q,
                          w1_ord=w1_ord, w2_ord=w2_ord):
                        EV["wload"].wait(t, w1_ord)
                        if it == hf * ITERS_HALF:
                            EV["b2acc"].wait(t, 2 * rt_count[0] - 1)
                        for seg in range(2):
                            for m in range(8):
                                hi = h_count[0]
                                h_count[0] += 1
                                if hi >= 2:
                                    EV["hevac"].wait(t, hi - 2)
                                hb = SB_ps[hi % 2]
                                for fc in range(8):
                                    ins = t.matmul(hb[:], W1q[:, fc, m * 128:(m + 1) * 128],
                                                   xfT[:, fc, seg * 512:(seg + 1) * 512],
                                                   start=(fc == 0), stop=(fc == 7))
                                EV["hfill"].inc(ins)
                        EV["wload"].wait(t, w2_ord)
                        for jt in range(8):
                            seg = jt // 4
                            for dseg in range(2):
                                yi = y_count[0]
                                y_count[0] += 1
                                if yi >= 2:
                                    EV["yacc"].wait(t, yi - 2)
                                yb = PP[yi % 2]
                                EV["hevac"].wait(t, it * 16 + seg * 8 + 7)
                                for Fc in range(8):
                                    ins = t.matmul(yb[:], h_sb[:, Fc, jt * 128:(jt + 1) * 128],
                                                   W2q[:, Fc, dseg * 512:(dseg + 1) * 512],
                                                   start=(Fc == 0), stop=(Fc == 7))
                                EV["yfill"].inc(ins)

                    @block.scalar
                    def _(s, e=e, qt=qt, it=it, hf=hf):
                        if it > 0:
                            EV["yfill"].wait(s, it * 16 - 1)
                        for seg in range(2):
                            for m in range(8):
                                hi = it * 16 + seg * 8 + m
                                EV["hfill"].wait(s, hi)
                                EV["hevac"].inc(s.activation(
                                    h_sb[:, m, seg * 512:(seg + 1) * 512],
                                    SB_ps[hi % 2][:], AF.Gelu,
                                    bias=b1_sb[:, e, qt * 8 + m:qt * 8 + m + 1]))

                    @block.vector
                    def _(v, e=e, it=it, hf=hf):
                        if it == hf * ITERS_HALF:
                            EV["b2acc"].wait(v, 2 * rt_count[0] - 1)
                        for jt in range(8):
                            for dseg in range(2):
                                yi = it * 16 + jt * 2 + dseg
                                EV["yfill"].wait(v, yi)
                                sl = slice(dseg * 512, (dseg + 1) * 512)
                                EV["yacc"].inc(v.scalar_tensor_tensor(
                                    out=acc[:, jt, sl], in0=PP[yi % 2][:],
                                    scalar=S_gates[:, jt, e:e + 1],
                                    in1=acc[:, jt, sl],
                                    op0=ALU.mult, op1=ALU.add))

            @block.sync
            def _(sync, hf=hf):
                EV["yacc"].wait(sync, y_count[0] - 1)
                EV["store"].inc(sync.dma_start(
                    out=out.rearrange("j p d -> p j d")[:, hf * 8:(hf + 1) * 8, :],
                    in_=acc[:]))

        @block.sync
        def _(sync):
            sync.wait_ge(EV["store"].h, 16 * EV["store"].n)
    return nc


_NC_CACHE = {}


def _get_nc():
    if "nc" not in _NC_CACHE:
        _NC_CACHE["nc"] = build()
    return _NC_CACHE["nc"]


def kernel(L0_attn_hidden_state, L0_attn_Wq, L0_attn_bq, L0_attn_Wk, L0_attn_bk,
           L0_attn_Wv, L0_attn_bv, L0_attn_Wo, L0_attn_bo,
           L0_ffn_W1, L0_ffn_b1, L0_ffn_W2, L0_ffn_b2, L0_ffn_Wr, L0_ffn_br):
    x = np.ascontiguousarray(np.asarray(L0_attn_hidden_state, dtype=np.float32))
    B, S, _ = x.shape
    nc = _get_nc()

    ident = np.eye(128, dtype=bfd)
    ii, jj = np.meshgrid(np.arange(128), np.arange(128), indexing="ij")
    mask8 = np.where(ii % 8 == jj % 8, 0.0, -1e30).astype(np.float32)

    common = dict(
        Wq=np.ascontiguousarray(L0_attn_Wq, np.float32),
        Wk=np.ascontiguousarray(L0_attn_Wk, np.float32),
        Wv=np.ascontiguousarray(L0_attn_Wv, np.float32),
        Wo=np.ascontiguousarray(L0_attn_Wo, np.float32),
        bq=np.ascontiguousarray(L0_attn_bq, np.float32),
        bk=np.ascontiguousarray(L0_attn_bk, np.float32),
        bv=np.ascontiguousarray(L0_attn_bv, np.float32),
        bo=np.ascontiguousarray(L0_attn_bo, np.float32),
        W1=np.ascontiguousarray(L0_ffn_W1, np.float32),
        b1=np.ascontiguousarray(L0_ffn_b1, np.float32),
        W2=np.ascontiguousarray(L0_ffn_W2, np.float32),
        b2=np.ascontiguousarray(L0_ffn_b2, np.float32),
        Wr=np.ascontiguousarray(L0_ffn_Wr, np.float32),
        br=np.ascontiguousarray(L0_ffn_br, np.float32),
        ident_hbm=ident, mask_hbm=mask8,
    )

    in_maps = []
    for c in range(8):
        b, c2 = divmod(c, 2)
        x_own = np.ascontiguousarray(x[b, 2048 * c2: 2048 * (c2 + 1)])
        xr = x[b].reshape(16, 2, 16, 8, DM)[:, c2]
        x_res = np.ascontiguousarray(
            xr.transpose(1, 0, 2, 3).reshape(16, 128, DM))
        im = dict(common)
        im["x_own"] = x_own
        im["x_res"] = x_res
        in_maps.append(im)

    res = run_bass_kernel_spmd(nc, in_maps, core_ids=list(range(8)))

    out = np.empty((B, S, DM), dtype=np.float32)
    for c in range(8):
        b, c2 = divmod(c, 2)
        oc = res.results[c]["out"]
        ov = oc.reshape(16, 16, 8, DM).transpose(1, 0, 2, 3)
        out[b].reshape(16, 2, 16, 8, DM)[:, c2] = ov
    return out


# version 03
# speedup vs baseline: 1.0915x; 1.0915x over previous
import numpy as np
import ml_dtypes
import concourse.bass as bass
import concourse.mybir as mybir
from concourse.bass_utils import run_bass_kernel_spmd

F32 = mybir.dt.float32
BF16 = mybir.dt.bfloat16
AF = mybir.ActivationFunctionType
ALU = mybir.AluOpType
AX = mybir.AxisListType
bfd = ml_dtypes.bfloat16

H = 16
D = 64
DM = 1024
FF = 4096
NE = 8
CT = 512
NGC = CT // 8
NQT = 4
NHALF = 2
JT_HALF = 8
ITERS_HALF = NE * NQT


class Evt:
    def __init__(self, stack, nc, name, step=1):
        self.h = stack.enter_context(nc.semaphore(name))
        self.step = step
        self.n = 0

    def inc(self, instr):
        instr.then_inc(self.h, self.step)
        self.n += 1
        return instr

    def wait(self, eng, i):
        assert 0 <= i < self.n or True
        eng.wait_ge(self.h, self.step * (i + 1))


def build():
    nc = bass.Bass("TRN2", target_bir_lowering=False,
                   detect_race_conditions=False)
    x_own = nc.dram_tensor("x_own", [2048, DM], F32, kind="ExternalInput").ap()
    x_res = nc.dram_tensor("x_res", [16, 128, DM], F32, kind="ExternalInput").ap()
    Wq = nc.dram_tensor("Wq", [DM, DM], F32, kind="ExternalInput").ap()
    Wk = nc.dram_tensor("Wk", [DM, DM], F32, kind="ExternalInput").ap()
    Wv = nc.dram_tensor("Wv", [DM, DM], F32, kind="ExternalInput").ap()
    Wo = nc.dram_tensor("Wo", [DM, DM], F32, kind="ExternalInput").ap()
    bq = nc.dram_tensor("bq", [DM], F32, kind="ExternalInput").ap()
    bk = nc.dram_tensor("bk", [DM], F32, kind="ExternalInput").ap()
    bv = nc.dram_tensor("bv", [DM], F32, kind="ExternalInput").ap()
    bo = nc.dram_tensor("bo", [DM], F32, kind="ExternalInput").ap()
    W1 = nc.dram_tensor("W1", [NE, DM, FF], F32, kind="ExternalInput").ap()
    b1 = nc.dram_tensor("b1", [NE, FF], F32, kind="ExternalInput").ap()
    W2 = nc.dram_tensor("W2", [NE, FF, DM], F32, kind="ExternalInput").ap()
    b2 = nc.dram_tensor("b2", [NE, DM], F32, kind="ExternalInput").ap()
    Wr = nc.dram_tensor("Wr", [DM, NE], F32, kind="ExternalInput").ap()
    br = nc.dram_tensor("br", [NE], F32, kind="ExternalInput").ap()
    ident_hbm = nc.dram_tensor("ident_hbm", [128, 128], BF16, kind="ExternalInput").ap()
    mask_hbm = nc.dram_tensor("mask_hbm", [128, 128], F32, kind="ExternalInput").ap()
    out = nc.dram_tensor("out", [16, 128, DM], F32, kind="ExternalOutput").ap()

    from contextlib import ExitStack
    st = ExitStack()
    with st:
        sbt = lambda *a: st.enter_context(nc.sbuf_tensor(*a))
        pst = lambda *a: st.enter_context(nc.psum_tensor(*a))

        ident = sbt("ident", [128, 128], BF16)
        mask8 = sbt("mask8", [128, 128], F32)
        bqkv_sb = sbt("bqkv_sb", [64, 3, H], F32)
        bo_rep = sbt("bo_rep", [128, DM], F32)
        br_rep = sbt("br_rep", [128, NE], F32)
        Wr_sb = sbt("Wr_sb", [128, 8, NE], BF16)
        b1_sb = sbt("b1_sb", [128, NE, 32], F32)
        b2_sb = sbt("b2_sb", [NE, DM], BF16)
        eps_t = sbt("eps_t", [128, 1], F32)
        ln_s = sbt("ln_s", [128, 2, 4], F32)
        sqd = sbt("sqd", [128, DM], F32)
        acc = sbt("acc", [128, JT_HALF, DM], F32)
        WA = [sbt(f"wa{i}", [128, 8, DM], BF16) for i in range(4)]
        x_sb = sbt("x_sb", [128, 2, DM], F32)
        xc_sb = sbt("xc_sb", [128, 2, DM], F32)
        xln_sb = sbt("xln_sb", [128, 2, DM], BF16)
        xlnT = sbt("xlnT", [128, 8, CT], BF16)
        preT2 = sbt("preT2", [128, 4, 8, 128], BF16)
        T_sb = sbt("T_sb", [128, 2, 4, 128], F32)
        P_sb = sbt("P_sb", [128, 2, 4, 128], F32)
        Pn_sb = sbt("Pn_sb", [128, 2, 4, 128], BF16)
        PT_sb = sbt("PT_sb", [128, 2, 4, 128], BF16)
        Vg_sb = sbt("Vg_sb", [128, 2, 4, D], BF16)
        smax = sbt("smax", [128, 2, 4], F32)
        ssum = sbt("ssum", [128, 2, 4], F32)
        srec = sbt("srec", [128, 2, 4], F32)
        S_gates = sbt("S_gates", [128, JT_HALF, NE], F32)
        rlog = sbt("rlog", [128, NE], F32)
        rmsk = sbt("rmsk", [128, NE], F32)
        rneg = sbt("rneg", [128, 1], F32)
        rsum = sbt("rsum", [128, 1], F32)
        rgat = sbt("rgat", [128, 1], F32)
        S_bf = sbt("S_bf", [128, NE], BF16)
        ST_sb = sbt("ST_sb", [NE, 128], BF16)
        qkv_region = sbt("qkv_region", [128, 48 * 1024 // 4], F32)
        base = nc.lookup_mloc(qkv_region).addr
        q_all = nc.alloc_sbuf_tensor_at("q_all", [64, NGC, H, 8], BF16, offset=base)
        k_all = nc.alloc_sbuf_tensor_at("k_all", [64, NGC, H, 8], BF16, offset=base + 16 * 1024)
        v_all = nc.alloc_sbuf_tensor_at("v_all", [64, NGC, H, 8], BF16, offset=base + 32 * 1024)
        xfT = nc.alloc_sbuf_tensor_at("xfT", [128, 8, 1024], BF16, offset=base)
        h_sb = nc.alloc_sbuf_tensor_at("h_sb", [128, 8, 1024], BF16, offset=base + 16 * 1024)

        PP = [pst("pp0", [128, 512], F32), pst("pp1", [128, 512], F32)]
        SB_ps = [pst("S0", [128, 512], F32), pst("S1", [128, 512], F32)]
        tp_ps = pst("tp", [128, 256], BF16)
        PT_ps = pst("PT0", [128, 512], BF16)
        VG_ps = pst("VG0", [128, 4 * D], BF16)
        OT_ps = pst("OT0", [64, 256], F32)

        ev_names = [
            "loads", "wload", "store", "init", "bq8",
            "xload", "xresload", "negsum", "negmean", "ssq", "std", "rstd",
            "xln", "xcw", "tp", "tpevac", "qkvfill", "qkvevac",
            "sqS", "sqVG", "sqPT", "sqOT", "Tadd", "Tmax", "Tsub", "P",
            "Psum", "Prec", "Pnm", "sqcopies", "OTevac", "wofill", "woevac", "boadd",
            "rfill", "rlogv", "rmax", "rexp", "rsum_", "rgate", "rmask",
            "rsg", "rsbf", "stT", "stTevac", "b2f", "b2acc",
            "hfill", "hevac", "yfill", "yacc",
        ]
        EV = {n: Evt(st, nc, n, 16 if n in ("loads", "wload", "store", "xload", "xresload") else 1)
              for n in ev_names}

        block = st.enter_context(nc.Block())

        @block.sync
        def _(sync):
            EV["loads"].inc(sync.dma_start(out=ident[:], in_=ident_hbm))
            EV["loads"].inc(sync.dma_start(out=mask8[:], in_=mask_hbm))
            EV["loads"].inc(sync.dma_start(out=bo_rep[:], in_=bo[None, :].broadcast_to([128, DM])))
            EV["loads"].inc(sync.dma_start(out=br_rep[:], in_=br[None, :].broadcast_to([128, NE])))
            for i, bsrc in enumerate((bq, bk, bv)):
                EV["loads"].inc(sync.dma_start(
                    out=bqkv_sb[:, i, :], in_=bsrc.rearrange("(h d) -> d h", d=64),
                    allow_slow_non_contiguous=True))
            EV["loads"].inc(sync.dma_start(
                out=b1_sb[:], in_=b1.rearrange("e (mq p) -> p e mq", p=128),
                allow_slow_non_contiguous=True))
        N_CONST_LOADS = EV["loads"].n

        @block.gpsimd
        def _(g):
            EV["wload"].inc(g.dma_start(out=Wr_sb[:], in_=Wr.rearrange("(fc p) e -> p fc e", p=128)))
            EV["wload"].inc(g.dma_start(out=b2_sb[:], in_=b2))
        N_CONST_WLOADS = EV["wload"].n

        @block.vector
        def _(v):
            v.memset(T_sb[:], 0.0)
            EV["init"].inc(v.memset(eps_t[:], 1e-5))

        @block.scalar
        def _(s):
            EV["loads"].wait(s, N_CONST_LOADS - 1)
            EV["init"].wait(s, 0)
            EV["bq8"].inc(s.activation(bqkv_sb[:, 0, :], bqkv_sb[:, 0, :],
                                       AF.Identity, scale=0.125))

        WQKVO = (Wq, Wk, Wv, Wo)
        W_re = lambda W: W.rearrange("(fc p) o -> p fc o", p=128)

        ln_count = [0]
        tp_count = [0]
        xcw_of_xtile = {}

        def emit_ln_tile(src_ap):
            i = ln_count[0]
            ln_count[0] += 1
            p = i % 2

            @block.vector
            def _(v):
                EV["negsum"].inc(v.tensor_reduce(
                    ln_s[:, p, 0:1], src_ap, axis=AX.X, op=ALU.add, negate=True))

            @block.scalar
            def _(s):
                EV["negsum"].wait(s, i)
                EV["negmean"].inc(s.activation(ln_s[:, p, 1:2], ln_s[:, p, 0:1],
                                               AF.Identity, scale=1.0 / DM))
                EV["negmean"].wait(s, i)
                EV["xcw"].inc(s.activation(xc_sb[:, p], src_ap, AF.Identity,
                                           bias=ln_s[:, p, 1:2]))
                EV["ssq"].inc(s.activation(sqd[:], xc_sb[:, p], AF.Square,
                                           accum_out=ln_s[:, p, 2:3]))
                EV["ssq"].wait(s, i)
                EV["std"].inc(s.activation(ln_s[:, p, 3:4], ln_s[:, p, 2:3],
                                           AF.Sqrt, scale=1.0 / DM, bias=eps_t[:]))

            @block.vector
            def _(v):
                EV["std"].wait(v, i)
                EV["rstd"].inc(v.reciprocal(ln_s[:, p, 1:2], ln_s[:, p, 3:4]))

            @block.scalar
            def _(s):
                EV["rstd"].wait(s, i)
                if tp_count[0] >= 8:
                    EV["tp"].wait(s, tp_count[0] - 5)
                EV["xln"].inc(s.activation(xln_sb[:, p], xc_sb[:, p],
                                           AF.Identity, scale=ln_s[:, p, 1:2]))
            return i

        def emit_transposes(dstT, dst_col0, xln_i):
            p = xln_i % 2
            for half4 in range(4):
                m = tp_count[0]
                tp_count[0] += 1

                @block.tensor
                def _(t, half4=half4, m=m):
                    if half4 == 0:
                        EV["xln"].wait(t, xln_i)
                    if m > 0:
                        EV["tpevac"].wait(t, m - 1)
                    for i2 in range(2):
                        fc = half4 * 2 + i2
                        ins = t.matmul(tp_ps[:, i2 * 128:(i2 + 1) * 128],
                                       xln_sb[:, p, fc * 128:(fc + 1) * 128],
                                       ident[:], is_transpose=True,
                                       start=(i2 == 0), stop=(i2 == 1))
                    EV["tp"].inc(ins)

                @block.vector
                def _(v, half4=half4, m=m):
                    EV["tp"].wait(v, m)
                    EV["tpevac"].inc(v.tensor_copy(
                        dstT[:, half4 * 2:(half4 + 1) * 2, dst_col0:dst_col0 + 128],
                        tp_ps[:]))

        grp_count = [0]
        qkv_count = [0]
        wo_count = [0]
        xl_count = [0]
        it_count = [0]
        h_count = [0]
        y_count = [0]
        rt_count = [0]

        def gslice(t_all, g):
            return t_all[:, g].rearrange("d h e -> d (h e)")

        for hf in range(NHALF):
            @block.sync
            def _(sync, hf=hf):
                EV["xresload"].inc(sync.dma_start(
                    out=acc[:],
                    in_=x_res.rearrange("j p d -> p j d")[:, hf * 8:(hf + 1) * 8, :]))

            @block.vector
            def _(v, hf=hf):
                EV["loads"].wait(v, N_CONST_LOADS - 1)
                EV["xresload"].wait(v, hf)
                for j in range(JT_HALF):
                    ins = v.tensor_tensor(acc[:, j], acc[:, j], bo_rep[:], ALU.add)
                EV["boadd"].inc(ins)

            @block.gpsimd
            def _(g, hf=hf):
                if hf > 0:
                    EV["yacc"].wait(g, y_count[0] - 1)
                for i, W in enumerate(WQKVO):
                    EV["wload"].inc(g.dma_start(out=WA[i][:], in_=W_re(W)))
            wl_attn_last = EV["wload"].n - 1

            for ch in range(2):
                for n in range(4):
                    tile = (hf * 2 + ch) * 4 + n
                    xi = xl_count[0]
                    xl_count[0] += 1

                    @block.sync
                    def _(sync, tile=tile, xi=xi):
                        if xi >= 2:
                            EV["xcw"].wait(sync, xcw_of_xtile[xi - 2])
                        EV["xload"].inc(sync.dma_start(
                            out=x_sb[:, xi % 2],
                            in_=x_own.rearrange("(t p) d -> t p d", p=128)[tile]))

                    @block.vector
                    def _(v, xi=xi):
                        EV["xload"].wait(v, xi)

                    ln_i = emit_ln_tile(x_sb[:, xi % 2])
                    xcw_of_xtile[xi] = ln_i
                    emit_transposes(xlnT, n * 128, ln_i)

                @block.tensor
                def _(t, hf=hf):
                    EV["tpevac"].wait(t, tp_count[0] - 1)
                    EV["bq8"].wait(t, 0)
                    EV["wload"].wait(t, wl_attn_last)
                    if hf > 0 and qkv_count[0] == hf * 96:
                        EV["yacc"].wait(t, y_count[0] - 1)
                    if wo_count[0] > 0:
                        EV["woevac"].wait(t, wo_count[0] - 1)
                    for idx in range(3 * H):
                        pi, h = divmod(idx, H)
                        gi = qkv_count[0] + idx
                        if gi >= 2:
                            EV["qkvevac"].wait(t, gi - 2)
                        ps = PP[gi % 2]
                        for fc in range(8):
                            ins = t.matmul(ps[0:64, :], WA[pi][:, fc, h * 64:(h + 1) * 64],
                                           xlnT[:, fc, :], start=(fc == 0), stop=(fc == 7))
                        EV["qkvfill"].inc(ins)

                @block.scalar
                def _(s):
                    dst_all = (q_all, k_all, v_all)
                    for idx in range(3 * H):
                        pi, h = divmod(idx, H)
                        gi = qkv_count[0] + idx
                        EV["qkvfill"].wait(s, gi)
                        EV["qkvevac"].inc(s.activation(
                            dst_all[pi][:, :, h, :], PP[gi % 2][0:64, :], AF.Identity,
                            bias=bqkv_sb[:, pi, h:h + 1],
                            scale=(0.125 if pi == 0 else 1.0)))
                qkv_count[0] += 3 * H

                g0 = grp_count[0]

                @block.tensor
                def _(t, g0=g0):
                    EV["qkvevac"].wait(t, qkv_count[0] - 1)
                    sq0 = g0 // 4
                    for sl in range(NGC // 4):
                        sq = sq0 + sl
                        pr = sq % 2
                        if sq >= 2:
                            EV["Tadd"].wait(t, sq - 2)
                        for i4 in range(4):
                            ins = t.matmul(SB_ps[pr][:, i4 * 128:(i4 + 1) * 128],
                                           gslice(q_all, sl * 4 + i4),
                                           gslice(k_all, sl * 4 + i4),
                                           start=(i4 == 0), stop=(i4 == 3))
                        EV["sqS"].inc(ins)
                        if sq >= 1:
                            EV["sqcopies"].wait(t, sq - 1)
                        for i4 in range(4):
                            ins = t.matmul(VG_ps[:, i4 * 64:(i4 + 1) * 64],
                                           gslice(v_all, sl * 4 + i4),
                                           ident[0:64, 0:64], is_transpose=True,
                                           start=(i4 == 0), stop=(i4 == 3))
                        EV["sqVG"].inc(ins)
                        EV["Pnm"].wait(t, sq)
                        for i4 in range(4):
                            ins = t.matmul(PT_ps[:, i4 * 128:(i4 + 1) * 128],
                                           Pn_sb[:, pr, i4], ident[:],
                                           is_transpose=True,
                                           start=(i4 == 0), stop=(i4 == 3))
                        EV["sqPT"].inc(ins)
                        EV["sqcopies"].wait(t, sq)
                        for pp2 in range(2):
                            pair = sq * 2 + pp2
                            if pair >= 1:
                                EV["OTevac"].wait(t, pair - 1)
                            for i2 in range(2):
                                i4 = pp2 * 2 + i2
                                ins = t.matmul(OT_ps[:, i2 * 128:(i2 + 1) * 128],
                                               Vg_sb[:, pr, i4], PT_sb[:, pr, i4],
                                               start=(i2 == 0), stop=(i2 == 1))
                            EV["sqOT"].inc(ins)

                @block.vector
                def _(v, g0=g0):
                    sq0 = g0 // 4

                    def emit_pair_evac(pair):
                        gl = (pair - sq0 * 2) * 2
                        EV["sqOT"].wait(v, pair)
                        j = gl // 16
                        rr = (gl % 16) // 2
                        ins = None
                        for sp in range(2):
                            srcap = (OT_ps.ap()
                                     .rearrange("d (g2 h u) -> d g2 h u", g2=2, h=H)
                                     [:, :, :, sp::2]
                                     .rearrange("d g2 h u -> d g2 u h"))
                            dst = (preT2.ap()
                                   .rearrange("p j c (h r) -> p j c h r", h=H)
                                   [sp * 64:(sp + 1) * 64, j, :, :, rr])
                            ins = v.tensor_copy(dst, srcap)
                        EV["OTevac"].inc(ins)

                    for sl in range(NGC // 4):
                        sq = sq0 + sl
                        pr = sq % 2
                        if sl > 0:
                            emit_pair_evac(sq * 2 - 2)
                            emit_pair_evac(sq * 2 - 1)
                        EV["sqS"].wait(v, sq)
                        mrep = mask8.ap()[:, None, :].broadcast_to([128, 4, 128])
                        EV["Tadd"].inc(v.tensor_tensor(
                            T_sb[:, pr], SB_ps[pr].ap().rearrange("p (a b) -> p a b", a=4),
                            mrep, ALU.add))
                        EV["Tadd"].wait(v, sq)
                        EV["Tmax"].inc(v.tensor_reduce(
                            smax[:, pr], T_sb[:, pr], axis=AX.X, op=ALU.max, negate=True))
                        EV["Tmax"].wait(v, sq)
                        EV["Tsub"].inc(v.tensor_tensor(
                            T_sb[:, pr], T_sb[:, pr],
                            smax[:, pr][:, :, None].broadcast_to([128, 4, 128]),
                            ALU.add))
                        EV["P"].wait(v, sq)
                        EV["Psum"].inc(v.tensor_reduce(
                            ssum[:, pr], P_sb[:, pr], axis=AX.X, op=ALU.add))
                        EV["Psum"].wait(v, sq)
                        EV["Prec"].inc(v.reciprocal(srec[:, pr], ssum[:, pr]))
                        EV["Prec"].wait(v, sq)
                        EV["Pnm"].inc(v.tensor_tensor(
                            Pn_sb[:, pr], P_sb[:, pr],
                            srec[:, pr][:, :, None].broadcast_to([128, 4, 128]),
                            ALU.mult))
                        EV["sqPT"].wait(v, sq)
                        v.tensor_copy(PT_sb[:, pr], PT_ps[:])
                        EV["sqcopies"].inc(v.tensor_copy(Vg_sb[:, pr], VG_ps[:]))
                    emit_pair_evac((sq0 + NGC // 4) * 2 - 2)
                    emit_pair_evac((sq0 + NGC // 4) * 2 - 1)

                @block.scalar
                def _(s, g0=g0):
                    sq0 = g0 // 4
                    for sl in range(NGC // 4):
                        sq = sq0 + sl
                        pr = sq % 2
                        EV["Tsub"].wait(s, sq)
                        EV["P"].inc(s.activation(P_sb[:, pr], T_sb[:, pr], AF.Exp))
                grp_count[0] += NGC

                @block.tensor
                def _(t):
                    EV["OTevac"].wait(t, grp_count[0] // 2 - 1)
                    for m in range(8):
                        jj, seg = divmod(m, 2)
                        wi = wo_count[0] + m
                        if wi >= 2:
                            EV["woevac"].wait(t, wi - 2)
                        ps = PP[wi % 2]
                        for cc in range(8):
                            ins = t.matmul(ps[:], preT2[:, jj, cc, :],
                                           WA[3][:, cc, seg * 512:(seg + 1) * 512],
                                           start=(cc == 0), stop=(cc == 7))
                        EV["wofill"].inc(ins)

                @block.vector
                def _(v, ch=ch, hf=hf):
                    EV["boadd"].wait(v, hf)
                    for m in range(8):
                        jj, seg = divmod(m, 2)
                        wi = wo_count[0] + m
                        EV["wofill"].wait(v, wi)
                        jl = ch * 4 + jj
                        sl = slice(seg * 512, (seg + 1) * 512)
                        EV["woevac"].inc(v.tensor_tensor(
                            acc[:, jl, sl], acc[:, jl, sl], PP[wi % 2][:], ALU.add))
                wo_count[0] += 8

            for jl in range(JT_HALF):
                ln_i = emit_ln_tile(acc[:, jl])
                emit_transposes(xfT, jl * 128, ln_i)

            for jl in range(JT_HALF):
                ri = rt_count[0]
                rt_count[0] += 1

                @block.tensor
                def _(t, jl=jl, ri=ri):
                    EV["tpevac"].wait(t, tp_count[0] - (JT_HALF - jl - 1) * 4 - 1)
                    if ri > 0:
                        EV["rlogv"].wait(t, ri - 1)
                    EV["wload"].wait(t, N_CONST_WLOADS - 1)
                    for fc in range(8):
                        ins = t.matmul(SB_ps[0][:, 256:256 + NE],
                                       xfT[:, fc, jl * 128:(jl + 1) * 128],
                                       Wr_sb[:, fc, :], start=(fc == 0), stop=(fc == 7))
                    EV["rfill"].inc(ins)

                @block.vector
                def _(v, ri=ri):
                    EV["rfill"].wait(v, ri)
                    EV["rlogv"].inc(v.tensor_tensor(rlog[:], SB_ps[0][:, 256:256 + NE],
                                                    br_rep[:], ALU.add))
                    EV["rlogv"].wait(v, ri)
                    EV["rmax"].inc(v.tensor_reduce(rneg[:], rlog[:], axis=AX.X,
                                                   op=ALU.max, negate=True))

                @block.scalar
                def _(s, ri=ri):
                    EV["rmax"].wait(s, ri)
                    EV["rexp"].inc(s.activation(rmsk[:], rlog[:], AF.Exp, bias=rneg[:]))

                @block.vector
                def _(v, jl=jl, ri=ri):
                    EV["rexp"].wait(v, ri)
                    EV["rsum_"].inc(v.tensor_reduce(rsum[:], rmsk[:], axis=AX.X, op=ALU.add))
                    EV["rsum_"].wait(v, ri)
                    EV["rgate"].inc(v.reciprocal(rgat[:], rsum[:]))
                    EV["rmask"].inc(v.tensor_scalar(out=rmsk[:], in0=rlog[:],
                                                    scalar1=rneg[:], scalar2=0.0,
                                                    op0=ALU.add, op1=ALU.is_ge))
                    EV["rgate"].wait(v, ri)
                    EV["rmask"].wait(v, ri)
                    EV["rsg"].inc(v.tensor_scalar(out=S_gates[:, jl, :], in0=rmsk[:],
                                                  scalar1=rgat[:], scalar2=None,
                                                  op0=ALU.mult))
                    EV["rsg"].wait(v, ri)
                    EV["rsbf"].inc(v.tensor_copy(S_bf[:], S_gates[:, jl, :]))

                @block.tensor
                def _(t, ri=ri):
                    EV["rsbf"].wait(t, ri)
                    if ri > 0:
                        EV["stTevac"].wait(t, ri - 1)
                    EV["stT"].inc(t.matmul(PT_ps[0:NE, 0:128], S_bf[:], ident[:],
                                           is_transpose=True, start=True, stop=True))

                @block.vector
                def _(v, ri=ri):
                    EV["stT"].wait(v, ri)
                    EV["stTevac"].inc(v.tensor_copy(ST_sb[:], PT_ps[0:NE, 0:128]))

                @block.tensor
                def _(t, ri=ri):
                    EV["stTevac"].wait(t, ri)
                    if ri > 0:
                        EV["b2acc"].wait(t, 2 * ri - 1)
                    for seg in range(2):
                        EV["b2f"].inc(t.matmul(PP[seg][:], ST_sb[:],
                                               b2_sb[:, seg * 512:(seg + 1) * 512],
                                               start=True, stop=True))

                @block.vector
                def _(v, jl=jl, ri=ri):
                    for seg in range(2):
                        EV["b2f"].wait(v, 2 * ri + seg)
                        sl = slice(seg * 512, (seg + 1) * 512)
                        EV["b2acc"].inc(v.tensor_tensor(
                            acc[:, jl, sl], acc[:, jl, sl], PP[seg][:], ALU.add))

            for e in range(NE):
                for qt in range(NQT):
                    it = it_count[0]
                    it_count[0] += 1
                    buf = it % 2
                    W1q = WA[buf * 2]
                    W2q = WA[buf * 2 + 1]
                    w1_ord = N_CONST_WLOADS + (hf + 1) * 4 + it * 2
                    w2_ord = w1_ord + 1

                    @block.gpsimd
                    def _(g, e=e, qt=qt, it=it, hf=hf, W1q=W1q, W2q=W2q):
                        if it == hf * ITERS_HALF:
                            EV["qkvfill"].wait(g, qkv_count[0] - 1)
                            EV["wofill"].wait(g, wo_count[0] - 1)
                        if it >= hf * ITERS_HALF + 2:
                            EV["yfill"].wait(g, (it - 1) * 16 - 1)
                        EV["wload"].inc(g.dma_start(
                            out=W1q[:],
                            in_=W1[e, :, qt * 1024:(qt + 1) * 1024]
                                .rearrange("(fc p) o -> p fc o", p=128)))
                        EV["wload"].inc(g.dma_start(
                            out=W2q[:],
                            in_=W2[e, qt * 1024:(qt + 1) * 1024, :]
                                .rearrange("(fc p) o -> p fc o", p=128)))

                    @block.tensor
                    def _(t, it=it, hf=hf, W1q=W1q, W2q=W2q,
                          w1_ord=w1_ord, w2_ord=w2_ord):
                        EV["wload"].wait(t, w1_ord)
                        if it == hf * ITERS_HALF:
                            EV["b2acc"].wait(t, 2 * rt_count[0] - 1)
                        for seg in range(2):
                            for m in range(8):
                                hi = h_count[0]
                                h_count[0] += 1
                                if hi >= 2:
                                    EV["hevac"].wait(t, hi - 2)
                                hb = SB_ps[hi % 2]
                                for fc in range(8):
                                    ins = t.matmul(hb[:], W1q[:, fc, m * 128:(m + 1) * 128],
                                                   xfT[:, fc, seg * 512:(seg + 1) * 512],
                                                   start=(fc == 0), stop=(fc == 7))
                                EV["hfill"].inc(ins)
                        EV["wload"].wait(t, w2_ord)
                        for jt in range(8):
                            seg = jt // 4
                            for dseg in range(2):
                                yi = y_count[0]
                                y_count[0] += 1
                                if yi >= 2:
                                    EV["yacc"].wait(t, yi - 2)
                                yb = PP[yi % 2]
                                EV["hevac"].wait(t, it * 16 + seg * 8 + 7)
                                for Fc in range(8):
                                    ins = t.matmul(yb[:], h_sb[:, Fc, jt * 128:(jt + 1) * 128],
                                                   W2q[:, Fc, dseg * 512:(dseg + 1) * 512],
                                                   start=(Fc == 0), stop=(Fc == 7))
                                EV["yfill"].inc(ins)

                    @block.scalar
                    def _(s, e=e, qt=qt, it=it, hf=hf):
                        if it > 0:
                            EV["yfill"].wait(s, it * 16 - 1)
                        for seg in range(2):
                            for m in range(8):
                                hi = it * 16 + seg * 8 + m
                                EV["hfill"].wait(s, hi)
                                EV["hevac"].inc(s.activation(
                                    h_sb[:, m, seg * 512:(seg + 1) * 512],
                                    SB_ps[hi % 2][:], AF.Gelu,
                                    bias=b1_sb[:, e, qt * 8 + m:qt * 8 + m + 1]))

                    @block.vector
                    def _(v, e=e, it=it, hf=hf):
                        if it == hf * ITERS_HALF:
                            EV["b2acc"].wait(v, 2 * rt_count[0] - 1)
                        for jt in range(8):
                            for dseg in range(2):
                                yi = it * 16 + jt * 2 + dseg
                                EV["yfill"].wait(v, yi)
                                sl = slice(dseg * 512, (dseg + 1) * 512)
                                EV["yacc"].inc(v.scalar_tensor_tensor(
                                    out=acc[:, jt, sl], in0=PP[yi % 2][:],
                                    scalar=S_gates[:, jt, e:e + 1],
                                    in1=acc[:, jt, sl],
                                    op0=ALU.mult, op1=ALU.add))

            @block.sync
            def _(sync, hf=hf):
                EV["yacc"].wait(sync, y_count[0] - 1)
                EV["store"].inc(sync.dma_start(
                    out=out.rearrange("j p d -> p j d")[:, hf * 8:(hf + 1) * 8, :],
                    in_=acc[:]))

        @block.sync
        def _(sync):
            sync.wait_ge(EV["store"].h, 16 * EV["store"].n)
    return nc


_NC_CACHE = {}


def _get_nc():
    if "nc" not in _NC_CACHE:
        _NC_CACHE["nc"] = build()
    return _NC_CACHE["nc"]


def kernel(L0_attn_hidden_state, L0_attn_Wq, L0_attn_bq, L0_attn_Wk, L0_attn_bk,
           L0_attn_Wv, L0_attn_bv, L0_attn_Wo, L0_attn_bo,
           L0_ffn_W1, L0_ffn_b1, L0_ffn_W2, L0_ffn_b2, L0_ffn_Wr, L0_ffn_br):
    x = np.ascontiguousarray(np.asarray(L0_attn_hidden_state, dtype=np.float32))
    B, S, _ = x.shape
    nc = _get_nc()

    ident = np.eye(128, dtype=bfd)
    ii, jj = np.meshgrid(np.arange(128), np.arange(128), indexing="ij")
    mask8 = np.where(ii % 8 == jj % 8, 0.0, -1e30).astype(np.float32)

    common = dict(
        Wq=np.ascontiguousarray(L0_attn_Wq, np.float32),
        Wk=np.ascontiguousarray(L0_attn_Wk, np.float32),
        Wv=np.ascontiguousarray(L0_attn_Wv, np.float32),
        Wo=np.ascontiguousarray(L0_attn_Wo, np.float32),
        bq=np.ascontiguousarray(L0_attn_bq, np.float32),
        bk=np.ascontiguousarray(L0_attn_bk, np.float32),
        bv=np.ascontiguousarray(L0_attn_bv, np.float32),
        bo=np.ascontiguousarray(L0_attn_bo, np.float32),
        W1=np.ascontiguousarray(L0_ffn_W1, np.float32),
        b1=np.ascontiguousarray(L0_ffn_b1, np.float32),
        W2=np.ascontiguousarray(L0_ffn_W2, np.float32),
        b2=np.ascontiguousarray(L0_ffn_b2, np.float32),
        Wr=np.ascontiguousarray(L0_ffn_Wr, np.float32),
        br=np.ascontiguousarray(L0_ffn_br, np.float32),
        ident_hbm=ident, mask_hbm=mask8,
    )

    in_maps = []
    for c in range(8):
        b, c2 = divmod(c, 2)
        x_own = np.ascontiguousarray(x[b, 2048 * c2: 2048 * (c2 + 1)])
        xr = x[b].reshape(16, 2, 16, 8, DM)[:, c2]
        x_res = np.ascontiguousarray(
            xr.transpose(1, 0, 2, 3).reshape(16, 128, DM))
        im = dict(common)
        im["x_own"] = x_own
        im["x_res"] = x_res
        in_maps.append(im)

    res = run_bass_kernel_spmd(nc, in_maps, core_ids=list(range(8)))

    out = np.empty((B, S, DM), dtype=np.float32)
    for c in range(8):
        b, c2 = divmod(c, 2)
        oc = res.results[c]["out"]
        ov = oc.reshape(16, 16, 8, DM).transpose(1, 0, 2, 3)
        out[b].reshape(16, 2, 16, 8, DM)[:, c2] = ov
    return out


# version 04
# speedup vs baseline: 1.4866x; 1.3619x over previous
import numpy as np
import ml_dtypes
import concourse.bass as bass
import concourse.mybir as mybir
from concourse.bass_utils import run_bass_kernel_spmd
from concourse.bass import IndirectOffsetOnAxis

F32 = mybir.dt.float32
I32 = mybir.dt.int32
BF16 = mybir.dt.bfloat16
AF = mybir.ActivationFunctionType
ALU = mybir.AluOpType
AX = mybir.AxisListType
bfd = ml_dtypes.bfloat16

H = 16
D = 64
DM = 1024
FF = 4096
NE = 8
CT = 512
NGC = CT // 8
NQT = 4
NHALF = 2
JT_HALF = 8
HT = 1024
ITERS_HALF = NE * NQT
CAP = 256


class Evt:
    def __init__(self, stack, nc, name, step=1):
        self.h = stack.enter_context(nc.semaphore(name))
        self.step = step
        self.n = 0

    def inc(self, instr):
        instr.then_inc(self.h, self.step)
        self.n += 1
        return instr

    def wait(self, eng, i):
        assert 0 <= i < self.n or True
        eng.wait_ge(self.h, self.step * (i + 1))


def build():
    nc = bass.Bass("TRN2", target_bir_lowering=False,
                   detect_race_conditions=False)
    x_own = nc.dram_tensor("x_own", [2048, DM], F32, kind="ExternalInput").ap()
    x_res = nc.dram_tensor("x_res", [16, 128, DM], F32, kind="ExternalInput").ap()
    Wq = nc.dram_tensor("Wq", [DM, DM], F32, kind="ExternalInput").ap()
    Wk = nc.dram_tensor("Wk", [DM, DM], F32, kind="ExternalInput").ap()
    Wv = nc.dram_tensor("Wv", [DM, DM], F32, kind="ExternalInput").ap()
    Wo = nc.dram_tensor("Wo", [DM, DM], F32, kind="ExternalInput").ap()
    bq = nc.dram_tensor("bq", [DM], F32, kind="ExternalInput").ap()
    bk = nc.dram_tensor("bk", [DM], F32, kind="ExternalInput").ap()
    bv = nc.dram_tensor("bv", [DM], F32, kind="ExternalInput").ap()
    bo = nc.dram_tensor("bo", [DM], F32, kind="ExternalInput").ap()
    W1 = nc.dram_tensor("W1", [NE, DM, FF], F32, kind="ExternalInput").ap()
    b1 = nc.dram_tensor("b1", [NE, FF], F32, kind="ExternalInput").ap()
    W2 = nc.dram_tensor("W2", [NE, FF, DM], F32, kind="ExternalInput").ap()
    b2 = nc.dram_tensor("b2", [NE, DM], F32, kind="ExternalInput").ap()
    Wr = nc.dram_tensor("Wr", [DM, NE], F32, kind="ExternalInput").ap()
    br = nc.dram_tensor("br", [NE], F32, kind="ExternalInput").ap()
    ident_hbm = nc.dram_tensor("ident_hbm", [128, 128], BF16, kind="ExternalInput").ap()
    idf32_hbm = nc.dram_tensor("idf32_hbm", [16, 16], F32, kind="ExternalInput").ap()
    mask_hbm = nc.dram_tensor("mask_hbm", [128, 128], F32, kind="ExternalInput").ap()
    evc_hbm = nc.dram_tensor("evc_hbm", [8, 2], BF16, kind="ExternalInput").ap()
    iota_hbm = nc.dram_tensor("iota_hbm", [128, 8], F32, kind="ExternalInput").ap()
    out = nc.dram_tensor("out", [16, 128, DM], F32, kind="ExternalOutput").ap()
    xln2_hbm = nc.dram_tensor("xln2_hbm", [HT + 1, DM], BF16, kind="Internal").ap()
    tok_tbl = nc.dram_tensor("tok_tbl", [NE * CAP, 1], F32, kind="Internal").ap()
    gate_tbl = nc.dram_tensor("gate_tbl", [NE * CAP, 1], F32, kind="Internal").ap()
    y_hbm = nc.dram_tensor("y_hbm", [HT + 1, DM], F32, kind="Internal").ap()
    g_hbm = nc.dram_tensor("g_hbm", [HT, 1], F32, kind="Internal").ap()

    from contextlib import ExitStack
    st = ExitStack()
    with st:
        sbt = lambda *a: st.enter_context(nc.sbuf_tensor(*a))
        pst = lambda *a: st.enter_context(nc.psum_tensor(*a))

        ident = sbt("ident", [128, 128], BF16)
        mask8 = sbt("mask8", [128, 128], F32)
        bqkv_sb = sbt("bqkv_sb", [64, 3, H], F32)
        bo_rep = sbt("bo_rep", [128, DM], F32)
        br_rep = sbt("br_rep", [128, NE], F32)
        Wr_sb = sbt("Wr_sb", [128, 8, NE], BF16)
        b1_sb = sbt("b1_sb", [128, NE, 32], F32)
        b2_sb = sbt("b2_sb", [NE, DM], BF16)
        eps_t = sbt("eps_t", [128, 1], F32)
        ln_s = sbt("ln_s", [128, 2, 4], F32)
        sqd = sbt("sqd", [128, DM], F32)
        acc = sbt("acc", [128, JT_HALF, DM], F32)
        WA = [sbt(f"wa{i}", [128, 8, DM], BF16) for i in range(4)]
        x_sb = sbt("x_sb", [128, 2, DM], F32)
        xc_sb = sbt("xc_sb", [128, 2, DM], F32)
        xln_sb = sbt("xln_sb", [128, 2, DM], BF16)
        xlnT = sbt("xlnT", [128, 8, CT], BF16)
        preT2 = sbt("preT2", [128, 4, 8, 128], BF16)
        T_sb = sbt("T_sb", [128, 2, 4, 128], F32)
        P_sb = T_sb
        Pn_sb = sbt("Pn_sb", [128, 2, 4, 128], BF16)
        PT_sb = sbt("PT_sb", [128, 2, 4, 128], BF16)
        Vg_sb = sbt("Vg_sb", [128, 2, 4, D], BF16)
        smax = sbt("smax", [128, 2, 4], F32)
        ssum = sbt("ssum", [128, 2, 4], F32)
        srec = sbt("srec", [128, 2, 4], F32)
        S_gates = sbt("S_gates", [128, JT_HALF, NE], F32)
        rlog = sbt("rlog", [128, NE], F32)
        rmsk = sbt("rmsk", [128, NE], F32)
        rneg = sbt("rneg", [128, 1], F32)
        rsum = sbt("rsum", [128, 1], F32)
        rgat = sbt("rgat", [128, 1], F32)
        S_bf = sbt("S_bf", [128, NE], BF16)
        ST_sb = sbt("ST_sb", [NE, 128], BF16)
        idf32 = sbt("idf32", [16, 16], F32)
        evc_sb = sbt("evc_sb", [8, 2], BF16)
        iota_sb = sbt("iota_sb", [128, 8], F32)
        zer81 = sbt("zer81", [8, 1], F32)
        g_f = sbt("g_f", [128, 8], F32)
        g_i = sbt("g_i", [128, 8], I32)
        gate_tok = sbt("gate_tok", [128, 8], F32)
        tok_i = sbt("tok_i", [128, 16], I32)
        tok_f = sbt("tok_f", [128, 16], F32)
        gate_slots = sbt("gate_slots", [128, 16], F32)
        fill_i = sbt("fill_i", [128, 16], F32)
        fill_f = sbt("fill_f", [128, 16], F32)
        tok_raw = sbt("tok_raw", [16, 128], F32)
        gate_raw = sbt("gate_raw", [16, 128], F32)
        qkv_region = sbt("qkv_region", [128, 48 * 1024 // 4], F32)
        base = nc.lookup_mloc(qkv_region).addr
        q_all = nc.alloc_sbuf_tensor_at("q_all", [64, NGC, H, 8], BF16, offset=base)
        k_all = nc.alloc_sbuf_tensor_at("k_all", [64, NGC, H, 8], BF16, offset=base + 16 * 1024)
        v_all = nc.alloc_sbuf_tensor_at("v_all", [64, NGC, H, 8], BF16, offset=base + 32 * 1024)
        xfT = nc.alloc_sbuf_tensor_at("xfT", [128, 8, 1024], BF16, offset=base)
        h_sb = nc.alloc_sbuf_tensor_at("h_sb", [128, 8, CAP], BF16, offset=base + 16 * 1024)
        xg = nc.alloc_sbuf_tensor_at("xg", [128, 2, DM], BF16, offset=base + 20 * 1024)
        xgT = nc.alloc_sbuf_tensor_at("xgT", [128, 8, CAP], BF16, offset=base + 24 * 1024)
        y_stage = nc.alloc_sbuf_tensor_at("y_stage", [128, 2, DM], F32, offset=base + 28 * 1024)
        yld = nc.alloc_sbuf_tensor_at("yld", [128, 2, DM], F32, offset=base + 36 * 1024)
        pre_base = nc.lookup_mloc(preT2).addr
        maskT = nc.alloc_sbuf_tensor_at("maskT", [8, HT], BF16, offset=pre_base)
        mp_bf = nc.alloc_sbuf_tensor_at("mp_bf", [8, HT], BF16, offset=pre_base + 2 * 1024)
        csum = nc.alloc_sbuf_tensor_at("csum", [8, HT], F32, offset=pre_base + 4 * 1024)
        g_row = nc.alloc_sbuf_tensor_at("g_row", [1, HT], F32, offset=nc.lookup_mloc(sqd).addr)
        zrow = nc.alloc_sbuf_tensor_at("zrow", [1, DM], BF16, offset=base + 28 * 1024)

        PP = [pst("pp0", [128, 512], F32), pst("pp1", [128, 512], F32)]
        SB_ps = [pst("S0", [128, 512], F32), pst("S1", [128, 512], F32)]
        tp_ps = pst("tp", [128, 256], BF16)
        PT_ps = pst("PT0", [128, 512], BF16)
        VG_ps = pst("VG0", [128, 4 * D], BF16)
        OT_ps = pst("OT0", [64, 256], F32)

        ev_names = [
            "loads", "wload", "store", "init", "bq8",
            "xload", "xresload", "negsum", "negmean", "ssq", "std", "rstd",
            "xln", "xcw", "tp", "tpevac", "qkvfill", "qkvevac",
            "sqS", "sqVG", "sqPT", "sqOT", "Tadd", "Tmax", "Tsub", "P",
            "Psum", "Prec", "Pnm", "sqcopies", "OTevac", "wofill", "woevac", "boadd",
            "rfill", "rlogv", "rmax", "rexp", "rsum_", "rgate", "rmask",
            "rsg", "rsbf", "stT", "stTevac", "b2f", "b2acc",
            "hfill", "hevac", "yfill", "yadd", "ysc", "yacc",
            "xw", "scand", "csm1", "gmm", "gevac", "gtr", "gpe", "gidx",
            "tfill", "scat", "tbl", "traw", "ttr", "xgdma", "xgtr", "xgevac",
            "yscat", "yrld", "fmem",
        ]
        DMA_EV = ("loads", "wload", "store", "xload", "xresload",
                  "xw", "tfill", "scat", "traw", "xgdma", "yscat", "yrld", "gtr")
        EV = {n: Evt(st, nc, n, 16 if n in DMA_EV else 1) for n in ev_names}

        block = st.enter_context(nc.Block())

        @block.sync
        def _(sync):
            EV["loads"].inc(sync.dma_start(out=ident[:], in_=ident_hbm))
            EV["loads"].inc(sync.dma_start(out=mask8[:], in_=mask_hbm))
            EV["loads"].inc(sync.dma_start(out=bo_rep[:], in_=bo[None, :].broadcast_to([128, DM])))
            EV["loads"].inc(sync.dma_start(out=br_rep[:], in_=br[None, :].broadcast_to([128, NE])))
            for i, bsrc in enumerate((bq, bk, bv)):
                EV["loads"].inc(sync.dma_start(
                    out=bqkv_sb[:, i, :], in_=bsrc.rearrange("(h d) -> d h", d=64),
                    allow_slow_non_contiguous=True))
            EV["loads"].inc(sync.dma_start(
                out=b1_sb[:], in_=b1.rearrange("e (mq p) -> p e mq", p=128),
                allow_slow_non_contiguous=True))
            EV["loads"].inc(sync.dma_start(out=idf32[:], in_=idf32_hbm))
            EV["loads"].inc(sync.dma_start(out=evc_sb[:], in_=evc_hbm))
            EV["loads"].inc(sync.dma_start(out=iota_sb[:], in_=iota_hbm))
        N_CONST_LOADS = EV["loads"].n

        @block.gpsimd
        def _(g):
            EV["wload"].inc(g.dma_start(out=Wr_sb[:], in_=Wr.rearrange("(fc p) e -> p fc e", p=128)))
            EV["wload"].inc(g.dma_start(out=b2_sb[:], in_=b2))
        N_CONST_WLOADS = EV["wload"].n

        @block.vector
        def _(v):
            v.memset(T_sb[:], 0.0)
            v.memset(zer81[:], 0.0)
            v.memset(zrow[:], 0.0)
            v.memset(fill_i[:], float(HT))
            EV["fmem"].inc(v.memset(fill_f[:], 0.0))
            EV["init"].inc(v.memset(eps_t[:], 1e-5))

        @block.scalar
        def _(s):
            EV["loads"].wait(s, N_CONST_LOADS - 1)
            EV["init"].wait(s, 0)
            EV["bq8"].inc(s.activation(bqkv_sb[:, 0, :], bqkv_sb[:, 0, :],
                                       AF.Identity, scale=0.125))

        @block.sync
        def _(sync):
            EV["fmem"].wait(sync, 0)
            EV["xw"].inc(sync.dma_start(out=xln2_hbm[HT:HT + 1, :], in_=zrow[:]))

        WQKVO = (Wq, Wk, Wv, Wo)
        W_re = lambda W: W.rearrange("(fc p) o -> p fc o", p=128)

        ln_count = [0]
        tp_count = [0]
        xcw_of_xtile = {}
        xw_of_ln = {}
        xg_free = {}

        def emit_ln_tile(src_ap):
            i = ln_count[0]
            ln_count[0] += 1
            p = i % 2

            @block.vector
            def _(v):
                EV["negsum"].inc(v.tensor_reduce(
                    ln_s[:, p, 0:1], src_ap, axis=AX.X, op=ALU.add, negate=True))

            @block.scalar
            def _(s):
                EV["negsum"].wait(s, i)
                EV["negmean"].inc(s.activation(ln_s[:, p, 1:2], ln_s[:, p, 0:1],
                                               AF.Identity, scale=1.0 / DM))
                EV["negmean"].wait(s, i)
                EV["xcw"].inc(s.activation(xc_sb[:, p], src_ap, AF.Identity,
                                           bias=ln_s[:, p, 1:2]))
                EV["ssq"].inc(s.activation(sqd[:], xc_sb[:, p], AF.Square,
                                           accum_out=ln_s[:, p, 2:3]))
                EV["ssq"].wait(s, i)
                EV["std"].inc(s.activation(ln_s[:, p, 3:4], ln_s[:, p, 2:3],
                                           AF.Sqrt, scale=1.0 / DM, bias=eps_t[:]))

            @block.vector
            def _(v):
                EV["std"].wait(v, i)
                EV["rstd"].inc(v.reciprocal(ln_s[:, p, 1:2], ln_s[:, p, 3:4]))

            @block.scalar
            def _(s):
                EV["rstd"].wait(s, i)
                if tp_count[0] >= 8:
                    EV["tp"].wait(s, tp_count[0] - 5)
                if (i - 2) in xw_of_ln:
                    EV["xw"].wait(s, xw_of_ln[i - 2])
                EV["xln"].inc(s.activation(xln_sb[:, p], xc_sb[:, p],
                                           AF.Identity, scale=ln_s[:, p, 1:2]))
            return i

        def emit_transposes(dstT, dst_col0, xln_i):
            p = xln_i % 2
            for half4 in range(4):
                m = tp_count[0]
                tp_count[0] += 1

                @block.tensor
                def _(t, half4=half4, m=m):
                    if half4 == 0:
                        EV["xln"].wait(t, xln_i)
                    if m > 0:
                        EV["tpevac"].wait(t, m - 1)
                    for i2 in range(2):
                        fc = half4 * 2 + i2
                        ins = t.matmul(tp_ps[:, i2 * 128:(i2 + 1) * 128],
                                       xln_sb[:, p, fc * 128:(fc + 1) * 128],
                                       ident[:], is_transpose=True,
                                       start=(i2 == 0), stop=(i2 == 1))
                    EV["tp"].inc(ins)

                @block.vector
                def _(v, half4=half4, m=m):
                    EV["tp"].wait(v, m)
                    EV["tpevac"].inc(v.tensor_copy(
                        dstT[:, half4 * 2:(half4 + 1) * 2, dst_col0:dst_col0 + 128],
                        tp_ps[:]))

        grp_count = [0]
        qkv_count = [0]
        wo_count = [0]
        xl_count = [0]
        it_count = [0]
        h_count = [0]
        y_count = [0]
        rt_count = [0]

        def gslice(t_all, g):
            return t_all[:, g].rearrange("d h e -> d (h e)")

        for hf in range(NHALF):
            @block.sync
            def _(sync, hf=hf):
                EV["xresload"].inc(sync.dma_start(
                    out=acc[:],
                    in_=x_res.rearrange("j p d -> p j d")[:, hf * 8:(hf + 1) * 8, :]))

            @block.vector
            def _(v, hf=hf):
                EV["loads"].wait(v, N_CONST_LOADS - 1)
                EV["xresload"].wait(v, hf)
                for j in range(JT_HALF):
                    ins = v.tensor_tensor(acc[:, j], acc[:, j], bo_rep[:], ALU.add)
                EV["boadd"].inc(ins)

            @block.gpsimd
            def _(g, hf=hf):
                if hf > 0:
                    EV["yacc"].wait(g, hf * 8 - 1)
                for i, W in enumerate(WQKVO):
                    EV["wload"].inc(g.dma_start(out=WA[i][:], in_=W_re(W)))
            wl_attn_last = EV["wload"].n - 1

            for ch in range(2):
                for n in range(4):
                    tile = (hf * 2 + ch) * 4 + n
                    xi = xl_count[0]
                    xl_count[0] += 1

                    @block.sync
                    def _(sync, tile=tile, xi=xi):
                        if xi >= 2:
                            EV["xcw"].wait(sync, xcw_of_xtile[xi - 2])
                        EV["xload"].inc(sync.dma_start(
                            out=x_sb[:, xi % 2],
                            in_=x_own.rearrange("(t p) d -> t p d", p=128)[tile]))

                    @block.vector
                    def _(v, xi=xi):
                        EV["xload"].wait(v, xi)

                    ln_i = emit_ln_tile(x_sb[:, xi % 2])
                    xcw_of_xtile[xi] = ln_i
                    emit_transposes(xlnT, n * 128, ln_i)

                @block.tensor
                def _(t, hf=hf):
                    EV["tpevac"].wait(t, tp_count[0] - 1)
                    EV["bq8"].wait(t, 0)
                    EV["wload"].wait(t, wl_attn_last)
                    if hf > 0 and qkv_count[0] == hf * 96:
                        EV["yadd"].wait(t, y_count[0] - 1)
                    if wo_count[0] > 0:
                        EV["woevac"].wait(t, wo_count[0] - 1)
                    for idx in range(3 * H):
                        pi, h = divmod(idx, H)
                        gi = qkv_count[0] + idx
                        if gi >= 2:
                            EV["qkvevac"].wait(t, gi - 2)
                        ps = PP[gi % 2]
                        for fc in range(8):
                            ins = t.matmul(ps[0:64, :], WA[pi][:, fc, h * 64:(h + 1) * 64],
                                           xlnT[:, fc, :], start=(fc == 0), stop=(fc == 7))
                        EV["qkvfill"].inc(ins)

                @block.scalar
                def _(s, hf=hf):
                    if hf > 0 and qkv_count[0] == hf * 96:
                        EV["yscat"].wait(s, hf * 16 - 1)
                        EV["yacc"].wait(s, hf * 8 - 1)
                    dst_all = (q_all, k_all, v_all)
                    for idx in range(3 * H):
                        pi, h = divmod(idx, H)
                        gi = qkv_count[0] + idx
                        EV["qkvfill"].wait(s, gi)
                        EV["qkvevac"].inc(s.activation(
                            dst_all[pi][:, :, h, :], PP[gi % 2][0:64, :], AF.Identity,
                            bias=bqkv_sb[:, pi, h:h + 1],
                            scale=(0.125 if pi == 0 else 1.0)))
                qkv_count[0] += 3 * H

                g0 = grp_count[0]

                @block.tensor
                def _(t, g0=g0):
                    EV["qkvevac"].wait(t, qkv_count[0] - 1)
                    sq0 = g0 // 4
                    for sl in range(NGC // 4):
                        sq = sq0 + sl
                        pr = sq % 2
                        if sq >= 2:
                            EV["Tadd"].wait(t, sq - 2)
                        for i4 in range(4):
                            ins = t.matmul(SB_ps[pr][:, i4 * 128:(i4 + 1) * 128],
                                           gslice(q_all, sl * 4 + i4),
                                           gslice(k_all, sl * 4 + i4),
                                           start=(i4 == 0), stop=(i4 == 3))
                        EV["sqS"].inc(ins)
                        if sq >= 1:
                            EV["sqcopies"].wait(t, sq - 1)
                        for i4 in range(4):
                            ins = t.matmul(VG_ps[:, i4 * 64:(i4 + 1) * 64],
                                           gslice(v_all, sl * 4 + i4),
                                           ident[0:64, 0:64], is_transpose=True,
                                           start=(i4 == 0), stop=(i4 == 3))
                        EV["sqVG"].inc(ins)
                        EV["Pnm"].wait(t, sq)
                        for i4 in range(4):
                            ins = t.matmul(PT_ps[:, i4 * 128:(i4 + 1) * 128],
                                           Pn_sb[:, pr, i4], ident[:],
                                           is_transpose=True,
                                           start=(i4 == 0), stop=(i4 == 3))
                        EV["sqPT"].inc(ins)
                        EV["sqcopies"].wait(t, sq)
                        for pp2 in range(2):
                            pair = sq * 2 + pp2
                            if pair >= 1:
                                EV["OTevac"].wait(t, pair - 1)
                            for i2 in range(2):
                                i4 = pp2 * 2 + i2
                                ins = t.matmul(OT_ps[:, i2 * 128:(i2 + 1) * 128],
                                               Vg_sb[:, pr, i4], PT_sb[:, pr, i4],
                                               start=(i2 == 0), stop=(i2 == 1))
                            EV["sqOT"].inc(ins)

                @block.vector
                def _(v, g0=g0):
                    sq0 = g0 // 4

                    def emit_pair_evac(pair):
                        gl = (pair - sq0 * 2) * 2
                        EV["sqOT"].wait(v, pair)
                        j = gl // 16
                        rr = (gl % 16) // 2
                        ins = None
                        for sp in range(2):
                            srcap = (OT_ps.ap()
                                     .rearrange("d (g2 h u) -> d g2 h u", g2=2, h=H)
                                     [:, :, :, sp::2]
                                     .rearrange("d g2 h u -> d g2 u h"))
                            dst = (preT2.ap()
                                   .rearrange("p j c (h r) -> p j c h r", h=H)
                                   [sp * 64:(sp + 1) * 64, j, :, :, rr])
                            ins = v.tensor_copy(dst, srcap)
                        EV["OTevac"].inc(ins)

                    for sl in range(NGC // 4):
                        sq = sq0 + sl
                        pr = sq % 2
                        if sl > 0:
                            emit_pair_evac(sq * 2 - 2)
                            emit_pair_evac(sq * 2 - 1)
                        EV["sqS"].wait(v, sq)
                        mrep = mask8.ap()[:, None, :].broadcast_to([128, 4, 128])
                        EV["Tadd"].inc(v.tensor_tensor(
                            T_sb[:, pr], SB_ps[pr].ap().rearrange("p (a b) -> p a b", a=4),
                            mrep, ALU.add))
                        EV["Tadd"].wait(v, sq)
                        EV["Tmax"].inc(v.tensor_reduce(
                            smax[:, pr], T_sb[:, pr], axis=AX.X, op=ALU.max, negate=True))
                        EV["Tmax"].wait(v, sq)
                        EV["Tsub"].inc(v.tensor_tensor(
                            T_sb[:, pr], T_sb[:, pr],
                            smax[:, pr][:, :, None].broadcast_to([128, 4, 128]),
                            ALU.add))
                        EV["P"].wait(v, sq)
                        EV["Psum"].inc(v.tensor_reduce(
                            ssum[:, pr], P_sb[:, pr], axis=AX.X, op=ALU.add))
                        EV["Psum"].wait(v, sq)
                        EV["Prec"].inc(v.reciprocal(srec[:, pr], ssum[:, pr]))
                        EV["Prec"].wait(v, sq)
                        EV["Pnm"].inc(v.tensor_tensor(
                            Pn_sb[:, pr], P_sb[:, pr],
                            srec[:, pr][:, :, None].broadcast_to([128, 4, 128]),
                            ALU.mult))
                        EV["sqPT"].wait(v, sq)
                        v.tensor_copy(PT_sb[:, pr], PT_ps[:])
                        EV["sqcopies"].inc(v.tensor_copy(Vg_sb[:, pr], VG_ps[:]))
                    emit_pair_evac((sq0 + NGC // 4) * 2 - 2)
                    emit_pair_evac((sq0 + NGC // 4) * 2 - 1)

                @block.scalar
                def _(s, g0=g0):
                    sq0 = g0 // 4
                    for sl in range(NGC // 4):
                        sq = sq0 + sl
                        pr = sq % 2
                        EV["Tsub"].wait(s, sq)
                        EV["P"].inc(s.activation(P_sb[:, pr], T_sb[:, pr], AF.Exp))
                grp_count[0] += NGC

                @block.tensor
                def _(t):
                    EV["OTevac"].wait(t, grp_count[0] // 2 - 1)
                    for m in range(8):
                        jj, seg = divmod(m, 2)
                        wi = wo_count[0] + m
                        if wi >= 2:
                            EV["woevac"].wait(t, wi - 2)
                        ps = PP[wi % 2]
                        for cc in range(8):
                            ins = t.matmul(ps[:], preT2[:, jj, cc, :],
                                           WA[3][:, cc, seg * 512:(seg + 1) * 512],
                                           start=(cc == 0), stop=(cc == 7))
                        EV["wofill"].inc(ins)

                @block.vector
                def _(v, ch=ch, hf=hf):
                    EV["boadd"].wait(v, hf)
                    for m in range(8):
                        jj, seg = divmod(m, 2)
                        wi = wo_count[0] + m
                        EV["wofill"].wait(v, wi)
                        jl = ch * 4 + jj
                        sl = slice(seg * 512, (seg + 1) * 512)
                        EV["woevac"].inc(v.tensor_tensor(
                            acc[:, jl, sl], acc[:, jl, sl], PP[wi % 2][:], ALU.add))
                wo_count[0] += 8

            for jl in range(JT_HALF):
                ln_i = emit_ln_tile(acc[:, jl])
                emit_transposes(xfT, jl * 128, ln_i)

                @block.sync
                def _(sync, jl=jl, ln_i=ln_i, hf=hf):
                    if hf > 0 and jl == 0:
                        EV["xgdma"].wait(sync, hf * 16 - 1)
                    EV["xln"].wait(sync, ln_i)
                    EV["xw"].inc(sync.dma_start(
                        out=xln2_hbm[jl * 128:(jl + 1) * 128, :],
                        in_=xln_sb[:, ln_i % 2]))
                xw_of_ln[ln_i] = EV["xw"].n - 1

            for jl in range(JT_HALF):
                ri = rt_count[0]
                rt_count[0] += 1

                @block.tensor
                def _(t, jl=jl, ri=ri):
                    EV["tpevac"].wait(t, tp_count[0] - (JT_HALF - jl - 1) * 4 - 1)
                    if ri > 0:
                        EV["rlogv"].wait(t, ri - 1)
                    EV["wload"].wait(t, N_CONST_WLOADS - 1)
                    for fc in range(8):
                        ins = t.matmul(SB_ps[0][:, 256:256 + NE],
                                       xfT[:, fc, jl * 128:(jl + 1) * 128],
                                       Wr_sb[:, fc, :], start=(fc == 0), stop=(fc == 7))
                    EV["rfill"].inc(ins)

                @block.vector
                def _(v, ri=ri):
                    EV["rfill"].wait(v, ri)
                    EV["rlogv"].inc(v.tensor_tensor(rlog[:], SB_ps[0][:, 256:256 + NE],
                                                    br_rep[:], ALU.add))
                    EV["rlogv"].wait(v, ri)
                    EV["rmax"].inc(v.tensor_reduce(rneg[:], rlog[:], axis=AX.X,
                                                   op=ALU.max, negate=True))

                @block.scalar
                def _(s, ri=ri):
                    EV["rmax"].wait(s, ri)
                    EV["rexp"].inc(s.activation(rmsk[:], rlog[:], AF.Exp, bias=rneg[:]))

                @block.vector
                def _(v, jl=jl, ri=ri):
                    EV["rexp"].wait(v, ri)
                    EV["rsum_"].inc(v.tensor_reduce(rsum[:], rmsk[:], axis=AX.X, op=ALU.add))
                    EV["rsum_"].wait(v, ri)
                    EV["rgate"].inc(v.reciprocal(rgat[:], rsum[:]))
                    EV["rmask"].inc(v.tensor_scalar(out=rmsk[:], in0=rlog[:],
                                                    scalar1=rneg[:], scalar2=0.0,
                                                    op0=ALU.add, op1=ALU.is_ge))
                    EV["rgate"].wait(v, ri)
                    EV["rmask"].wait(v, ri)
                    EV["rsg"].inc(v.tensor_scalar(out=S_gates[:, jl, :], in0=rmsk[:],
                                                  scalar1=rgat[:], scalar2=None,
                                                  op0=ALU.mult))
                    EV["rsg"].wait(v, ri)
                    EV["rsbf"].inc(v.tensor_copy(S_bf[:], S_gates[:, jl, :]))

                @block.tensor
                def _(t, ri=ri):
                    EV["rsbf"].wait(t, ri)
                    if ri > 0:
                        EV["stTevac"].wait(t, ri - 1)
                    EV["stT"].inc(t.matmul(PT_ps[0:NE, 0:128], S_bf[:], ident[:],
                                           is_transpose=True, start=True, stop=True))

                @block.vector
                def _(v, ri=ri, jl=jl):
                    EV["stT"].wait(v, ri)
                    EV["stTevac"].inc(v.tensor_copy(ST_sb[:], PT_ps[0:NE, 0:128]))
                    v.tensor_scalar(out=maskT[:, jl * 128:(jl + 1) * 128],
                                    in0=ST_sb[:], scalar1=0.0, scalar2=None,
                                    op0=ALU.is_gt)
                    v.tensor_copy(gate_tok[:, jl:jl + 1], rgat[:])

                @block.tensor
                def _(t, ri=ri):
                    EV["stTevac"].wait(t, ri)
                    if ri > 0:
                        EV["b2acc"].wait(t, 2 * ri - 1)
                    for seg in range(2):
                        EV["b2f"].inc(t.matmul(PP[seg][:], ST_sb[:],
                                               b2_sb[:, seg * 512:(seg + 1) * 512],
                                               start=True, stop=True))

                @block.vector
                def _(v, jl=jl, ri=ri):
                    for seg in range(2):
                        EV["b2f"].wait(v, 2 * ri + seg)
                        sl = slice(seg * 512, (seg + 1) * 512)
                        EV["b2acc"].inc(v.tensor_tensor(
                            acc[:, jl, sl], acc[:, jl, sl], PP[seg][:], ALU.add))

            @block.vector
            def _(v, hf=hf):
                EV["scand"].inc(v.tensor_tensor_scan(
                    csum[:], maskT[:], zer81.ap().broadcast_to([8, HT]), 0.0,
                    op0=ALU.add, op1=ALU.add))
                EV["scand"].wait(v, hf)
                v.tensor_scalar(out=csum[:], in0=csum[:], scalar1=-1.0,
                                scalar2=None, op0=ALU.add)
                EV["csm1"].inc(v.tensor_tensor(mp_bf[:], csum[:], maskT[:], ALU.mult))

            @block.tensor
            def _(t, hf=hf):
                EV["csm1"].wait(t, hf)
                EV["rlogv"].wait(t, rt_count[0] - 1)
                for ck in range(2):
                    t.matmul(SB_ps[ck][0:1, 0:512], evc_sb[:, 1:2],
                             maskT[:, ck * 512:(ck + 1) * 512],
                             start=True, stop=False)
                    EV["gmm"].inc(t.matmul(SB_ps[ck][0:1, 0:512], evc_sb[:, 0:1],
                                           mp_bf[:, ck * 512:(ck + 1) * 512],
                                           start=False, stop=True))

            @block.vector
            def _(v, hf=hf):
                for ck in range(2):
                    EV["gmm"].wait(v, hf * 2 + ck)
                    EV["gevac"].inc(v.tensor_copy(
                        g_row[0:1, ck * 512:(ck + 1) * 512], SB_ps[ck][0:1, 0:512]))

            @block.gpsimd
            def _(g, hf=hf):
                EV["gevac"].wait(g, hf * 2 + 1)
                g.dma_start(out=g_hbm.rearrange("(o f) o2 -> o (f o2)", o=1),
                            in_=g_row[:]).then_inc(EV["gtr"].h, 16)
                EV["gtr"].n += 1
                EV["gtr"].wait(g, hf * 2)
                EV["gtr"].inc(g.dma_start(
                    out=g_f[:],
                    in_=g_hbm.rearrange("(j p) o -> p (j o)", p=128),
                    allow_slow_non_contiguous=True))

            @block.vector
            def _(v, hf=hf):
                EV["gtr"].wait(v, hf * 2 + 1)
                EV["gidx"].inc(v.tensor_copy(g_i[:], g_f[:]))
                EV["gpe"].inc(v.engine_nop())

            @block.gpsimd
            def _(g, hf=hf):
                EV["fmem"].wait(g, 0)
                if hf > 0:
                    EV["traw"].wait(g, hf * 2 - 1)
                    EV["yrld"].wait(g, hf * 8 - 1)
                EV["tfill"].inc(g.dma_start(
                    out=tok_tbl.rearrange("(p c) o -> p (c o)", c=16), in_=fill_i[:]))
                EV["tfill"].inc(g.dma_start(
                    out=gate_tbl.rearrange("(p c) o -> p (c o)", c=16), in_=fill_f[:]))
                EV["gidx"].wait(g, hf)
                for jl in range(8):
                    EV["scat"].inc(g.indirect_dma_start(
                        out=tok_tbl,
                        out_offset=IndirectOffsetOnAxis(ap=g_i[:, jl:jl + 1], axis=0),
                        in_=iota_sb[:, jl:jl + 1], in_offset=None))
                    EV["scat"].inc(g.indirect_dma_start(
                        out=gate_tbl,
                        out_offset=IndirectOffsetOnAxis(ap=g_i[:, jl:jl + 1], axis=0),
                        in_=gate_tok[:, jl:jl + 1], in_offset=None))

            @block.sync
            def _(sync, hf=hf):
                EV["scat"].wait(sync, (hf + 1) * 16 - 1)
                EV["traw"].inc(sync.dma_start(
                    out=tok_raw[:], in_=tok_tbl.rearrange("(c p) o -> c (p o)", p=128)))
                EV["traw"].inc(sync.dma_start(
                    out=gate_raw[:], in_=gate_tbl.rearrange("(c p) o -> c (p o)", p=128)))

            @block.tensor
            def _(t, hf=hf):
                EV["traw"].wait(t, (hf + 1) * 2 - 1)
                EV["gpe"].wait(t, hf)
                for k, raw in enumerate((tok_raw, gate_raw)):
                    ins = t.matmul(PP[0][:, k * 16:(k + 1) * 16], raw[:],
                                   idf32[:], is_transpose=True,
                                   start=(k == 0), stop=(k == 1))
                EV["ttr"].inc(ins)

            @block.vector
            def _(v, hf=hf):
                EV["ttr"].wait(v, hf)
                if hf > 0:
                    EV["yscat"].wait(v, hf * 16 - 1)
                v.tensor_copy(tok_f[:], PP[0][:, 0:16])
                v.tensor_copy(gate_slots[:], PP[0][:, 16:32])
                EV["tbl"].inc(v.tensor_copy(tok_i[:], tok_f[:]))

            for e in range(NE):
                @block.gpsimd
                def _(g, e=e, hf=hf):
                    EV["tbl"].wait(g, hf)
                    EV["xw"].wait(g, (hf + 1) * 8)
                    prev = (hf, e - 1) if e >= 1 else (hf - 1, NE - 1)
                    if prev in xg_free:
                        EV["tp"].wait(g, xg_free[prev])
                    for tile in range(2):
                        EV["xgdma"].inc(g.indirect_dma_start(
                            out=xg[:, tile], out_offset=None,
                            in_=xln2_hbm,
                            in_offset=IndirectOffsetOnAxis(
                                ap=tok_i[:, e * 2 + tile:e * 2 + tile + 1], axis=0)))

                @block.tensor
                def _(t, e=e, hf=hf):
                    prev_it = hf * ITERS_HALF + e * 4 - 1
                    if e >= 1:
                        EV["hfill"].wait(t, (prev_it + 1) * 8 - 1)
                    for tile in range(2):
                        EV["xgdma"].wait(t, hf * 16 + e * 2 + tile)
                        for half4 in range(4):
                            m = tp_count[0]
                            tp_count[0] += 1
                            if m > 0:
                                EV["tpevac"].wait(t, m - 1)
                            for i2 in range(2):
                                fc = half4 * 2 + i2
                                ins = t.matmul(tp_ps[:, i2 * 128:(i2 + 1) * 128],
                                               xg[:, tile, fc * 128:(fc + 1) * 128],
                                               ident[:], is_transpose=True,
                                               start=(i2 == 0), stop=(i2 == 1))
                            EV["tp"].inc(ins)
                xg_free[(hf, e)] = tp_count[0] - 1

                tpe_hold = tp_count[0]

                @block.vector
                def _(v, e=e, hf=hf, tpe_hold=tpe_hold):
                    for k in range(8):
                        m = tpe_hold - 8 + k
                        tile, half4 = divmod(k, 4)
                        EV["tp"].wait(v, m)
                        EV["tpevac"].inc(v.tensor_copy(
                            xgT[:, half4 * 2:(half4 + 1) * 2,
                                tile * 128:(tile + 1) * 128],
                            tp_ps[:]))

                for qt in range(NQT):
                    it = it_count[0]
                    it_count[0] += 1
                    buf = it % 2
                    W1q = WA[buf * 2]
                    W2q = WA[buf * 2 + 1]
                    w1_ord = N_CONST_WLOADS + (hf + 1) * 4 + it * 2
                    w2_ord = w1_ord + 1

                    @block.gpsimd
                    def _(g, e=e, qt=qt, it=it, hf=hf, W1q=W1q, W2q=W2q):
                        if it == hf * ITERS_HALF:
                            EV["qkvfill"].wait(g, qkv_count[0] - 1)
                            EV["wofill"].wait(g, wo_count[0] - 1)
                        if it >= hf * ITERS_HALF + 2:
                            EV["yfill"].wait(g, (it - 1) * 4 - 1)
                        EV["wload"].inc(g.dma_start(
                            out=W1q[:],
                            in_=W1[e, :, qt * 1024:(qt + 1) * 1024]
                                .rearrange("(fc p) o -> p fc o", p=128)))
                        EV["wload"].inc(g.dma_start(
                            out=W2q[:],
                            in_=W2[e, qt * 1024:(qt + 1) * 1024, :]
                                .rearrange("(fc p) o -> p fc o", p=128)))

                    @block.tensor
                    def _(t, e=e, qt=qt, it=it, hf=hf, W1q=W1q, W2q=W2q,
                          w1_ord=w1_ord, w2_ord=w2_ord, tpe_hold=tpe_hold):
                        EV["wload"].wait(t, w1_ord)
                        if it == hf * ITERS_HALF:
                            EV["gidx"].wait(t, hf)
                        if qt == 0:
                            EV["tpevac"].wait(t, tpe_hold - 1)
                        for m in range(8):
                            hi = h_count[0]
                            h_count[0] += 1
                            if hi >= 2:
                                EV["hevac"].wait(t, hi - 2)
                            hb = SB_ps[hi % 2]
                            for fc in range(8):
                                ins = t.matmul(hb[:, 0:CAP],
                                               W1q[:, fc, m * 128:(m + 1) * 128],
                                               xgT[:, fc, :],
                                               start=(fc == 0), stop=(fc == 7))
                            EV["hfill"].inc(ins)
                        EV["wload"].wait(t, w2_ord)
                        if it == hf * ITERS_HALF:
                            EV["gpe"].wait(t, hf)
                        for tile in range(2):
                            for dseg in range(2):
                                yi = y_count[0]
                                y_count[0] += 1
                                if yi >= 2:
                                    EV["yadd"].wait(t, yi - 2)
                                yb = PP[yi % 2]
                                EV["hevac"].wait(t, it * 8 + 7)
                                for Fc in range(8):
                                    ins = t.matmul(
                                        yb[:],
                                        h_sb[:, Fc, tile * 128:(tile + 1) * 128],
                                        W2q[:, Fc, dseg * 512:(dseg + 1) * 512],
                                        start=(Fc == 0), stop=(Fc == 7))
                                EV["yfill"].inc(ins)

                    @block.scalar
                    def _(s, e=e, qt=qt, it=it, hf=hf):
                        if it > 0:
                            EV["yfill"].wait(s, it * 4 - 1)
                        for m in range(8):
                            hi = it * 8 + m
                            EV["hfill"].wait(s, hi)
                            EV["hevac"].inc(s.activation(
                                h_sb[:, m, :], SB_ps[hi % 2][:, 0:CAP], AF.Gelu,
                                bias=b1_sb[:, e, qt * 8 + m:qt * 8 + m + 1]))

                    @block.vector
                    def _(v, e=e, qt=qt, it=it, hf=hf):
                        for tile in range(2):
                            for dseg in range(2):
                                yi = it * 4 + tile * 2 + dseg
                                if qt == 0 and tile == 0 and dseg == 0                                         and (hf * 8 + e) >= 1:
                                    EV["yscat"].wait(v, (hf * 8 + e) * 2 - 1)
                                EV["yfill"].wait(v, yi)
                                sl = slice(dseg * 512, (dseg + 1) * 512)
                                if qt == 0:
                                    ins = v.tensor_copy(y_stage[:, tile, sl],
                                                        PP[yi % 2][:])
                                else:
                                    ins = v.tensor_tensor(y_stage[:, tile, sl],
                                                          y_stage[:, tile, sl],
                                                          PP[yi % 2][:], ALU.add)
                                EV["yadd"].inc(ins)

                @block.scalar
                def _(s, e=e, hf=hf):
                    last_it = hf * ITERS_HALF + e * 4 + 3
                    EV["yadd"].wait(s, (last_it + 1) * 4 - 1)
                    for tile in range(2):
                        EV["ysc"].inc(s.activation(
                            y_stage[:, tile], y_stage[:, tile], AF.Identity,
                            scale=gate_slots[:, e * 2 + tile:e * 2 + tile + 1]))

                @block.gpsimd
                def _(g, e=e, hf=hf):
                    EV["ysc"].wait(g, (hf * 8 + e) * 2 + 1)
                    for tile in range(2):
                        EV["yscat"].inc(g.indirect_dma_start(
                            out=y_hbm,
                            out_offset=IndirectOffsetOnAxis(
                                ap=tok_i[:, e * 2 + tile:e * 2 + tile + 1], axis=0),
                            in_=y_stage[:, tile], in_offset=None))

            for jl in range(JT_HALF):
                @block.sync
                def _(sync, jl=jl, hf=hf):
                    if jl == 0:
                        EV["yscat"].wait(sync, (hf + 1) * 16 - 1)
                    if hf * 8 + jl >= 2:
                        EV["yacc"].wait(sync, hf * 8 + jl - 2)
                    EV["yrld"].inc(sync.dma_start(
                        out=yld[:, jl % 2],
                        in_=y_hbm[jl * 128:(jl + 1) * 128, :]))

                @block.vector
                def _(v, jl=jl, hf=hf):
                    EV["yrld"].wait(v, hf * 8 + jl)
                    EV["yacc"].inc(v.tensor_tensor(
                        acc[:, jl], acc[:, jl], yld[:, jl % 2], ALU.add))

            @block.sync
            def _(sync, hf=hf):
                EV["yacc"].wait(sync, (hf + 1) * 8 - 1)
                EV["store"].inc(sync.dma_start(
                    out=out.rearrange("j p d -> p j d")[:, hf * 8:(hf + 1) * 8, :],
                    in_=acc[:]))

        @block.sync
        def _(sync):
            for name in DMA_EV:
                if EV[name].n:
                    sync.wait_ge(EV[name].h, EV[name].step * EV[name].n)
    return nc


_NC_CACHE = {}


def _get_nc():
    if "nc" not in _NC_CACHE:
        _NC_CACHE["nc"] = build()
    return _NC_CACHE["nc"]


def kernel(L0_attn_hidden_state, L0_attn_Wq, L0_attn_bq, L0_attn_Wk, L0_attn_bk,
           L0_attn_Wv, L0_attn_bv, L0_attn_Wo, L0_attn_bo,
           L0_ffn_W1, L0_ffn_b1, L0_ffn_W2, L0_ffn_b2, L0_ffn_Wr, L0_ffn_br):
    x = np.ascontiguousarray(np.asarray(L0_attn_hidden_state, dtype=np.float32))
    B, S, _ = x.shape
    nc = _get_nc()

    ident = np.eye(128, dtype=bfd)
    ii, jj = np.meshgrid(np.arange(128), np.arange(128), indexing="ij")
    mask8 = np.where(ii % 8 == jj % 8, 0.0, -1e30).astype(np.float32)
    idf32 = np.eye(16, dtype=np.float32)
    evc = np.stack([np.ones(8), np.arange(8) * 256.0], axis=1).astype(bfd)
    iota = (np.arange(8)[None, :] * 128 + np.arange(128)[:, None]).astype(np.float32)

    common = dict(
        Wq=np.ascontiguousarray(L0_attn_Wq, np.float32),
        Wk=np.ascontiguousarray(L0_attn_Wk, np.float32),
        Wv=np.ascontiguousarray(L0_attn_Wv, np.float32),
        Wo=np.ascontiguousarray(L0_attn_Wo, np.float32),
        bq=np.ascontiguousarray(L0_attn_bq, np.float32),
        bk=np.ascontiguousarray(L0_attn_bk, np.float32),
        bv=np.ascontiguousarray(L0_attn_bv, np.float32),
        bo=np.ascontiguousarray(L0_attn_bo, np.float32),
        W1=np.ascontiguousarray(L0_ffn_W1, np.float32),
        b1=np.ascontiguousarray(L0_ffn_b1, np.float32),
        W2=np.ascontiguousarray(L0_ffn_W2, np.float32),
        b2=np.ascontiguousarray(L0_ffn_b2, np.float32),
        Wr=np.ascontiguousarray(L0_ffn_Wr, np.float32),
        br=np.ascontiguousarray(L0_ffn_br, np.float32),
        ident_hbm=ident, mask_hbm=mask8,
        idf32_hbm=idf32, evc_hbm=evc, iota_hbm=iota,
    )

    in_maps = []
    for c in range(8):
        b, c2 = divmod(c, 2)
        x_own = np.ascontiguousarray(x[b, 2048 * c2: 2048 * (c2 + 1)])
        xr = x[b].reshape(16, 2, 16, 8, DM)[:, c2]
        x_res = np.ascontiguousarray(
            xr.transpose(1, 0, 2, 3).reshape(16, 128, DM))
        im = dict(common)
        im["x_own"] = x_own
        im["x_res"] = x_res
        in_maps.append(im)

    res = run_bass_kernel_spmd(nc, in_maps, core_ids=list(range(8)))

    out = np.empty((B, S, DM), dtype=np.float32)
    for c in range(8):
        b, c2 = divmod(c, 2)
        oc = res.results[c]["out"]
        ov = oc.reshape(16, 16, 8, DM).transpose(1, 0, 2, 3)
        out[b].reshape(16, 2, 16, 8, DM)[:, c2] = ov
    return out


# version 05
# speedup vs baseline: 1.4931x; 1.0044x over previous
import numpy as np
import ml_dtypes
import concourse.bass as bass
import concourse.mybir as mybir
from concourse.bass_utils import run_bass_kernel_spmd
from concourse.bass import IndirectOffsetOnAxis

F32 = mybir.dt.float32
I32 = mybir.dt.int32
BF16 = mybir.dt.bfloat16
AF = mybir.ActivationFunctionType
ALU = mybir.AluOpType
AX = mybir.AxisListType
bfd = ml_dtypes.bfloat16

H = 16
D = 64
DM = 1024
FF = 4096
NE = 8
CT = 512
NGC = CT // 8
NQT = 4
NHALF = 2
JT_HALF = 8
HT = 1024
ITERS_HALF = NE * NQT
CAP = 256


class Evt:
    def __init__(self, stack, nc, name, step=1):
        self.h = stack.enter_context(nc.semaphore(name))
        self.step = step
        self.n = 0

    def inc(self, instr):
        instr.then_inc(self.h, self.step)
        self.n += 1
        return instr

    def wait(self, eng, i):
        assert 0 <= i < self.n or True
        eng.wait_ge(self.h, self.step * (i + 1))


def build():
    nc = bass.Bass("TRN2", target_bir_lowering=False,
                   detect_race_conditions=False)
    x_own = nc.dram_tensor("x_own", [2048, DM], F32, kind="ExternalInput").ap()
    x_res = nc.dram_tensor("x_res", [16, 128, DM], F32, kind="ExternalInput").ap()
    Wq = nc.dram_tensor("Wq", [DM, DM], F32, kind="ExternalInput").ap()
    Wk = nc.dram_tensor("Wk", [DM, DM], F32, kind="ExternalInput").ap()
    Wv = nc.dram_tensor("Wv", [DM, DM], F32, kind="ExternalInput").ap()
    Wo = nc.dram_tensor("Wo", [DM, DM], F32, kind="ExternalInput").ap()
    bq = nc.dram_tensor("bq", [DM], F32, kind="ExternalInput").ap()
    bk = nc.dram_tensor("bk", [DM], F32, kind="ExternalInput").ap()
    bv = nc.dram_tensor("bv", [DM], F32, kind="ExternalInput").ap()
    bo = nc.dram_tensor("bo", [DM], F32, kind="ExternalInput").ap()
    W1 = nc.dram_tensor("W1", [NE, DM, FF], F32, kind="ExternalInput").ap()
    b1 = nc.dram_tensor("b1", [NE, FF], F32, kind="ExternalInput").ap()
    W2 = nc.dram_tensor("W2", [NE, FF, DM], F32, kind="ExternalInput").ap()
    b2 = nc.dram_tensor("b2", [NE, DM], F32, kind="ExternalInput").ap()
    Wr = nc.dram_tensor("Wr", [DM, NE], F32, kind="ExternalInput").ap()
    br = nc.dram_tensor("br", [NE], F32, kind="ExternalInput").ap()
    ident_hbm = nc.dram_tensor("ident_hbm", [128, 128], BF16, kind="ExternalInput").ap()
    idf32_hbm = nc.dram_tensor("idf32_hbm", [16, 16], F32, kind="ExternalInput").ap()
    mask_hbm = nc.dram_tensor("mask_hbm", [128, 128], F32, kind="ExternalInput").ap()
    evc_hbm = nc.dram_tensor("evc_hbm", [8, 2], BF16, kind="ExternalInput").ap()
    iota_hbm = nc.dram_tensor("iota_hbm", [128, 8], F32, kind="ExternalInput").ap()
    out = nc.dram_tensor("out", [16, 128, DM], F32, kind="ExternalOutput").ap()
    xln2_hbm = nc.dram_tensor("xln2_hbm", [HT + 1, DM], BF16, kind="Internal").ap()
    tok_tbl = nc.dram_tensor("tok_tbl", [NE * CAP, 1], F32, kind="Internal").ap()
    gate_tbl = nc.dram_tensor("gate_tbl", [NE * CAP, 1], F32, kind="Internal").ap()
    y_hbm = nc.dram_tensor("y_hbm", [HT + 1, DM], F32, kind="Internal").ap()
    g_hbm = nc.dram_tensor("g_hbm", [HT, 1], F32, kind="Internal").ap()

    from contextlib import ExitStack
    st = ExitStack()
    with st:
        sbt = lambda *a: st.enter_context(nc.sbuf_tensor(*a))
        pst = lambda *a: st.enter_context(nc.psum_tensor(*a))

        ident = sbt("ident", [128, 128], BF16)
        mask8 = sbt("mask8", [128, 128], F32)
        bqkv_sb = sbt("bqkv_sb", [64, 3, H], F32)
        bo_rep = sbt("bo_rep", [128, DM], F32)
        br_rep = sbt("br_rep", [128, NE], F32)
        Wr_sb = sbt("Wr_sb", [128, 8, NE], BF16)
        b1_sb = sbt("b1_sb", [128, NE, 32], F32)
        b2_sb = sbt("b2_sb", [NE, DM], BF16)
        eps_t = sbt("eps_t", [128, 1], F32)
        ln_s = sbt("ln_s", [128, 2, 4], F32)
        sqd = sbt("sqd", [128, DM], F32)
        acc = sbt("acc", [128, JT_HALF, DM], F32)
        WA = [sbt(f"wa{i}", [128, 8, DM], BF16) for i in range(4)]
        x_sb = sbt("x_sb", [128, 2, DM], F32)
        xc_sb = sbt("xc_sb", [128, 2, DM], F32)
        xln_sb = sbt("xln_sb", [128, 2, DM], BF16)
        xlnT = sbt("xlnT", [128, 8, CT], BF16)
        preT2 = sbt("preT2", [128, 4, 8, 128], BF16)
        T_sb = sbt("T_sb", [128, 2, 4, 128], F32)
        P_sb = T_sb
        Pn_sb = sbt("Pn_sb", [128, 2, 4, 128], BF16)
        PT_sb = sbt("PT_sb", [128, 2, 4, 128], BF16)
        Vg_sb = sbt("Vg_sb", [128, 2, 4, D], BF16)
        smax = sbt("smax", [128, 2, 4], F32)
        ssum = sbt("ssum", [128, 2, 4], F32)
        srec = sbt("srec", [128, 2, 4], F32)
        S_gates = sbt("S_gates", [128, JT_HALF, NE], F32)
        rlog = sbt("rlog", [128, NE], F32)
        rmsk = sbt("rmsk", [128, NE], F32)
        rneg = sbt("rneg", [128, 1], F32)
        rsum = sbt("rsum", [128, 1], F32)
        rgat = sbt("rgat", [128, 1], F32)
        S_bf = sbt("S_bf", [128, NE], BF16)
        ST_sb = sbt("ST_sb", [NE, 128], BF16)
        idf32 = sbt("idf32", [16, 16], F32)
        evc_sb = sbt("evc_sb", [8, 2], BF16)
        iota_sb = sbt("iota_sb", [128, 8], F32)
        zer81 = sbt("zer81", [8, 1], F32)
        g_f = sbt("g_f", [128, 8], F32)
        g_i = sbt("g_i", [128, 8], I32)
        gate_tok = sbt("gate_tok", [128, 8], F32)
        tok_i = sbt("tok_i", [128, 16], I32)
        tok_f = sbt("tok_f", [128, 16], F32)
        gate_slots = sbt("gate_slots", [128, 16], F32)
        fill_i = sbt("fill_i", [128, 16], F32)
        fill_f = sbt("fill_f", [128, 16], F32)
        tok_raw = sbt("tok_raw", [16, 128], F32)
        gate_raw = sbt("gate_raw", [16, 128], F32)
        qkv_region = sbt("qkv_region", [128, 48 * 1024 // 4], F32)
        base = nc.lookup_mloc(qkv_region).addr
        q_all = nc.alloc_sbuf_tensor_at("q_all", [64, NGC, H, 8], BF16, offset=base)
        k_all = nc.alloc_sbuf_tensor_at("k_all", [64, NGC, H, 8], BF16, offset=base + 16 * 1024)
        v_all = nc.alloc_sbuf_tensor_at("v_all", [64, NGC, H, 8], BF16, offset=base + 32 * 1024)
        xfT = nc.alloc_sbuf_tensor_at("xfT", [128, 8, 1024], BF16, offset=base)
        h_sb = nc.alloc_sbuf_tensor_at("h_sb", [128, 8, CAP], BF16, offset=base + 16 * 1024)
        xg = nc.alloc_sbuf_tensor_at("xg", [128, 2, DM], BF16, offset=base + 20 * 1024)
        xgT = nc.alloc_sbuf_tensor_at("xgT", [128, 8, CAP], BF16, offset=base + 24 * 1024)
        y_stage = nc.alloc_sbuf_tensor_at("y_stage", [128, 2, DM], F32, offset=base + 28 * 1024)
        yld = nc.alloc_sbuf_tensor_at("yld", [128, 2, DM], F32, offset=base + 36 * 1024)
        pre_base = nc.lookup_mloc(preT2).addr
        maskT = nc.alloc_sbuf_tensor_at("maskT", [8, HT], BF16, offset=pre_base)
        mp_bf = nc.alloc_sbuf_tensor_at("mp_bf", [8, HT], BF16, offset=pre_base + 2 * 1024)
        csum = nc.alloc_sbuf_tensor_at("csum", [8, HT], F32, offset=pre_base + 4 * 1024)
        g_row = nc.alloc_sbuf_tensor_at("g_row", [1, HT], F32, offset=nc.lookup_mloc(sqd).addr)
        zrow = nc.alloc_sbuf_tensor_at("zrow", [1, DM], BF16, offset=base + 28 * 1024)

        PP = [pst("pp0", [128, 512], F32), pst("pp1", [128, 512], F32)]
        SB_ps = [pst("S0", [128, 512], F32), pst("S1", [128, 512], F32)]
        tp_ps = pst("tp", [128, 256], BF16)
        PT_ps = pst("PT0", [128, 512], BF16)
        VG_ps = pst("VG0", [128, 4 * D], BF16)
        OT_ps = pst("OT0", [64, 256], F32)

        ev_names = [
            "loads", "wload", "store", "init", "bq8",
            "xload", "xresload", "negsum", "negmean", "ssq", "std", "rstd",
            "xln", "xcw", "tp", "tpevac", "qkvfill", "qkvevac",
            "sqS", "sqVG", "sqPT", "sqOT", "Tadd", "Tmax", "Tsub", "P",
            "Psum", "Prec", "Pnm", "sqcopies", "OTevac", "wofill", "woevac", "boadd",
            "rfill", "rlogv", "rmax", "rexp", "rsum_", "rgate", "rmask",
            "rsg", "rsbf", "stT", "stTevac", "b2f", "b2acc",
            "hfill", "hevac", "yfill", "yadd", "ysc", "yacc",
            "xw", "scand", "csm1", "gmm", "gevac", "gtr", "gpe", "gidx",
            "tfill", "scat", "tbl", "traw", "ttr", "xgdma", "xgtr", "xgevac",
            "yscat", "yrld", "fmem",
        ]
        DMA_EV = ("loads", "wload", "store", "xload", "xresload",
                  "xw", "tfill", "scat", "traw", "xgdma", "yscat", "yrld", "gtr")
        EV = {n: Evt(st, nc, n, 16 if n in DMA_EV else 1) for n in ev_names}

        block = st.enter_context(nc.Block())

        @block.sync
        def _(sync):
            EV["loads"].inc(sync.dma_start(out=ident[:], in_=ident_hbm))
            EV["loads"].inc(sync.dma_start(out=mask8[:], in_=mask_hbm))
            EV["loads"].inc(sync.dma_start(out=bo_rep[:], in_=bo[None, :].broadcast_to([128, DM])))
            EV["loads"].inc(sync.dma_start(out=br_rep[:], in_=br[None, :].broadcast_to([128, NE])))
            for i, bsrc in enumerate((bq, bk, bv)):
                EV["loads"].inc(sync.dma_start(
                    out=bqkv_sb[:, i, :], in_=bsrc.rearrange("(h d) -> d h", d=64),
                    allow_slow_non_contiguous=True))
            EV["loads"].inc(sync.dma_start(
                out=b1_sb[:], in_=b1.rearrange("e (mq p) -> p e mq", p=128),
                allow_slow_non_contiguous=True))
            EV["loads"].inc(sync.dma_start(out=idf32[:], in_=idf32_hbm))
            EV["loads"].inc(sync.dma_start(out=evc_sb[:], in_=evc_hbm))
            EV["loads"].inc(sync.dma_start(out=iota_sb[:], in_=iota_hbm))
        N_CONST_LOADS = EV["loads"].n

        @block.gpsimd
        def _(g):
            EV["wload"].inc(g.dma_start(out=Wr_sb[:], in_=Wr.rearrange("(fc p) e -> p fc e", p=128)))
            EV["wload"].inc(g.dma_start(out=b2_sb[:], in_=b2))
        N_CONST_WLOADS = EV["wload"].n

        @block.vector
        def _(v):
            v.memset(T_sb[:], 0.0)
            v.memset(zer81[:], 0.0)
            v.memset(zrow[:], 0.0)
            v.memset(fill_i[:], float(HT))
            EV["fmem"].inc(v.memset(fill_f[:], 0.0))
            EV["init"].inc(v.memset(eps_t[:], 1e-5))

        @block.scalar
        def _(s):
            EV["loads"].wait(s, N_CONST_LOADS - 1)
            EV["init"].wait(s, 0)
            EV["bq8"].inc(s.activation(bqkv_sb[:, 0, :], bqkv_sb[:, 0, :],
                                       AF.Identity, scale=0.125))

        @block.sync
        def _(sync):
            EV["fmem"].wait(sync, 0)
            EV["xw"].inc(sync.dma_start(out=xln2_hbm[HT:HT + 1, :], in_=zrow[:]))

        WQKVO = (Wq, Wk, Wv, Wo)
        W_re = lambda W: W.rearrange("(fc p) o -> p fc o", p=128)

        ln_count = [0]
        tp_count = [0]
        xcw_of_xtile = {}
        xw_of_ln = {}
        xg_free = {}

        def emit_ln_tile(src_ap):
            i = ln_count[0]
            ln_count[0] += 1
            p = i % 2

            @block.vector
            def _(v):
                EV["negsum"].inc(v.tensor_reduce(
                    ln_s[:, p, 0:1], src_ap, axis=AX.X, op=ALU.add, negate=True))
                EV["negsum"].wait(v, i)
                EV["negmean"].inc(v.tensor_scalar_mul(
                    ln_s[:, p, 1:2], ln_s[:, p, 0:1], 1.0 / DM))
                EV["negmean"].wait(v, i)
                EV["xcw"].inc(v.tensor_scalar(
                    out=xc_sb[:, p], in0=src_ap, scalar1=ln_s[:, p, 1:2],
                    scalar2=None, op0=ALU.add))

            @block.scalar
            def _(s):
                EV["xcw"].wait(s, i)
                EV["ssq"].inc(s.activation(sqd[:], xc_sb[:, p], AF.Square,
                                           accum_out=ln_s[:, p, 2:3]))
                EV["ssq"].wait(s, i)
                EV["std"].inc(s.activation(ln_s[:, p, 3:4], ln_s[:, p, 2:3],
                                           AF.Sqrt, scale=1.0 / DM, bias=eps_t[:]))

            @block.vector
            def _(v):
                EV["std"].wait(v, i)
                EV["rstd"].inc(v.reciprocal(ln_s[:, p, 1:2], ln_s[:, p, 3:4]))
                EV["rstd"].wait(v, i)
                if tp_count[0] >= 8:
                    EV["tp"].wait(v, tp_count[0] - 5)
                if (i - 2) in xw_of_ln:
                    EV["xw"].wait(v, xw_of_ln[i - 2])
                EV["xln"].inc(v.tensor_scalar_mul(
                    xln_sb[:, p], xc_sb[:, p], ln_s[:, p, 1:2]))
            return i

        def emit_transposes(dstT, dst_col0, xln_i):
            p = xln_i % 2
            for half4 in range(4):
                m = tp_count[0]
                tp_count[0] += 1

                @block.tensor
                def _(t, half4=half4, m=m):
                    if half4 == 0:
                        EV["xln"].wait(t, xln_i)
                    if m > 0:
                        EV["tpevac"].wait(t, m - 1)
                    for i2 in range(2):
                        fc = half4 * 2 + i2
                        ins = t.matmul(tp_ps[:, i2 * 128:(i2 + 1) * 128],
                                       xln_sb[:, p, fc * 128:(fc + 1) * 128],
                                       ident[:], is_transpose=True,
                                       start=(i2 == 0), stop=(i2 == 1))
                    EV["tp"].inc(ins)

                @block.vector
                def _(v, half4=half4, m=m):
                    EV["tp"].wait(v, m)
                    EV["tpevac"].inc(v.tensor_copy(
                        dstT[:, half4 * 2:(half4 + 1) * 2, dst_col0:dst_col0 + 128],
                        tp_ps[:]))

        grp_count = [0]
        qkv_count = [0]
        wo_count = [0]
        xl_count = [0]
        it_count = [0]
        h_count = [0]
        y_count = [0]
        rt_count = [0]

        def gslice(t_all, g):
            return t_all[:, g].rearrange("d h e -> d (h e)")

        for hf in range(NHALF):
            @block.sync
            def _(sync, hf=hf):
                EV["xresload"].inc(sync.dma_start(
                    out=acc[:],
                    in_=x_res.rearrange("j p d -> p j d")[:, hf * 8:(hf + 1) * 8, :]))

            @block.vector
            def _(v, hf=hf):
                EV["loads"].wait(v, N_CONST_LOADS - 1)
                EV["xresload"].wait(v, hf)
                for j in range(JT_HALF):
                    ins = v.tensor_tensor(acc[:, j], acc[:, j], bo_rep[:], ALU.add)
                EV["boadd"].inc(ins)

            @block.gpsimd
            def _(g, hf=hf):
                if hf > 0:
                    EV["yacc"].wait(g, hf * 8 - 1)
                for i, W in enumerate(WQKVO):
                    EV["wload"].inc(g.dma_start(out=WA[i][:], in_=W_re(W)))
            wl_attn_last = EV["wload"].n - 1

            for ch in range(2):
                for n in range(4):
                    tile = (hf * 2 + ch) * 4 + n
                    xi = xl_count[0]
                    xl_count[0] += 1

                    @block.sync
                    def _(sync, tile=tile, xi=xi):
                        if xi >= 2:
                            EV["xcw"].wait(sync, xcw_of_xtile[xi - 2])
                        EV["xload"].inc(sync.dma_start(
                            out=x_sb[:, xi % 2],
                            in_=x_own.rearrange("(t p) d -> t p d", p=128)[tile]))

                    @block.vector
                    def _(v, xi=xi):
                        EV["xload"].wait(v, xi)

                    ln_i = emit_ln_tile(x_sb[:, xi % 2])
                    xcw_of_xtile[xi] = ln_i
                    emit_transposes(xlnT, n * 128, ln_i)

                @block.tensor
                def _(t, hf=hf):
                    EV["tpevac"].wait(t, tp_count[0] - 1)
                    EV["bq8"].wait(t, 0)
                    EV["wload"].wait(t, wl_attn_last)
                    if hf > 0 and qkv_count[0] == hf * 96:
                        EV["yadd"].wait(t, y_count[0] - 1)
                    if wo_count[0] > 0:
                        EV["woevac"].wait(t, wo_count[0] - 1)
                    for idx in range(3 * H):
                        pi, h = divmod(idx, H)
                        gi = qkv_count[0] + idx
                        if gi >= 2:
                            EV["qkvevac"].wait(t, gi - 2)
                        ps = PP[gi % 2]
                        for fc in range(8):
                            ins = t.matmul(ps[0:64, :], WA[pi][:, fc, h * 64:(h + 1) * 64],
                                           xlnT[:, fc, :], start=(fc == 0), stop=(fc == 7))
                        EV["qkvfill"].inc(ins)

                @block.scalar
                def _(s, hf=hf):
                    if hf > 0 and qkv_count[0] == hf * 96:
                        EV["yscat"].wait(s, hf * 16 - 1)
                        EV["yacc"].wait(s, hf * 8 - 1)
                    dst_all = (q_all, k_all, v_all)
                    for idx in range(3 * H):
                        pi, h = divmod(idx, H)
                        gi = qkv_count[0] + idx
                        EV["qkvfill"].wait(s, gi)
                        EV["qkvevac"].inc(s.activation(
                            dst_all[pi][:, :, h, :], PP[gi % 2][0:64, :], AF.Identity,
                            bias=bqkv_sb[:, pi, h:h + 1],
                            scale=(0.125 if pi == 0 else 1.0)))
                qkv_count[0] += 3 * H

                g0 = grp_count[0]

                @block.tensor
                def _(t, g0=g0):
                    EV["qkvevac"].wait(t, qkv_count[0] - 1)
                    sq0 = g0 // 4
                    for sl in range(NGC // 4):
                        sq = sq0 + sl
                        pr = sq % 2
                        if sq >= 2:
                            EV["Tadd"].wait(t, sq - 2)
                        for i4 in range(4):
                            ins = t.matmul(SB_ps[pr][:, i4 * 128:(i4 + 1) * 128],
                                           gslice(q_all, sl * 4 + i4),
                                           gslice(k_all, sl * 4 + i4),
                                           start=(i4 == 0), stop=(i4 == 3))
                        EV["sqS"].inc(ins)
                        if sq >= 1:
                            EV["sqcopies"].wait(t, sq - 1)
                        for i4 in range(4):
                            ins = t.matmul(VG_ps[:, i4 * 64:(i4 + 1) * 64],
                                           gslice(v_all, sl * 4 + i4),
                                           ident[0:64, 0:64], is_transpose=True,
                                           start=(i4 == 0), stop=(i4 == 3))
                        EV["sqVG"].inc(ins)
                        EV["Pnm"].wait(t, sq)
                        for i4 in range(4):
                            ins = t.matmul(PT_ps[:, i4 * 128:(i4 + 1) * 128],
                                           Pn_sb[:, pr, i4], ident[:],
                                           is_transpose=True,
                                           start=(i4 == 0), stop=(i4 == 3))
                        EV["sqPT"].inc(ins)
                        EV["sqcopies"].wait(t, sq)
                        for pp2 in range(2):
                            pair = sq * 2 + pp2
                            if pair >= 1:
                                EV["OTevac"].wait(t, pair - 1)
                            for i2 in range(2):
                                i4 = pp2 * 2 + i2
                                ins = t.matmul(OT_ps[:, i2 * 128:(i2 + 1) * 128],
                                               Vg_sb[:, pr, i4], PT_sb[:, pr, i4],
                                               start=(i2 == 0), stop=(i2 == 1))
                            EV["sqOT"].inc(ins)

                @block.vector
                def _(v, g0=g0):
                    sq0 = g0 // 4

                    def emit_pair_evac(pair):
                        gl = (pair - sq0 * 2) * 2
                        EV["sqOT"].wait(v, pair)
                        j = gl // 16
                        rr = (gl % 16) // 2
                        ins = None
                        for sp in range(2):
                            srcap = (OT_ps.ap()
                                     .rearrange("d (g2 h u) -> d g2 h u", g2=2, h=H)
                                     [:, :, :, sp::2]
                                     .rearrange("d g2 h u -> d g2 u h"))
                            dst = (preT2.ap()
                                   .rearrange("p j c (h r) -> p j c h r", h=H)
                                   [sp * 64:(sp + 1) * 64, j, :, :, rr])
                            ins = v.tensor_copy(dst, srcap)
                        EV["OTevac"].inc(ins)

                    for sl in range(NGC // 4):
                        sq = sq0 + sl
                        pr = sq % 2
                        if sl > 0:
                            emit_pair_evac(sq * 2 - 2)
                            emit_pair_evac(sq * 2 - 1)
                        EV["sqS"].wait(v, sq)
                        mrep = mask8.ap()[:, None, :].broadcast_to([128, 4, 128])
                        EV["Tadd"].inc(v.tensor_tensor(
                            T_sb[:, pr], SB_ps[pr].ap().rearrange("p (a b) -> p a b", a=4),
                            mrep, ALU.add))
                        EV["Tadd"].wait(v, sq)
                        EV["Tmax"].inc(v.tensor_reduce(
                            smax[:, pr], T_sb[:, pr], axis=AX.X, op=ALU.max, negate=True))
                        EV["Tmax"].wait(v, sq)
                        EV["Tsub"].inc(v.tensor_tensor(
                            T_sb[:, pr], T_sb[:, pr],
                            smax[:, pr][:, :, None].broadcast_to([128, 4, 128]),
                            ALU.add))
                        EV["P"].wait(v, sq)
                        EV["Psum"].inc(v.tensor_reduce(
                            ssum[:, pr], P_sb[:, pr], axis=AX.X, op=ALU.add))
                        EV["Psum"].wait(v, sq)
                        EV["Prec"].inc(v.reciprocal(srec[:, pr], ssum[:, pr]))
                        EV["Prec"].wait(v, sq)
                        EV["Pnm"].inc(v.tensor_tensor(
                            Pn_sb[:, pr], P_sb[:, pr],
                            srec[:, pr][:, :, None].broadcast_to([128, 4, 128]),
                            ALU.mult))
                        EV["sqPT"].wait(v, sq)
                        v.tensor_copy(PT_sb[:, pr], PT_ps[:])
                        EV["sqcopies"].inc(v.tensor_copy(Vg_sb[:, pr], VG_ps[:]))
                    emit_pair_evac((sq0 + NGC // 4) * 2 - 2)
                    emit_pair_evac((sq0 + NGC // 4) * 2 - 1)

                @block.scalar
                def _(s, g0=g0):
                    sq0 = g0 // 4
                    for sl in range(NGC // 4):
                        sq = sq0 + sl
                        pr = sq % 2
                        EV["Tsub"].wait(s, sq)
                        EV["P"].inc(s.activation(P_sb[:, pr], T_sb[:, pr], AF.Exp))
                grp_count[0] += NGC

                @block.tensor
                def _(t):
                    EV["OTevac"].wait(t, grp_count[0] // 2 - 1)
                    for m in range(8):
                        jj, seg = divmod(m, 2)
                        wi = wo_count[0] + m
                        if wi >= 2:
                            EV["woevac"].wait(t, wi - 2)
                        ps = PP[wi % 2]
                        for cc in range(8):
                            ins = t.matmul(ps[:], preT2[:, jj, cc, :],
                                           WA[3][:, cc, seg * 512:(seg + 1) * 512],
                                           start=(cc == 0), stop=(cc == 7))
                        EV["wofill"].inc(ins)

                @block.vector
                def _(v, ch=ch, hf=hf):
                    EV["boadd"].wait(v, hf)
                    for m in range(8):
                        jj, seg = divmod(m, 2)
                        wi = wo_count[0] + m
                        EV["wofill"].wait(v, wi)
                        jl = ch * 4 + jj
                        sl = slice(seg * 512, (seg + 1) * 512)
                        EV["woevac"].inc(v.tensor_tensor(
                            acc[:, jl, sl], acc[:, jl, sl], PP[wi % 2][:], ALU.add))
                wo_count[0] += 8

            for jl in range(JT_HALF):
                ln_i = emit_ln_tile(acc[:, jl])
                emit_transposes(xfT, jl * 128, ln_i)

                @block.sync
                def _(sync, jl=jl, ln_i=ln_i, hf=hf):
                    if hf > 0 and jl == 0:
                        EV["xgdma"].wait(sync, hf * 16 - 1)
                    EV["xln"].wait(sync, ln_i)
                    EV["xw"].inc(sync.dma_start(
                        out=xln2_hbm[jl * 128:(jl + 1) * 128, :],
                        in_=xln_sb[:, ln_i % 2]))
                xw_of_ln[ln_i] = EV["xw"].n - 1

            for jl in range(JT_HALF):
                ri = rt_count[0]
                rt_count[0] += 1

                @block.tensor
                def _(t, jl=jl, ri=ri):
                    EV["tpevac"].wait(t, tp_count[0] - (JT_HALF - jl - 1) * 4 - 1)
                    if ri > 0:
                        EV["rlogv"].wait(t, ri - 1)
                    EV["wload"].wait(t, N_CONST_WLOADS - 1)
                    for fc in range(8):
                        ins = t.matmul(SB_ps[0][:, 256:256 + NE],
                                       xfT[:, fc, jl * 128:(jl + 1) * 128],
                                       Wr_sb[:, fc, :], start=(fc == 0), stop=(fc == 7))
                    EV["rfill"].inc(ins)

                @block.vector
                def _(v, ri=ri):
                    EV["rfill"].wait(v, ri)
                    EV["rlogv"].inc(v.tensor_tensor(rlog[:], SB_ps[0][:, 256:256 + NE],
                                                    br_rep[:], ALU.add))
                    EV["rlogv"].wait(v, ri)
                    EV["rmax"].inc(v.tensor_reduce(rneg[:], rlog[:], axis=AX.X,
                                                   op=ALU.max, negate=True))

                @block.scalar
                def _(s, ri=ri):
                    EV["rmax"].wait(s, ri)
                    EV["rexp"].inc(s.activation(rmsk[:], rlog[:], AF.Exp, bias=rneg[:]))

                @block.vector
                def _(v, jl=jl, ri=ri):
                    EV["rexp"].wait(v, ri)
                    EV["rsum_"].inc(v.tensor_reduce(rsum[:], rmsk[:], axis=AX.X, op=ALU.add))
                    EV["rsum_"].wait(v, ri)
                    EV["rgate"].inc(v.reciprocal(rgat[:], rsum[:]))
                    EV["rmask"].inc(v.tensor_scalar(out=rmsk[:], in0=rlog[:],
                                                    scalar1=rneg[:], scalar2=0.0,
                                                    op0=ALU.add, op1=ALU.is_ge))
                    EV["rgate"].wait(v, ri)
                    EV["rmask"].wait(v, ri)
                    EV["rsg"].inc(v.tensor_scalar(out=S_gates[:, jl, :], in0=rmsk[:],
                                                  scalar1=rgat[:], scalar2=None,
                                                  op0=ALU.mult))
                    EV["rsg"].wait(v, ri)
                    EV["rsbf"].inc(v.tensor_copy(S_bf[:], S_gates[:, jl, :]))

                @block.tensor
                def _(t, ri=ri):
                    EV["rsbf"].wait(t, ri)
                    if ri > 0:
                        EV["stTevac"].wait(t, ri - 1)
                    EV["stT"].inc(t.matmul(PT_ps[0:NE, 0:128], S_bf[:], ident[:],
                                           is_transpose=True, start=True, stop=True))

                @block.vector
                def _(v, ri=ri, jl=jl):
                    EV["stT"].wait(v, ri)
                    EV["stTevac"].inc(v.tensor_copy(ST_sb[:], PT_ps[0:NE, 0:128]))
                    v.tensor_scalar(out=maskT[:, jl * 128:(jl + 1) * 128],
                                    in0=ST_sb[:], scalar1=0.0, scalar2=None,
                                    op0=ALU.is_gt)
                    v.tensor_copy(gate_tok[:, jl:jl + 1], rgat[:])

                @block.tensor
                def _(t, ri=ri):
                    EV["stTevac"].wait(t, ri)
                    if ri > 0:
                        EV["b2acc"].wait(t, 2 * ri - 1)
                    for seg in range(2):
                        EV["b2f"].inc(t.matmul(PP[seg][:], ST_sb[:],
                                               b2_sb[:, seg * 512:(seg + 1) * 512],
                                               start=True, stop=True))

                @block.vector
                def _(v, jl=jl, ri=ri):
                    for seg in range(2):
                        EV["b2f"].wait(v, 2 * ri + seg)
                        sl = slice(seg * 512, (seg + 1) * 512)
                        EV["b2acc"].inc(v.tensor_tensor(
                            acc[:, jl, sl], acc[:, jl, sl], PP[seg][:], ALU.add))

            @block.vector
            def _(v, hf=hf):
                EV["scand"].inc(v.tensor_tensor_scan(
                    csum[:], maskT[:], zer81.ap().broadcast_to([8, HT]), 0.0,
                    op0=ALU.add, op1=ALU.add))
                EV["scand"].wait(v, hf)
                v.tensor_scalar(out=csum[:], in0=csum[:], scalar1=-1.0,
                                scalar2=None, op0=ALU.add)
                EV["csm1"].inc(v.tensor_tensor(mp_bf[:], csum[:], maskT[:], ALU.mult))

            @block.tensor
            def _(t, hf=hf):
                EV["csm1"].wait(t, hf)
                EV["rlogv"].wait(t, rt_count[0] - 1)
                for ck in range(2):
                    t.matmul(SB_ps[ck][0:1, 0:512], evc_sb[:, 1:2],
                             maskT[:, ck * 512:(ck + 1) * 512],
                             start=True, stop=False)
                    EV["gmm"].inc(t.matmul(SB_ps[ck][0:1, 0:512], evc_sb[:, 0:1],
                                           mp_bf[:, ck * 512:(ck + 1) * 512],
                                           start=False, stop=True))

            @block.vector
            def _(v, hf=hf):
                for ck in range(2):
                    EV["gmm"].wait(v, hf * 2 + ck)
                    EV["gevac"].inc(v.tensor_copy(
                        g_row[0:1, ck * 512:(ck + 1) * 512], SB_ps[ck][0:1, 0:512]))

            @block.gpsimd
            def _(g, hf=hf):
                EV["gevac"].wait(g, hf * 2 + 1)
                g.dma_start(out=g_hbm.rearrange("(o f) o2 -> o (f o2)", o=1),
                            in_=g_row[:]).then_inc(EV["gtr"].h, 16)
                EV["gtr"].n += 1
                EV["gtr"].wait(g, hf * 2)
                EV["gtr"].inc(g.dma_start(
                    out=g_f[:],
                    in_=g_hbm.rearrange("(j p) o -> p (j o)", p=128),
                    allow_slow_non_contiguous=True))

            @block.vector
            def _(v, hf=hf):
                EV["gtr"].wait(v, hf * 2 + 1)
                EV["gidx"].inc(v.tensor_copy(g_i[:], g_f[:]))
                EV["gpe"].inc(v.engine_nop())

            @block.gpsimd
            def _(g, hf=hf):
                EV["fmem"].wait(g, 0)
                if hf > 0:
                    EV["traw"].wait(g, hf * 2 - 1)
                    EV["yrld"].wait(g, hf * 8 - 1)
                EV["tfill"].inc(g.dma_start(
                    out=tok_tbl.rearrange("(p c) o -> p (c o)", c=16), in_=fill_i[:]))
                EV["tfill"].inc(g.dma_start(
                    out=gate_tbl.rearrange("(p c) o -> p (c o)", c=16), in_=fill_f[:]))
                EV["gidx"].wait(g, hf)
                for jl in range(8):
                    EV["scat"].inc(g.indirect_dma_start(
                        out=tok_tbl,
                        out_offset=IndirectOffsetOnAxis(ap=g_i[:, jl:jl + 1], axis=0),
                        in_=iota_sb[:, jl:jl + 1], in_offset=None))
                    EV["scat"].inc(g.indirect_dma_start(
                        out=gate_tbl,
                        out_offset=IndirectOffsetOnAxis(ap=g_i[:, jl:jl + 1], axis=0),
                        in_=gate_tok[:, jl:jl + 1], in_offset=None))

            @block.sync
            def _(sync, hf=hf):
                EV["scat"].wait(sync, (hf + 1) * 16 - 1)
                EV["traw"].inc(sync.dma_start(
                    out=tok_raw[:], in_=tok_tbl.rearrange("(c p) o -> c (p o)", p=128)))
                EV["traw"].inc(sync.dma_start(
                    out=gate_raw[:], in_=gate_tbl.rearrange("(c p) o -> c (p o)", p=128)))

            @block.tensor
            def _(t, hf=hf):
                EV["traw"].wait(t, (hf + 1) * 2 - 1)
                EV["gpe"].wait(t, hf)
                for k, raw in enumerate((tok_raw, gate_raw)):
                    ins = t.matmul(PP[0][:, k * 16:(k + 1) * 16], raw[:],
                                   idf32[:], is_transpose=True,
                                   start=(k == 0), stop=(k == 1))
                EV["ttr"].inc(ins)

            @block.vector
            def _(v, hf=hf):
                EV["ttr"].wait(v, hf)
                if hf > 0:
                    EV["yscat"].wait(v, hf * 16 - 1)
                v.tensor_copy(tok_f[:], PP[0][:, 0:16])
                v.tensor_copy(gate_slots[:], PP[0][:, 16:32])
                EV["tbl"].inc(v.tensor_copy(tok_i[:], tok_f[:]))

            for e in range(NE):
                @block.gpsimd
                def _(g, e=e, hf=hf):
                    EV["tbl"].wait(g, hf)
                    EV["xw"].wait(g, (hf + 1) * 8)
                    prev = (hf, e - 1) if e >= 1 else (hf - 1, NE - 1)
                    if prev in xg_free:
                        EV["tp"].wait(g, xg_free[prev])
                    for tile in range(2):
                        EV["xgdma"].inc(g.indirect_dma_start(
                            out=xg[:, tile], out_offset=None,
                            in_=xln2_hbm,
                            in_offset=IndirectOffsetOnAxis(
                                ap=tok_i[:, e * 2 + tile:e * 2 + tile + 1], axis=0)))

                @block.tensor
                def _(t, e=e, hf=hf):
                    prev_it = hf * ITERS_HALF + e * 4 - 1
                    if e >= 1:
                        EV["hfill"].wait(t, (prev_it + 1) * 8 - 1)
                    for tile in range(2):
                        EV["xgdma"].wait(t, hf * 16 + e * 2 + tile)
                        for half4 in range(4):
                            m = tp_count[0]
                            tp_count[0] += 1
                            if m > 0:
                                EV["tpevac"].wait(t, m - 1)
                            for i2 in range(2):
                                fc = half4 * 2 + i2
                                ins = t.matmul(tp_ps[:, i2 * 128:(i2 + 1) * 128],
                                               xg[:, tile, fc * 128:(fc + 1) * 128],
                                               ident[:], is_transpose=True,
                                               start=(i2 == 0), stop=(i2 == 1))
                            EV["tp"].inc(ins)
                xg_free[(hf, e)] = tp_count[0] - 1

                tpe_hold = tp_count[0]

                @block.vector
                def _(v, e=e, hf=hf, tpe_hold=tpe_hold):
                    for k in range(8):
                        m = tpe_hold - 8 + k
                        tile, half4 = divmod(k, 4)
                        EV["tp"].wait(v, m)
                        EV["tpevac"].inc(v.tensor_copy(
                            xgT[:, half4 * 2:(half4 + 1) * 2,
                                tile * 128:(tile + 1) * 128],
                            tp_ps[:]))

                for qt in range(NQT):
                    it = it_count[0]
                    it_count[0] += 1
                    buf = it % 2
                    W1q = WA[buf * 2]
                    W2q = WA[buf * 2 + 1]
                    w1_ord = N_CONST_WLOADS + (hf + 1) * 4 + it * 2
                    w2_ord = w1_ord + 1

                    @block.gpsimd
                    def _(g, e=e, qt=qt, it=it, hf=hf, W1q=W1q, W2q=W2q):
                        if it == hf * ITERS_HALF:
                            EV["qkvfill"].wait(g, qkv_count[0] - 1)
                            EV["wofill"].wait(g, wo_count[0] - 1)
                        if it >= hf * ITERS_HALF + 2:
                            EV["yfill"].wait(g, (it - 1) * 4 - 1)
                        EV["wload"].inc(g.dma_start(
                            out=W1q[:],
                            in_=W1[e, :, qt * 1024:(qt + 1) * 1024]
                                .rearrange("(fc p) o -> p fc o", p=128)))
                        EV["wload"].inc(g.dma_start(
                            out=W2q[:],
                            in_=W2[e, qt * 1024:(qt + 1) * 1024, :]
                                .rearrange("(fc p) o -> p fc o", p=128)))

                    @block.tensor
                    def _(t, e=e, qt=qt, it=it, hf=hf, W1q=W1q, W2q=W2q,
                          w1_ord=w1_ord, w2_ord=w2_ord, tpe_hold=tpe_hold):
                        EV["wload"].wait(t, w1_ord)
                        if it == hf * ITERS_HALF:
                            EV["gidx"].wait(t, hf)
                        if qt == 0:
                            EV["tpevac"].wait(t, tpe_hold - 1)
                        for m in range(8):
                            hi = h_count[0]
                            h_count[0] += 1
                            if hi >= 2:
                                EV["hevac"].wait(t, hi - 2)
                            hb = SB_ps[hi % 2]
                            for fc in range(8):
                                ins = t.matmul(hb[:, 0:CAP],
                                               W1q[:, fc, m * 128:(m + 1) * 128],
                                               xgT[:, fc, :],
                                               start=(fc == 0), stop=(fc == 7))
                            EV["hfill"].inc(ins)
                        EV["wload"].wait(t, w2_ord)
                        if it == hf * ITERS_HALF:
                            EV["gpe"].wait(t, hf)
                        for tile in range(2):
                            for dseg in range(2):
                                yi = y_count[0]
                                y_count[0] += 1
                                if yi >= 2:
                                    EV["yadd"].wait(t, yi - 2)
                                yb = PP[yi % 2]
                                EV["hevac"].wait(t, it * 8 + 7)
                                for Fc in range(8):
                                    ins = t.matmul(
                                        yb[:],
                                        h_sb[:, Fc, tile * 128:(tile + 1) * 128],
                                        W2q[:, Fc, dseg * 512:(dseg + 1) * 512],
                                        start=(Fc == 0), stop=(Fc == 7))
                                EV["yfill"].inc(ins)

                    @block.scalar
                    def _(s, e=e, qt=qt, it=it, hf=hf):
                        if it > 0:
                            EV["yfill"].wait(s, it * 4 - 1)
                        for m in range(8):
                            hi = it * 8 + m
                            EV["hfill"].wait(s, hi)
                            EV["hevac"].inc(s.activation(
                                h_sb[:, m, :], SB_ps[hi % 2][:, 0:CAP], AF.Gelu,
                                bias=b1_sb[:, e, qt * 8 + m:qt * 8 + m + 1]))

                    @block.vector
                    def _(v, e=e, qt=qt, it=it, hf=hf):
                        for tile in range(2):
                            for dseg in range(2):
                                yi = it * 4 + tile * 2 + dseg
                                if qt == 0 and tile == 0 and dseg == 0                                         and (hf * 8 + e) >= 1:
                                    EV["yscat"].wait(v, (hf * 8 + e) * 2 - 1)
                                EV["yfill"].wait(v, yi)
                                sl = slice(dseg * 512, (dseg + 1) * 512)
                                if qt == 0:
                                    ins = v.tensor_copy(y_stage[:, tile, sl],
                                                        PP[yi % 2][:])
                                else:
                                    ins = v.tensor_tensor(y_stage[:, tile, sl],
                                                          y_stage[:, tile, sl],
                                                          PP[yi % 2][:], ALU.add)
                                EV["yadd"].inc(ins)

                @block.vector
                def _(v, e=e, hf=hf):
                    for tile in range(2):
                        EV["ysc"].inc(v.tensor_scalar_mul(
                            y_stage[:, tile], y_stage[:, tile],
                            gate_slots[:, e * 2 + tile:e * 2 + tile + 1]))

                @block.gpsimd
                def _(g, e=e, hf=hf):
                    EV["ysc"].wait(g, (hf * 8 + e) * 2 + 1)
                    for tile in range(2):
                        EV["yscat"].inc(g.indirect_dma_start(
                            out=y_hbm,
                            out_offset=IndirectOffsetOnAxis(
                                ap=tok_i[:, e * 2 + tile:e * 2 + tile + 1], axis=0),
                            in_=y_stage[:, tile], in_offset=None))

            for jl in range(JT_HALF):
                @block.sync
                def _(sync, jl=jl, hf=hf):
                    if jl == 0:
                        EV["yscat"].wait(sync, (hf + 1) * 16 - 1)
                    if hf * 8 + jl >= 2:
                        EV["yacc"].wait(sync, hf * 8 + jl - 2)
                    EV["yrld"].inc(sync.dma_start(
                        out=yld[:, jl % 2],
                        in_=y_hbm[jl * 128:(jl + 1) * 128, :]))

                @block.vector
                def _(v, jl=jl, hf=hf):
                    EV["yrld"].wait(v, hf * 8 + jl)
                    EV["yacc"].inc(v.tensor_tensor(
                        acc[:, jl], acc[:, jl], yld[:, jl % 2], ALU.add))

            @block.sync
            def _(sync, hf=hf):
                EV["yacc"].wait(sync, (hf + 1) * 8 - 1)
                EV["store"].inc(sync.dma_start(
                    out=out.rearrange("j p d -> p j d")[:, hf * 8:(hf + 1) * 8, :],
                    in_=acc[:]))

        @block.sync
        def _(sync):
            for name in DMA_EV:
                if EV[name].n:
                    sync.wait_ge(EV[name].h, EV[name].step * EV[name].n)
    return nc


_NC_CACHE = {}


def _get_nc():
    if "nc" not in _NC_CACHE:
        _NC_CACHE["nc"] = build()
    return _NC_CACHE["nc"]


def kernel(L0_attn_hidden_state, L0_attn_Wq, L0_attn_bq, L0_attn_Wk, L0_attn_bk,
           L0_attn_Wv, L0_attn_bv, L0_attn_Wo, L0_attn_bo,
           L0_ffn_W1, L0_ffn_b1, L0_ffn_W2, L0_ffn_b2, L0_ffn_Wr, L0_ffn_br):
    x = np.ascontiguousarray(np.asarray(L0_attn_hidden_state, dtype=np.float32))
    B, S, _ = x.shape
    nc = _get_nc()

    ident = np.eye(128, dtype=bfd)
    ii, jj = np.meshgrid(np.arange(128), np.arange(128), indexing="ij")
    mask8 = np.where(ii % 8 == jj % 8, 0.0, -1e30).astype(np.float32)
    idf32 = np.eye(16, dtype=np.float32)
    evc = np.stack([np.ones(8), np.arange(8) * 256.0], axis=1).astype(bfd)
    iota = (np.arange(8)[None, :] * 128 + np.arange(128)[:, None]).astype(np.float32)

    common = dict(
        Wq=np.ascontiguousarray(L0_attn_Wq, np.float32),
        Wk=np.ascontiguousarray(L0_attn_Wk, np.float32),
        Wv=np.ascontiguousarray(L0_attn_Wv, np.float32),
        Wo=np.ascontiguousarray(L0_attn_Wo, np.float32),
        bq=np.ascontiguousarray(L0_attn_bq, np.float32),
        bk=np.ascontiguousarray(L0_attn_bk, np.float32),
        bv=np.ascontiguousarray(L0_attn_bv, np.float32),
        bo=np.ascontiguousarray(L0_attn_bo, np.float32),
        W1=np.ascontiguousarray(L0_ffn_W1, np.float32),
        b1=np.ascontiguousarray(L0_ffn_b1, np.float32),
        W2=np.ascontiguousarray(L0_ffn_W2, np.float32),
        b2=np.ascontiguousarray(L0_ffn_b2, np.float32),
        Wr=np.ascontiguousarray(L0_ffn_Wr, np.float32),
        br=np.ascontiguousarray(L0_ffn_br, np.float32),
        ident_hbm=ident, mask_hbm=mask8,
        idf32_hbm=idf32, evc_hbm=evc, iota_hbm=iota,
    )

    in_maps = []
    for c in range(8):
        b, c2 = divmod(c, 2)
        x_own = np.ascontiguousarray(x[b, 2048 * c2: 2048 * (c2 + 1)])
        xr = x[b].reshape(16, 2, 16, 8, DM)[:, c2]
        x_res = np.ascontiguousarray(
            xr.transpose(1, 0, 2, 3).reshape(16, 128, DM))
        im = dict(common)
        im["x_own"] = x_own
        im["x_res"] = x_res
        in_maps.append(im)

    res = run_bass_kernel_spmd(nc, in_maps, core_ids=list(range(8)))

    out = np.empty((B, S, DM), dtype=np.float32)
    for c in range(8):
        b, c2 = divmod(c, 2)
        oc = res.results[c]["out"]
        ov = oc.reshape(16, 16, 8, DM).transpose(1, 0, 2, 3)
        out[b].reshape(16, 2, 16, 8, DM)[:, c2] = ov
    return out


# version 07
# speedup vs baseline: 1.8728x; 1.2543x over previous
import numpy as np
import ml_dtypes
import concourse.bass as bass
import concourse.mybir as mybir
from concourse.bass_utils import run_bass_kernel_spmd
from concourse.bass import IndirectOffsetOnAxis

F32 = mybir.dt.float32
F16 = mybir.dt.float16
I32 = mybir.dt.int32
BF16 = mybir.dt.bfloat16
AF = mybir.ActivationFunctionType
ALU = mybir.AluOpType
AX = mybir.AxisListType
bfd = ml_dtypes.bfloat16

H = 16
D = 64
DM = 1024
FF = 4096
NE = 8
CT = 512
NGC = CT // 8
NQT = 4
NHALF = 2
JT_HALF = 8
HT = 1024
ITERS_HALF = NE * NQT
CAP = 512


class Evt:
    def __init__(self, stack, nc, name, step=1):
        self.h = stack.enter_context(nc.semaphore(name))
        self.step = step
        self.n = 0

    def inc(self, instr):
        instr.then_inc(self.h, self.step)
        self.n += 1
        return instr

    def wait(self, eng, i):
        assert 0 <= i < self.n or True
        eng.wait_ge(self.h, self.step * (i + 1))


def build():
    nc = bass.Bass("TRN2", target_bir_lowering=False,
                   detect_race_conditions=False)
    x_own = nc.dram_tensor("x_own", [2048, DM], F32, kind="ExternalInput").ap()
    x_res = nc.dram_tensor("x_res", [16, 128, DM], F32, kind="ExternalInput").ap()
    Wq = nc.dram_tensor("Wq", [DM, DM], F32, kind="ExternalInput").ap()
    Wk = nc.dram_tensor("Wk", [DM, DM], F32, kind="ExternalInput").ap()
    Wv = nc.dram_tensor("Wv", [DM, DM], F32, kind="ExternalInput").ap()
    Wo = nc.dram_tensor("Wo", [DM, DM], F32, kind="ExternalInput").ap()
    bq = nc.dram_tensor("bq", [DM], F32, kind="ExternalInput").ap()
    bk = nc.dram_tensor("bk", [DM], F32, kind="ExternalInput").ap()
    bv = nc.dram_tensor("bv", [DM], F32, kind="ExternalInput").ap()
    bo = nc.dram_tensor("bo", [DM], F32, kind="ExternalInput").ap()
    W1 = nc.dram_tensor("W1", [NE, DM, FF], F32, kind="ExternalInput").ap()
    b1 = nc.dram_tensor("b1", [NE, FF], F32, kind="ExternalInput").ap()
    W2 = nc.dram_tensor("W2", [NE, FF, DM], F32, kind="ExternalInput").ap()
    b2 = nc.dram_tensor("b2", [NE, DM], F32, kind="ExternalInput").ap()
    Wr = nc.dram_tensor("Wr", [DM, NE], F32, kind="ExternalInput").ap()
    br = nc.dram_tensor("br", [NE], F32, kind="ExternalInput").ap()
    ident_hbm = nc.dram_tensor("ident_hbm", [128, 128], BF16, kind="ExternalInput").ap()
    idf32_hbm = nc.dram_tensor("idf32_hbm", [32, 32], F32, kind="ExternalInput").ap()
    mask_hbm = nc.dram_tensor("mask_hbm", [128, 128], F32, kind="ExternalInput").ap()
    evc_hbm = nc.dram_tensor("evc_hbm", [8, 2], F16, kind="ExternalInput").ap()
    iota_hbm = nc.dram_tensor("iota_hbm", [128, 16], F32, kind="ExternalInput").ap()
    out = nc.dram_tensor("out", [16, 128, DM], F32, kind="ExternalOutput").ap()
    NTOK = 2 * HT
    xln2_hbm = nc.dram_tensor("xln2_hbm", [NTOK + 1, DM], BF16, kind="Internal").ap()
    tok_tbl = nc.dram_tensor("tok_tbl", [NE * CAP, 1], F32, kind="Internal").ap()
    gate_tbl = nc.dram_tensor("gate_tbl", [NE * CAP, 1], F32, kind="Internal").ap()
    y_hbm = nc.dram_tensor("y_hbm", [NTOK + 1, DM], F32, kind="Internal").ap()
    g_hbm = nc.dram_tensor("g_hbm", [NTOK, 1], F32, kind="Internal").ap()
    maskT_hbm = nc.dram_tensor("maskT_hbm", [8, HT], F16, kind="Internal").ap()

    from contextlib import ExitStack
    st = ExitStack()
    with st:
        sbt = lambda *a: st.enter_context(nc.sbuf_tensor(*a))
        pst = lambda *a: st.enter_context(nc.psum_tensor(*a))

        ident = sbt("ident", [128, 128], BF16)
        mask8 = sbt("mask8", [128, 128], F32)
        bqkv_sb = sbt("bqkv_sb", [64, 3, H], F32)
        bo_rep = sbt("bo_rep", [128, DM], F32)
        br_rep = sbt("br_rep", [128, NE], F32)
        Wr_sb = sbt("Wr_sb", [128, 8, NE], BF16)
        b1_sb = sbt("b1_sb", [128, NE, 32], F32)
        b2_sb = sbt("b2_sb", [NE, DM], BF16)
        eps_t = sbt("eps_t", [128, 1], F32)
        ln_s = sbt("ln_s", [128, 2, 4], F32)
        sqd = sbt("sqd", [128, DM], F32)
        acc = sbt("acc", [128, JT_HALF, DM], F32)
        WA = [sbt(f"wa{i}", [128, 8, DM], BF16) for i in range(4)]
        x_sb = sbt("x_sb", [128, 2, DM], F32)
        xc_sb = sbt("xc_sb", [128, 2, DM], F32)
        xln_sb = sbt("xln_sb", [128, 2, DM], BF16)
        xlnT = sbt("xlnT", [128, 8, CT], BF16)
        preT2 = sbt("preT2", [128, 4, 8, 128], BF16)
        T_sb = sbt("T_sb", [128, 2, 4, 128], F32)
        P_sb = T_sb
        Pn_sb = sbt("Pn_sb", [128, 2, 4, 128], BF16)
        PT_sb = sbt("PT_sb", [128, 2, 4, 128], BF16)
        Vg_sb = sbt("Vg_sb", [128, 2, 4, D], BF16)
        smax = sbt("smax", [128, 2, 4], F32)
        ssum = sbt("ssum", [128, 2, 4], F32)
        srec = sbt("srec", [128, 2, 4], F32)
        S_gates = sbt("S_gates", [128, JT_HALF, NE], F32)
        rlog = sbt("rlog", [128, NE], F32)
        rmsk = sbt("rmsk", [128, NE], F32)
        rneg = sbt("rneg", [128, 1], F32)
        rsum = sbt("rsum", [128, 1], F32)
        rgat = sbt("rgat", [128, 1], F32)
        S_bf = sbt("S_bf", [128, NE], BF16)
        ST_sb = sbt("ST_sb", [NE, 128], BF16)
        idf32 = sbt("idf32", [32, 32], F32)
        evc_sb = sbt("evc_sb", [8, 2], F16)
        iota_sb = sbt("iota_sb", [128, 16], F32)
        zer81 = sbt("zer81", [8, 1], F32)
        g_f = sbt("g_f", [128, 16], F32)
        g_stage = sbt("g_stage", [1, 512], F32)
        g_i = sbt("g_i", [128, 16], I32)
        gate_tok = sbt("gate_tok", [128, 16], F32)
        tok_i = sbt("tok_i", [128, 32], I32)
        tok_f = sbt("tok_f", [128, 32], F32)
        gate_slots = sbt("gate_slots", [128, 32], F32)
        fill_i = sbt("fill_i", [128, 32], F32)
        fill_f = sbt("fill_f", [128, 32], F32)
        tok_raw = sbt("tok_raw", [32, 128], F32)
        gate_raw = sbt("gate_raw", [32, 128], F32)
        qkv_region = sbt("qkv_region", [128, 48 * 1024 // 4], F32)
        base = nc.lookup_mloc(qkv_region).addr
        q_all = nc.alloc_sbuf_tensor_at("q_all", [64, NGC, H, 8], BF16, offset=base)
        k_all = nc.alloc_sbuf_tensor_at("k_all", [64, NGC, H, 8], BF16, offset=base + 16 * 1024)
        v_all = nc.alloc_sbuf_tensor_at("v_all", [64, NGC, H, 8], BF16, offset=base + 32 * 1024)
        xfT = nc.alloc_sbuf_tensor_at("xfT", [128, 8, 1024], BF16, offset=base)
        h_sb = nc.alloc_sbuf_tensor_at("h_sb", [128, 8, CAP], BF16, offset=base)
        xg = nc.alloc_sbuf_tensor_at("xg", [128, 4, DM], BF16, offset=base + 8 * 1024)
        xgT = nc.alloc_sbuf_tensor_at("xgT", [128, 8, CAP], BF16, offset=base + 16 * 1024)
        y_stage = nc.alloc_sbuf_tensor_at("y_stage", [128, 4, DM], F32, offset=base + 24 * 1024)
        yld = nc.alloc_sbuf_tensor_at("yld", [128, 2, DM], F32, offset=base + 40 * 1024)
        maskT = nc.alloc_sbuf_tensor_at("maskT", [8, 2 * HT], F16,
                                        offset=nc.lookup_mloc(x_sb).addr)
        csum = nc.alloc_sbuf_tensor_at("csum", [8, 2 * HT], F16,
                                       offset=nc.lookup_mloc(x_sb).addr + 4096)
        mp_f16 = nc.alloc_sbuf_tensor_at("mp_f16", [8, 2 * HT], F16,
                                         offset=nc.lookup_mloc(xc_sb).addr)
        zrow = nc.alloc_sbuf_tensor_at("zrow", [1, DM], BF16, offset=base + 40 * 1024)

        PP = [pst("pp0", [128, 512], F32), pst("pp1", [128, 512], F32)]
        SB_ps = [pst("S0", [128, 512], F32), pst("S1", [128, 512], F32)]
        tp_ps = pst("tp", [128, 256], BF16)
        PT_ps = pst("PT0", [128, 512], BF16)
        VG_ps = pst("VG0", [128, 4 * D], BF16)
        OT_ps = pst("OT0", [64, 256], F32)

        ev_names = [
            "loads", "wload", "store", "init", "bq8",
            "xload", "xresload", "negsum", "negmean", "ssq", "std", "rstd",
            "xln", "xcw", "tp", "tpevac", "qkvfill", "qkvevac",
            "sqS", "sqVG", "sqPT", "sqOT", "Tadd", "Tmax", "Tsub", "P",
            "Psum", "Prec", "Pnm", "sqcopies", "OTevac", "wofill", "woevac", "boadd",
            "rfill", "rlogv", "rmax", "rexp", "rsum_", "rgate", "rmask",
            "rsg", "rsbf", "stT", "stTevac", "b2f", "b2acc",
            "hfill", "hevac", "yfill", "yadd", "ysc", "yacc",
            "xw", "scand", "csm1", "gmm", "gevac", "gtr", "gsp", "gpe", "gidx",
            "tfill", "scat", "tbl", "traw", "ttr", "xgdma", "xgtr", "xgevac",
            "yscat", "yrld", "mld", "fmem", "mTw", "mst", "mldT",
        ]
        DMA_EV = ("loads", "wload", "store", "xload", "xresload",
                  "xw", "tfill", "scat", "traw", "xgdma", "yscat", "yrld", "gtr", "mld",
                  "mst", "mldT", "gsp")
        EV = {n: Evt(st, nc, n, 16 if n in DMA_EV else 1) for n in ev_names}

        block = st.enter_context(nc.Block())

        @block.sync
        def _(sync):
            EV["loads"].inc(sync.dma_start(out=ident[:], in_=ident_hbm))
            EV["loads"].inc(sync.dma_start(out=mask8[:], in_=mask_hbm))
            EV["loads"].inc(sync.dma_start(out=bo_rep[:], in_=bo[None, :].broadcast_to([128, DM])))
            EV["loads"].inc(sync.dma_start(out=br_rep[:], in_=br[None, :].broadcast_to([128, NE])))
            for i, bsrc in enumerate((bq, bk, bv)):
                EV["loads"].inc(sync.dma_start(
                    out=bqkv_sb[:, i, :], in_=bsrc.rearrange("(h d) -> d h", d=64),
                    allow_slow_non_contiguous=True))
            EV["loads"].inc(sync.dma_start(
                out=b1_sb[:], in_=b1.rearrange("e (mq p) -> p e mq", p=128),
                allow_slow_non_contiguous=True))
            EV["loads"].inc(sync.dma_start(out=idf32[:], in_=idf32_hbm))
            EV["loads"].inc(sync.dma_start(out=evc_sb[:], in_=evc_hbm))
            EV["loads"].inc(sync.dma_start(out=iota_sb[:], in_=iota_hbm))
        N_CONST_LOADS = EV["loads"].n

        @block.gpsimd
        def _(g):
            EV["wload"].inc(g.dma_start(out=Wr_sb[:], in_=Wr.rearrange("(fc p) e -> p fc e", p=128)))
            EV["wload"].inc(g.dma_start(out=b2_sb[:], in_=b2))
        N_CONST_WLOADS = EV["wload"].n

        @block.vector
        def _(v):
            v.memset(T_sb[:], 0.0)
            v.memset(zer81[:], 0.0)
            v.memset(zrow[:], 0.0)
            v.memset(fill_i[:], float(2 * HT))
            EV["fmem"].inc(v.memset(fill_f[:], 0.0))
            EV["init"].inc(v.memset(eps_t[:], 1e-5))

        @block.scalar
        def _(s):
            EV["loads"].wait(s, N_CONST_LOADS - 1)
            EV["init"].wait(s, 0)
            EV["bq8"].inc(s.activation(bqkv_sb[:, 0, :], bqkv_sb[:, 0, :],
                                       AF.Identity, scale=0.125))

        @block.sync
        def _(sync):
            EV["fmem"].wait(sync, 0)
            EV["xw"].inc(sync.dma_start(out=xln2_hbm[2 * HT:2 * HT + 1, :], in_=zrow[:]))

        WQKVO = (Wq, Wk, Wv, Wo)
        W_re = lambda W: W.rearrange("(fc p) o -> p fc o", p=128)

        ln_count = [0]
        tp_count = [0]
        xcw_of_xtile = {}
        xw_of_ln = {}
        xg_free = {}

        def emit_ln_tile(src_ap):
            i = ln_count[0]
            ln_count[0] += 1
            p = i % 2

            @block.vector
            def _(v):
                EV["negsum"].inc(v.tensor_reduce(
                    ln_s[:, p, 0:1], src_ap, axis=AX.X, op=ALU.add, negate=True))
                EV["negsum"].wait(v, i)
                EV["negmean"].inc(v.tensor_scalar_mul(
                    ln_s[:, p, 1:2], ln_s[:, p, 0:1], 1.0 / DM))
                EV["negmean"].wait(v, i)
                EV["xcw"].inc(v.tensor_scalar(
                    out=xc_sb[:, p], in0=src_ap, scalar1=ln_s[:, p, 1:2],
                    scalar2=None, op0=ALU.add))

            @block.scalar
            def _(s):
                EV["xcw"].wait(s, i)
                EV["ssq"].inc(s.activation(sqd[:], xc_sb[:, p], AF.Square,
                                           accum_out=ln_s[:, p, 2:3]))
                EV["ssq"].wait(s, i)
                EV["std"].inc(s.activation(ln_s[:, p, 3:4], ln_s[:, p, 2:3],
                                           AF.Sqrt, scale=1.0 / DM, bias=eps_t[:]))

            @block.vector
            def _(v):
                EV["std"].wait(v, i)
                EV["rstd"].inc(v.reciprocal(ln_s[:, p, 1:2], ln_s[:, p, 3:4]))
                EV["rstd"].wait(v, i)
                if tp_count[0] >= 8:
                    EV["tp"].wait(v, tp_count[0] - 5)
                if (i - 2) in xw_of_ln:
                    EV["xw"].wait(v, xw_of_ln[i - 2])
                EV["xln"].inc(v.tensor_scalar_mul(
                    xln_sb[:, p], xc_sb[:, p], ln_s[:, p, 1:2]))
            return i

        def emit_transposes(dstT, dst_col0, xln_i):
            p = xln_i % 2
            for half4 in range(4):
                m = tp_count[0]
                tp_count[0] += 1

                @block.tensor
                def _(t, half4=half4, m=m):
                    if half4 == 0:
                        EV["xln"].wait(t, xln_i)
                    if m > 0:
                        EV["tpevac"].wait(t, m - 1)
                    for i2 in range(2):
                        fc = half4 * 2 + i2
                        ins = t.matmul(tp_ps[:, i2 * 128:(i2 + 1) * 128],
                                       xln_sb[:, p, fc * 128:(fc + 1) * 128],
                                       ident[:], is_transpose=True,
                                       start=(i2 == 0), stop=(i2 == 1))
                    EV["tp"].inc(ins)

                @block.vector
                def _(v, half4=half4, m=m):
                    EV["tp"].wait(v, m)
                    EV["tpevac"].inc(v.tensor_copy(
                        dstT[:, half4 * 2:(half4 + 1) * 2, dst_col0:dst_col0 + 128],
                        tp_ps[:]))

        grp_count = [0]
        qkv_count = [0]
        wo_count = [0]
        xl_count = [0]
        it_count = [0]
        h_count = [0]
        y_count = [0]
        rt_count = [0]

        def gslice(t_all, g):
            return t_all[:, g].rearrange("d h e -> d (h e)")

        for hf in range(NHALF):
            @block.sync
            def _(sync, hf=hf):
                EV["xresload"].inc(sync.dma_start(
                    out=acc[:],
                    in_=x_res.rearrange("j p d -> p j d")[:, hf * 8:(hf + 1) * 8, :]))

            @block.vector
            def _(v, hf=hf):
                EV["loads"].wait(v, N_CONST_LOADS - 1)
                EV["xresload"].wait(v, hf)
                for j in range(JT_HALF):
                    ins = v.tensor_tensor(acc[:, j], acc[:, j], bo_rep[:], ALU.add)
                EV["boadd"].inc(ins)

            @block.gpsimd
            def _(g, hf=hf):
                if hf > 0:
                    EV["qkvfill"].wait(g, qkv_count[0] - 1)
                    EV["wofill"].wait(g, wo_count[0] - 1)
                for i, W in enumerate(WQKVO):
                    EV["wload"].inc(g.dma_start(out=WA[i][:], in_=W_re(W)))
            wl_attn_last = EV["wload"].n - 1

            for ch in range(2):
                for n in range(4):
                    tile = (hf * 2 + ch) * 4 + n
                    xi = xl_count[0]
                    xl_count[0] += 1

                    @block.sync
                    def _(sync, tile=tile, xi=xi):
                        if xi == 8:
                            EV["mst"].wait(sync, 0)
                        if xi >= 2:
                            EV["xcw"].wait(sync, xcw_of_xtile[xi - 2])
                        EV["xload"].inc(sync.dma_start(
                            out=x_sb[:, xi % 2],
                            in_=x_own.rearrange("(t p) d -> t p d", p=128)[tile]))

                    @block.vector
                    def _(v, xi=xi):
                        EV["xload"].wait(v, xi)

                    ln_i = emit_ln_tile(x_sb[:, xi % 2])
                    xcw_of_xtile[xi] = ln_i
                    emit_transposes(xlnT, n * 128, ln_i)

                @block.tensor
                def _(t, hf=hf):
                    EV["tpevac"].wait(t, tp_count[0] - 1)
                    EV["bq8"].wait(t, 0)
                    EV["wload"].wait(t, wl_attn_last)
                    if hf > 0 and qkv_count[0] == hf * 96:
                        EV["b2acc"].wait(t, 2 * rt_count[0] - 1)
                    if wo_count[0] > 0:
                        EV["woevac"].wait(t, wo_count[0] - 1)
                    for idx in range(3 * H):
                        pi, h = divmod(idx, H)
                        gi = qkv_count[0] + idx
                        if gi >= 2:
                            EV["qkvevac"].wait(t, gi - 2)
                        ps = PP[gi % 2]
                        for fc in range(8):
                            ins = t.matmul(ps[0:64, :], WA[pi][:, fc, h * 64:(h + 1) * 64],
                                           xlnT[:, fc, :], start=(fc == 0), stop=(fc == 7))
                        EV["qkvfill"].inc(ins)

                @block.scalar
                def _(s, hf=hf):
                    dst_all = (q_all, k_all, v_all)
                    for idx in range(3 * H):
                        pi, h = divmod(idx, H)
                        gi = qkv_count[0] + idx
                        EV["qkvfill"].wait(s, gi)
                        EV["qkvevac"].inc(s.activation(
                            dst_all[pi][:, :, h, :], PP[gi % 2][0:64, :], AF.Identity,
                            bias=bqkv_sb[:, pi, h:h + 1],
                            scale=(0.125 if pi == 0 else 1.0)))
                qkv_count[0] += 3 * H

                g0 = grp_count[0]

                @block.tensor
                def _(t, g0=g0):
                    EV["qkvevac"].wait(t, qkv_count[0] - 1)
                    sq0 = g0 // 4
                    for sl in range(NGC // 4):
                        sq = sq0 + sl
                        pr = sq % 2
                        if sq >= 2:
                            EV["Tadd"].wait(t, sq - 2)
                        for i4 in range(4):
                            ins = t.matmul(SB_ps[pr][:, i4 * 128:(i4 + 1) * 128],
                                           gslice(q_all, sl * 4 + i4),
                                           gslice(k_all, sl * 4 + i4),
                                           start=(i4 == 0), stop=(i4 == 3))
                        EV["sqS"].inc(ins)
                        if sq >= 1:
                            EV["sqcopies"].wait(t, sq - 1)
                        for i4 in range(4):
                            ins = t.matmul(VG_ps[:, i4 * 64:(i4 + 1) * 64],
                                           gslice(v_all, sl * 4 + i4),
                                           ident[0:64, 0:64], is_transpose=True,
                                           start=(i4 == 0), stop=(i4 == 3))
                        EV["sqVG"].inc(ins)
                        EV["Pnm"].wait(t, sq)
                        for i4 in range(4):
                            ins = t.matmul(PT_ps[:, i4 * 128:(i4 + 1) * 128],
                                           Pn_sb[:, pr, i4], ident[:],
                                           is_transpose=True,
                                           start=(i4 == 0), stop=(i4 == 3))
                        EV["sqPT"].inc(ins)
                        EV["sqcopies"].wait(t, sq)
                        for pp2 in range(2):
                            pair = sq * 2 + pp2
                            if pair >= 1:
                                EV["OTevac"].wait(t, pair - 1)
                            for i2 in range(2):
                                i4 = pp2 * 2 + i2
                                ins = t.matmul(OT_ps[:, i2 * 128:(i2 + 1) * 128],
                                               Vg_sb[:, pr, i4], PT_sb[:, pr, i4],
                                               start=(i2 == 0), stop=(i2 == 1))
                            EV["sqOT"].inc(ins)

                @block.vector
                def _(v, g0=g0):
                    sq0 = g0 // 4

                    def emit_pair_evac(pair):
                        gl = (pair - sq0 * 2) * 2
                        EV["sqOT"].wait(v, pair)
                        j = gl // 16
                        rr = (gl % 16) // 2
                        ins = None
                        for sp in range(2):
                            srcap = (OT_ps.ap()
                                     .rearrange("d (g2 h u) -> d g2 h u", g2=2, h=H)
                                     [:, :, :, sp::2]
                                     .rearrange("d g2 h u -> d g2 u h"))
                            dst = (preT2.ap()
                                   .rearrange("p j c (h r) -> p j c h r", h=H)
                                   [sp * 64:(sp + 1) * 64, j, :, :, rr])
                            ins = v.tensor_copy(dst, srcap)
                        EV["OTevac"].inc(ins)

                    for sl in range(NGC // 4):
                        sq = sq0 + sl
                        pr = sq % 2
                        if sl > 0:
                            emit_pair_evac(sq * 2 - 2)
                            emit_pair_evac(sq * 2 - 1)
                        EV["sqS"].wait(v, sq)
                        mrep = mask8.ap()[:, None, :].broadcast_to([128, 4, 128])
                        EV["Tadd"].inc(v.tensor_tensor(
                            T_sb[:, pr], SB_ps[pr].ap().rearrange("p (a b) -> p a b", a=4),
                            mrep, ALU.add))
                        EV["Tadd"].wait(v, sq)
                        EV["Tmax"].inc(v.tensor_reduce(
                            smax[:, pr], T_sb[:, pr], axis=AX.X, op=ALU.max, negate=True))
                        EV["Tmax"].wait(v, sq)
                        EV["Tsub"].inc(v.tensor_tensor(
                            T_sb[:, pr], T_sb[:, pr],
                            smax[:, pr][:, :, None].broadcast_to([128, 4, 128]),
                            ALU.add))
                        EV["P"].wait(v, sq)
                        EV["Psum"].inc(v.tensor_reduce(
                            ssum[:, pr], P_sb[:, pr], axis=AX.X, op=ALU.add))
                        EV["Psum"].wait(v, sq)
                        EV["Prec"].inc(v.reciprocal(srec[:, pr], ssum[:, pr]))
                        EV["Prec"].wait(v, sq)
                        EV["Pnm"].inc(v.tensor_tensor(
                            Pn_sb[:, pr], P_sb[:, pr],
                            srec[:, pr][:, :, None].broadcast_to([128, 4, 128]),
                            ALU.mult))
                        EV["sqPT"].wait(v, sq)
                        v.tensor_copy(PT_sb[:, pr], PT_ps[:])
                        EV["sqcopies"].inc(v.tensor_copy(Vg_sb[:, pr], VG_ps[:]))
                    emit_pair_evac((sq0 + NGC // 4) * 2 - 2)
                    emit_pair_evac((sq0 + NGC // 4) * 2 - 1)

                @block.scalar
                def _(s, g0=g0):
                    sq0 = g0 // 4
                    for sl in range(NGC // 4):
                        sq = sq0 + sl
                        pr = sq % 2
                        EV["Tsub"].wait(s, sq)
                        EV["P"].inc(s.activation(P_sb[:, pr], T_sb[:, pr], AF.Exp))
                grp_count[0] += NGC

                @block.tensor
                def _(t):
                    EV["OTevac"].wait(t, grp_count[0] // 2 - 1)
                    for m in range(8):
                        jj, seg = divmod(m, 2)
                        wi = wo_count[0] + m
                        if wi >= 2:
                            EV["woevac"].wait(t, wi - 2)
                        ps = PP[wi % 2]
                        for cc in range(8):
                            ins = t.matmul(ps[:], preT2[:, jj, cc, :],
                                           WA[3][:, cc, seg * 512:(seg + 1) * 512],
                                           start=(cc == 0), stop=(cc == 7))
                        EV["wofill"].inc(ins)

                @block.vector
                def _(v, ch=ch, hf=hf):
                    EV["boadd"].wait(v, hf)
                    for m in range(8):
                        jj, seg = divmod(m, 2)
                        wi = wo_count[0] + m
                        EV["wofill"].wait(v, wi)
                        jl = ch * 4 + jj
                        sl = slice(seg * 512, (seg + 1) * 512)
                        EV["woevac"].inc(v.tensor_tensor(
                            acc[:, jl, sl], acc[:, jl, sl], PP[wi % 2][:], ALU.add))
                wo_count[0] += 8

            for jl in range(JT_HALF):
                ln_i = emit_ln_tile(acc[:, jl])
                emit_transposes(xfT, jl * 128, ln_i)

                @block.sync
                def _(sync, jl=jl, ln_i=ln_i, hf=hf):
                    EV["xln"].wait(sync, ln_i)
                    row0 = hf * HT + jl * 128
                    EV["xw"].inc(sync.dma_start(
                        out=xln2_hbm[row0:row0 + 128, :],
                        in_=xln_sb[:, ln_i % 2]))
                xw_of_ln[ln_i] = EV["xw"].n - 1

            for jl in range(JT_HALF):
                ri = rt_count[0]
                rt_count[0] += 1

                @block.tensor
                def _(t, jl=jl, ri=ri):
                    EV["tpevac"].wait(t, tp_count[0] - (JT_HALF - jl - 1) * 4 - 1)
                    if ri > 0:
                        EV["rlogv"].wait(t, ri - 1)
                    EV["wload"].wait(t, N_CONST_WLOADS - 1)
                    for fc in range(8):
                        ins = t.matmul(SB_ps[0][:, 256:256 + NE],
                                       xfT[:, fc, jl * 128:(jl + 1) * 128],
                                       Wr_sb[:, fc, :], start=(fc == 0), stop=(fc == 7))
                    EV["rfill"].inc(ins)

                @block.vector
                def _(v, ri=ri):
                    EV["rfill"].wait(v, ri)
                    EV["rlogv"].inc(v.tensor_tensor(rlog[:], SB_ps[0][:, 256:256 + NE],
                                                    br_rep[:], ALU.add))
                    EV["rlogv"].wait(v, ri)
                    EV["rmax"].inc(v.tensor_reduce(rneg[:], rlog[:], axis=AX.X,
                                                   op=ALU.max, negate=True))

                @block.scalar
                def _(s, ri=ri):
                    EV["rmax"].wait(s, ri)
                    EV["rexp"].inc(s.activation(rmsk[:], rlog[:], AF.Exp, bias=rneg[:]))

                @block.vector
                def _(v, jl=jl, ri=ri):
                    EV["rexp"].wait(v, ri)
                    EV["rsum_"].inc(v.tensor_reduce(rsum[:], rmsk[:], axis=AX.X, op=ALU.add))
                    EV["rsum_"].wait(v, ri)
                    EV["rgate"].inc(v.reciprocal(rgat[:], rsum[:]))
                    EV["rmask"].inc(v.tensor_scalar(out=rmsk[:], in0=rlog[:],
                                                    scalar1=rneg[:], scalar2=0.0,
                                                    op0=ALU.add, op1=ALU.is_ge))
                    EV["rgate"].wait(v, ri)
                    EV["rmask"].wait(v, ri)
                    EV["rsg"].inc(v.tensor_scalar(out=S_gates[:, jl, :], in0=rmsk[:],
                                                  scalar1=rgat[:], scalar2=None,
                                                  op0=ALU.mult))
                    EV["rsg"].wait(v, ri)
                    EV["rsbf"].inc(v.tensor_copy(S_bf[:], S_gates[:, jl, :]))

                @block.tensor
                def _(t, ri=ri):
                    EV["rsbf"].wait(t, ri)
                    if ri > 0:
                        EV["stTevac"].wait(t, ri - 1)
                    EV["stT"].inc(t.matmul(PT_ps[0:NE, 0:128], S_bf[:], ident[:],
                                           is_transpose=True, start=True, stop=True))

                @block.vector
                def _(v, ri=ri, jl=jl, hf=hf):
                    EV["stT"].wait(v, ri)
                    EV["stTevac"].inc(v.tensor_copy(ST_sb[:], PT_ps[0:NE, 0:128]))
                    gc = hf * JT_HALF + jl
                    EV["mTw"].inc(v.tensor_scalar(
                        out=maskT[:, gc * 128:(gc + 1) * 128],
                        in0=ST_sb[:], scalar1=0.0, scalar2=None, op0=ALU.is_gt))
                    v.tensor_copy(gate_tok[:, hf * 8 + jl:hf * 8 + jl + 1], rgat[:])

                @block.tensor
                def _(t, ri=ri):
                    EV["stTevac"].wait(t, ri)
                    if ri > 0:
                        EV["b2acc"].wait(t, 2 * ri - 1)
                    for seg in range(2):
                        EV["b2f"].inc(t.matmul(PP[seg][:], ST_sb[:],
                                               b2_sb[:, seg * 512:(seg + 1) * 512],
                                               start=True, stop=True))

                @block.vector
                def _(v, jl=jl, ri=ri):
                    for seg in range(2):
                        EV["b2f"].wait(v, 2 * ri + seg)
                        sl = slice(seg * 512, (seg + 1) * 512)
                        EV["b2acc"].inc(v.tensor_tensor(
                            acc[:, jl, sl], acc[:, jl, sl], PP[seg][:], ALU.add))

            if hf == 0:
                @block.sync
                def _(sync):
                    EV["mTw"].wait(sync, JT_HALF - 1)
                    EV["mst"].inc(sync.dma_start(out=maskT_hbm, in_=maskT[:, 0:HT]))

            @block.sync
            def _(sync, hf=hf):
                EV["b2acc"].wait(sync, 2 * rt_count[0] - 1)
                EV["store"].inc(sync.dma_start(
                    out=out.rearrange("j p d -> p j d")[:, hf * 8:(hf + 1) * 8, :],
                    in_=acc[:]))

        @block.sync
        def _(sync):
            EV["xcw"].wait(sync, xcw_of_xtile[15])
            EV["mldT"].inc(sync.dma_start(out=maskT[:, 0:HT], in_=maskT_hbm))

        @block.vector
        def _(v):
            EV["mldT"].wait(v, 0)
            EV["scand"].inc(v.tensor_tensor_scan(
                csum[:], maskT[:], zer81.ap().broadcast_to([8, 2 * HT]), 0.0,
                op0=ALU.add, op1=ALU.add))
            EV["scand"].wait(v, 0)
            v.tensor_scalar(out=csum[:], in0=csum[:], scalar1=-1.0,
                            scalar2=None, op0=ALU.add)
            EV["csm1"].inc(v.tensor_tensor(mp_f16[:], csum[:], maskT[:], ALU.mult))

        @block.tensor
        def _(t):
            EV["csm1"].wait(t, 0)
            EV["rlogv"].wait(t, rt_count[0] - 1)
            for ck in range(4):
                if ck >= 2:
                    EV["gevac"].wait(t, ck - 2)
                t.matmul(SB_ps[ck % 2][0:1, 0:512], evc_sb[:, 1:2],
                         maskT[:, ck * 512:(ck + 1) * 512],
                         start=True, stop=False)
                EV["gmm"].inc(t.matmul(SB_ps[ck % 2][0:1, 0:512], evc_sb[:, 0:1],
                                       mp_f16[:, ck * 512:(ck + 1) * 512],
                                       start=False, stop=True))

        for ck in range(4):
            @block.vector
            def _(v, ck=ck):
                EV["gmm"].wait(v, ck)
                EV["gevac"].inc(v.tensor_copy(g_stage[:], SB_ps[ck % 2][0:1, 0:512]))

            @block.sync
            def _(sync, ck=ck):
                EV["gevac"].wait(sync, ck)
                EV["gtr"].inc(sync.dma_start(
                    out=g_hbm[ck * 512:(ck + 1) * 512, :]
                        .rearrange("(o f) o2 -> o (f o2)", o=1),
                    in_=g_stage[:]))
            @block.vector
            def _(v, ck=ck):
                EV["gtr"].wait(v, ck)

        @block.gpsimd
        def _(g):
            EV["gtr"].wait(g, 3)
            EV["gsp"].inc(g.dma_start(
                out=g_f[:],
                in_=g_hbm.rearrange("(j p) o -> p (j o)", p=128),
                allow_slow_non_contiguous=True))

        @block.vector
        def _(v):
            EV["gsp"].wait(v, 0)
            EV["gidx"].inc(v.tensor_copy(g_i[:], g_f[:]))
            EV["gpe"].inc(v.engine_nop())

        @block.gpsimd
        def _(g):
            EV["fmem"].wait(g, 0)
            EV["tfill"].inc(g.dma_start(
                out=tok_tbl.rearrange("(p c) o -> p (c o)", c=32), in_=fill_i[:]))
            EV["tfill"].inc(g.dma_start(
                out=gate_tbl.rearrange("(p c) o -> p (c o)", c=32), in_=fill_f[:]))
            EV["tfill"].wait(g, 1)
            EV["gidx"].wait(g, 0)
            for c in range(16):
                EV["scat"].inc(g.indirect_dma_start(
                    out=tok_tbl,
                    out_offset=IndirectOffsetOnAxis(ap=g_i[:, c:c + 1], axis=0),
                    in_=iota_sb[:, c:c + 1], in_offset=None))
                EV["scat"].inc(g.indirect_dma_start(
                    out=gate_tbl,
                    out_offset=IndirectOffsetOnAxis(ap=g_i[:, c:c + 1], axis=0),
                    in_=gate_tok[:, c:c + 1], in_offset=None))

        @block.sync
        def _(sync):
            EV["scat"].wait(sync, 31)
            EV["traw"].inc(sync.dma_start(
                out=tok_raw[:], in_=tok_tbl.rearrange("(c p) o -> c (p o)", p=128)))
            EV["traw"].inc(sync.dma_start(
                out=gate_raw[:], in_=gate_tbl.rearrange("(c p) o -> c (p o)", p=128)))

        @block.tensor
        def _(t):
            EV["traw"].wait(t, 1)
            EV["b2acc"].wait(t, 2 * rt_count[0] - 1)
            for k, raw in enumerate((tok_raw, gate_raw)):
                ins = t.matmul(PP[0][:, k * 32:(k + 1) * 32], raw[:],
                               idf32[:], is_transpose=True,
                               start=(k == 0), stop=(k == 1))
            EV["ttr"].inc(ins)

        @block.vector
        def _(v):
            EV["ttr"].wait(v, 0)
            v.tensor_copy(tok_f[:], PP[0][:, 0:32])
            v.tensor_copy(gate_slots[:], PP[0][:, 32:64])
            EV["tbl"].inc(v.tensor_copy(tok_i[:], tok_f[:]))

        wl_ffn_base = EV["wload"].n
        for e in range(NE):
            @block.gpsimd
            def _(g, e=e):
                EV["tbl"].wait(g, 0)
                EV["xw"].wait(g, EV["xw"].n - 1)
                if (e - 1) in xg_free:
                    EV["tp"].wait(g, xg_free[e - 1])
                for tile in range(4):
                    EV["xgdma"].inc(g.indirect_dma_start(
                        out=xg[:, tile], out_offset=None,
                        in_=xln2_hbm,
                        in_offset=IndirectOffsetOnAxis(
                            ap=tok_i[:, e * 4 + tile:e * 4 + tile + 1], axis=0)))

            @block.tensor
            def _(t, e=e):
                if e >= 1:
                    EV["hfill"].wait(t, (e * 4) * 8 - 1)
                for tile in range(4):
                    EV["xgdma"].wait(t, e * 4 + tile)
                    for half4 in range(4):
                        m = tp_count[0]
                        tp_count[0] += 1
                        if m > 0:
                            EV["tpevac"].wait(t, m - 1)
                        for i2 in range(2):
                            fc = half4 * 2 + i2
                            ins = t.matmul(tp_ps[:, i2 * 128:(i2 + 1) * 128],
                                           xg[:, tile, fc * 128:(fc + 1) * 128],
                                           ident[:], is_transpose=True,
                                           start=(i2 == 0), stop=(i2 == 1))
                        EV["tp"].inc(ins)
            xg_free[e] = tp_count[0] - 1

            tpe_hold = tp_count[0]

            @block.vector
            def _(v, e=e, tpe_hold=tpe_hold):
                for k in range(16):
                    m = tpe_hold - 16 + k
                    tile, half4 = divmod(k, 4)
                    EV["tp"].wait(v, m)
                    EV["tpevac"].inc(v.tensor_copy(
                        xgT[:, half4 * 2:(half4 + 1) * 2,
                            tile * 128:(tile + 1) * 128],
                        tp_ps[:]))

            for qt in range(NQT):
                it = it_count[0]
                it_count[0] += 1
                buf = it % 2
                W1q = WA[buf * 2]
                W2q = WA[buf * 2 + 1]
                w1_ord = wl_ffn_base + it * 2
                w2_ord = w1_ord + 1

                @block.gpsimd
                def _(g, e=e, qt=qt, it=it, W1q=W1q, W2q=W2q):
                    if it == 0:
                        EV["qkvfill"].wait(g, qkv_count[0] - 1)
                        EV["wofill"].wait(g, wo_count[0] - 1)
                    if it >= 2:
                        EV["yfill"].wait(g, (it - 1) * 8 - 1)
                    EV["wload"].inc(g.dma_start(
                        out=W1q[:],
                        in_=W1[e, :, qt * 1024:(qt + 1) * 1024]
                            .rearrange("(fc p) o -> p fc o", p=128)))
                    EV["wload"].inc(g.dma_start(
                        out=W2q[:],
                        in_=W2[e, qt * 1024:(qt + 1) * 1024, :]
                            .rearrange("(fc p) o -> p fc o", p=128)))

                @block.tensor
                def _(t, e=e, qt=qt, it=it, W1q=W1q, W2q=W2q,
                      w1_ord=w1_ord, w2_ord=w2_ord, tpe_hold=tpe_hold):
                    EV["wload"].wait(t, w1_ord)
                    if it == 0:
                        EV["gidx"].wait(t, 0)
                    if qt == 0:
                        EV["tpevac"].wait(t, tpe_hold - 1)
                    for m in range(8):
                        hi = h_count[0]
                        h_count[0] += 1
                        if hi >= 2:
                            EV["hevac"].wait(t, hi - 2)
                        hb = SB_ps[hi % 2]
                        for fc in range(8):
                            ins = t.matmul(hb[:, 0:CAP],
                                           W1q[:, fc, m * 128:(m + 1) * 128],
                                           xgT[:, fc, :],
                                           start=(fc == 0), stop=(fc == 7))
                        EV["hfill"].inc(ins)
                    EV["wload"].wait(t, w2_ord)
                    if it == 0:
                        EV["tbl"].wait(t, 0)
                    for tile in range(4):
                        for dseg in range(2):
                            yi = y_count[0]
                            y_count[0] += 1
                            if yi >= 2:
                                EV["yadd"].wait(t, yi - 2)
                            yb = PP[yi % 2]
                            EV["hevac"].wait(t, it * 8 + 7)
                            for Fc in range(8):
                                ins = t.matmul(
                                    yb[:],
                                    h_sb[:, Fc, tile * 128:(tile + 1) * 128],
                                    W2q[:, Fc, dseg * 512:(dseg + 1) * 512],
                                    start=(Fc == 0), stop=(Fc == 7))
                            EV["yfill"].inc(ins)

                @block.scalar
                def _(s, e=e, qt=qt, it=it):
                    if it > 0:
                        EV["yfill"].wait(s, it * 8 - 1)
                    for m in range(8):
                        hi = it * 8 + m
                        EV["hfill"].wait(s, hi)
                        EV["hevac"].inc(s.activation(
                            h_sb[:, m, :], SB_ps[hi % 2][:, 0:CAP], AF.Gelu,
                            bias=b1_sb[:, e, qt * 8 + m:qt * 8 + m + 1]))

                @block.vector
                def _(v, e=e, qt=qt, it=it):
                    for tile in range(4):
                        for dseg in range(2):
                            yi = it * 8 + tile * 2 + dseg
                            if qt == 0 and tile == 0 and dseg == 0 and e >= 1:
                                EV["yscat"].wait(v, e * 4 - 1)
                            EV["yfill"].wait(v, yi)
                            sl = slice(dseg * 512, (dseg + 1) * 512)
                            if qt == 0:
                                ins = v.tensor_copy(y_stage[:, tile, sl],
                                                    PP[yi % 2][:])
                            else:
                                ins = v.tensor_tensor(y_stage[:, tile, sl],
                                                      y_stage[:, tile, sl],
                                                      PP[yi % 2][:], ALU.add)
                            EV["yadd"].inc(ins)

            @block.vector
            def _(v, e=e):
                for tile in range(4):
                    EV["ysc"].inc(v.tensor_scalar_mul(
                        y_stage[:, tile], y_stage[:, tile],
                        gate_slots[:, e * 4 + tile:e * 4 + tile + 1]))

            @block.gpsimd
            def _(g, e=e):
                EV["ysc"].wait(g, e * 4 + 3)
                for tile in range(4):
                    EV["yscat"].inc(g.indirect_dma_start(
                        out=y_hbm,
                        out_offset=IndirectOffsetOnAxis(
                            ap=tok_i[:, e * 4 + tile:e * 4 + tile + 1], axis=0),
                        in_=y_stage[:, tile], in_offset=None))

        for tg in range(16):
            hf2, jl = divmod(tg, 8)

            @block.sync
            def _(sync, tg=tg, hf2=hf2, jl=jl):
                if tg == 0:
                    EV["yscat"].wait(sync, NE * 4 - 1)
                if tg >= 2:
                    EV["yacc"].wait(sync, tg - 2)
                row0 = tg * 128
                EV["yrld"].inc(sync.dma_start(
                    out=yld[:, tg % 2], in_=y_hbm[row0:row0 + 128, :]))
                EV["mld"].inc(sync.dma_start(
                    out=acc[:, jl], in_=out[tg]))

            @block.vector
            def _(v, tg=tg, jl=jl):
                EV["yrld"].wait(v, tg)
                EV["mld"].wait(v, tg)
                EV["yacc"].inc(v.tensor_tensor(
                    acc[:, jl], acc[:, jl], yld[:, tg % 2], ALU.add))

            @block.sync
            def _(sync, tg=tg, jl=jl):
                EV["yacc"].wait(sync, tg)
                EV["store"].inc(sync.dma_start(out=out[tg], in_=acc[:, jl]))

        @block.sync
        def _(sync):
            for name in DMA_EV:
                if EV[name].n:
                    sync.wait_ge(EV[name].h, EV[name].step * EV[name].n)
    return nc


_NC_CACHE = {}


def _get_nc():
    if "nc" not in _NC_CACHE:
        _NC_CACHE["nc"] = build()
    return _NC_CACHE["nc"]


def kernel(L0_attn_hidden_state, L0_attn_Wq, L0_attn_bq, L0_attn_Wk, L0_attn_bk,
           L0_attn_Wv, L0_attn_bv, L0_attn_Wo, L0_attn_bo,
           L0_ffn_W1, L0_ffn_b1, L0_ffn_W2, L0_ffn_b2, L0_ffn_Wr, L0_ffn_br):
    x = np.ascontiguousarray(np.asarray(L0_attn_hidden_state, dtype=np.float32))
    B, S, _ = x.shape
    nc = _get_nc()

    ident = np.eye(128, dtype=bfd)
    ii, jj = np.meshgrid(np.arange(128), np.arange(128), indexing="ij")
    mask8 = np.where(ii % 8 == jj % 8, 0.0, -1e30).astype(np.float32)
    idf32 = np.eye(32, dtype=np.float32)
    evc = np.stack([np.ones(8), np.arange(8) * 512.0], axis=1).astype(np.float16)
    iota = (np.arange(16)[None, :] * 128 + np.arange(128)[:, None]).astype(np.float32)

    common = dict(
        Wq=np.ascontiguousarray(L0_attn_Wq, np.float32),
        Wk=np.ascontiguousarray(L0_attn_Wk, np.float32),
        Wv=np.ascontiguousarray(L0_attn_Wv, np.float32),
        Wo=np.ascontiguousarray(L0_attn_Wo, np.float32),
        bq=np.ascontiguousarray(L0_attn_bq, np.float32),
        bk=np.ascontiguousarray(L0_attn_bk, np.float32),
        bv=np.ascontiguousarray(L0_attn_bv, np.float32),
        bo=np.ascontiguousarray(L0_attn_bo, np.float32),
        W1=np.ascontiguousarray(L0_ffn_W1, np.float32),
        b1=np.ascontiguousarray(L0_ffn_b1, np.float32),
        W2=np.ascontiguousarray(L0_ffn_W2, np.float32),
        b2=np.ascontiguousarray(L0_ffn_b2, np.float32),
        Wr=np.ascontiguousarray(L0_ffn_Wr, np.float32),
        br=np.ascontiguousarray(L0_ffn_br, np.float32),
        ident_hbm=ident, mask_hbm=mask8,
        idf32_hbm=idf32, evc_hbm=evc, iota_hbm=iota,
    )

    in_maps = []
    for c in range(8):
        b, c2 = divmod(c, 2)
        x_own = np.ascontiguousarray(x[b, 2048 * c2: 2048 * (c2 + 1)])
        xr = x[b].reshape(16, 2, 16, 8, DM)[:, c2]
        x_res = np.ascontiguousarray(
            xr.transpose(1, 0, 2, 3).reshape(16, 128, DM))
        im = dict(common)
        im["x_own"] = x_own
        im["x_res"] = x_res
        in_maps.append(im)

    res = run_bass_kernel_spmd(nc, in_maps, core_ids=list(range(8)))

    out = np.empty((B, S, DM), dtype=np.float32)
    for c in range(8):
        b, c2 = divmod(c, 2)
        oc = res.results[c]["out"]
        ov = oc.reshape(16, 16, 8, DM).transpose(1, 0, 2, 3)
        out[b].reshape(16, 2, 16, 8, DM)[:, c2] = ov
    return out


# version 09 (final)
# speedup vs baseline: 1.9157x; 1.0229x over previous
import numpy as np
import ml_dtypes
import concourse.bass as bass
import concourse.mybir as mybir
from concourse.bass_utils import run_bass_kernel_spmd
from concourse.bass import IndirectOffsetOnAxis

F32 = mybir.dt.float32
F16 = mybir.dt.float16
I32 = mybir.dt.int32
BF16 = mybir.dt.bfloat16
AF = mybir.ActivationFunctionType
ALU = mybir.AluOpType
AX = mybir.AxisListType
bfd = ml_dtypes.bfloat16

H = 16
D = 64
DM = 1024
FF = 4096
NE = 8
CT = 512
NGC = CT // 8
NQT = 4
NHALF = 2
JT_HALF = 8
HT = 1024
ITERS_HALF = NE * NQT
CAP = 512


class Evt:
    def __init__(self, stack, nc, name, step=1):
        self.h = stack.enter_context(nc.semaphore(name))
        self.step = step
        self.n = 0

    def inc(self, instr):
        instr.then_inc(self.h, self.step)
        self.n += 1
        return instr

    def wait(self, eng, i):
        assert 0 <= i < self.n or True
        eng.wait_ge(self.h, self.step * (i + 1))


def build():
    nc = bass.Bass("TRN2", target_bir_lowering=False,
                   detect_race_conditions=False)
    x_own = nc.dram_tensor("x_own", [2048, DM], F32, kind="ExternalInput").ap()
    x_res = nc.dram_tensor("x_res", [16, 128, DM], F32, kind="ExternalInput").ap()
    Wq = nc.dram_tensor("Wq", [DM, DM], F32, kind="ExternalInput").ap()
    Wk = nc.dram_tensor("Wk", [DM, DM], F32, kind="ExternalInput").ap()
    Wv = nc.dram_tensor("Wv", [DM, DM], F32, kind="ExternalInput").ap()
    Wo = nc.dram_tensor("Wo", [DM, DM], F32, kind="ExternalInput").ap()
    bq = nc.dram_tensor("bq", [DM], F32, kind="ExternalInput").ap()
    bk = nc.dram_tensor("bk", [DM], F32, kind="ExternalInput").ap()
    bv = nc.dram_tensor("bv", [DM], F32, kind="ExternalInput").ap()
    bo = nc.dram_tensor("bo", [DM], F32, kind="ExternalInput").ap()
    W1 = nc.dram_tensor("W1", [NE, DM, FF], F32, kind="ExternalInput").ap()
    b1 = nc.dram_tensor("b1", [NE, FF], F32, kind="ExternalInput").ap()
    W2 = nc.dram_tensor("W2", [NE, FF, DM], F32, kind="ExternalInput").ap()
    b2 = nc.dram_tensor("b2", [NE, DM], F32, kind="ExternalInput").ap()
    Wr = nc.dram_tensor("Wr", [DM, NE], F32, kind="ExternalInput").ap()
    br = nc.dram_tensor("br", [NE], F32, kind="ExternalInput").ap()
    ident_hbm = nc.dram_tensor("ident_hbm", [128, 128], BF16, kind="ExternalInput").ap()
    idf32_hbm = nc.dram_tensor("idf32_hbm", [32, 32], F32, kind="ExternalInput").ap()
    mask_hbm = nc.dram_tensor("mask_hbm", [128, 128], F32, kind="ExternalInput").ap()
    evc_hbm = nc.dram_tensor("evc_hbm", [8, 2], F16, kind="ExternalInput").ap()
    iota_hbm = nc.dram_tensor("iota_hbm", [128, 16], F32, kind="ExternalInput").ap()
    out = nc.dram_tensor("out", [16, 128, DM], F32, kind="ExternalOutput").ap()
    NTOK = 2 * HT
    xln2_hbm = nc.dram_tensor("xln2_hbm", [NTOK + 1, DM], BF16, kind="Internal").ap()
    tok_tbl = nc.dram_tensor("tok_tbl", [NE * CAP, 1], F32, kind="Internal").ap()
    gate_tbl = nc.dram_tensor("gate_tbl", [NE * CAP, 1], F32, kind="Internal").ap()
    g_hbm = nc.dram_tensor("g_hbm", [NTOK, 1], F32, kind="Internal").ap()
    maskT_hbm = nc.dram_tensor("maskT_hbm", [8, HT], F16, kind="Internal").ap()

    from contextlib import ExitStack
    st = ExitStack()
    with st:
        sbt = lambda *a: st.enter_context(nc.sbuf_tensor(*a))
        pst = lambda *a: st.enter_context(nc.psum_tensor(*a))

        ident = sbt("ident", [128, 128], BF16)
        mask8 = sbt("mask8", [128, 128], F32)
        bqkv_sb = sbt("bqkv_sb", [64, 3, H], F32)
        bo_rep = sbt("bo_rep", [128, DM], F32)
        br_rep = sbt("br_rep", [128, NE], F32)
        Wr_sb = sbt("Wr_sb", [128, 8, NE], BF16)
        b1_sb = sbt("b1_sb", [128, NE, 32], F32)
        b2_sb = sbt("b2_sb", [NE, DM], BF16)
        eps_t = sbt("eps_t", [128, 1], F32)
        ln_s = sbt("ln_s", [128, 2, 4], F32)
        sqd = sbt("sqd", [128, DM], F32)
        acc = sbt("acc", [128, JT_HALF, DM], F32)
        WA = [sbt(f"wa{i}", [128, 8, DM], BF16) for i in range(4)]
        x_sb = sbt("x_sb", [128, 2, DM], F32)
        xc_sb = sbt("xc_sb", [128, 2, DM], F32)
        xln_sb = sbt("xln_sb", [128, 2, DM], BF16)
        xlnT = sbt("xlnT", [128, 8, CT], BF16)
        preT2 = sbt("preT2", [128, 4, 8, 128], BF16)
        T_sb = sbt("T_sb", [128, 2, 4, 128], F32)
        P_sb = T_sb
        Pn_sb = sbt("Pn_sb", [128, 2, 4, 128], BF16)
        PT_sb = sbt("PT_sb", [128, 2, 4, 128], BF16)
        Vg_sb = sbt("Vg_sb", [128, 2, 4, D], BF16)
        smax = sbt("smax", [128, 2, 4], F32)
        ssum = sbt("ssum", [128, 2, 4], F32)
        srec = sbt("srec", [128, 2, 4], F32)
        S_gates = sbt("S_gates", [128, JT_HALF, NE], F32)
        rlog = sbt("rlog", [128, NE], F32)
        rmsk = sbt("rmsk", [128, NE], F32)
        rneg = sbt("rneg", [128, 1], F32)
        rsum = sbt("rsum", [128, 1], F32)
        rgat = sbt("rgat", [128, 1], F32)
        S_bf = sbt("S_bf", [128, NE], BF16)
        ST_sb = sbt("ST_sb", [NE, 128], BF16)
        idf32 = sbt("idf32", [32, 32], F32)
        evc_sb = sbt("evc_sb", [8, 2], F16)
        iota_sb = sbt("iota_sb", [128, 16], F32)
        zer81 = sbt("zer81", [8, 1], F32)
        g_f = sbt("g_f", [128, 16], F32)
        g_stage = sbt("g_stage", [1, 512], F32)
        g_i = sbt("g_i", [128, 16], I32)
        gate_tok = sbt("gate_tok", [128, 16], F32)
        tok_i = sbt("tok_i", [128, 32], I32)
        tok_f = sbt("tok_f", [128, 32], F32)
        gate_slots = sbt("gate_slots", [128, 32], F32)
        fill_i = sbt("fill_i", [128, 32], F32)
        fill_f = sbt("fill_f", [128, 32], F32)
        tok_raw = sbt("tok_raw", [32, 128], F32)
        gate_raw = sbt("gate_raw", [32, 128], F32)
        qkv_region = sbt("qkv_region", [128, 48 * 1024 // 4], F32)
        base = nc.lookup_mloc(qkv_region).addr
        q_all = nc.alloc_sbuf_tensor_at("q_all", [64, NGC, H, 8], BF16, offset=base)
        k_all = nc.alloc_sbuf_tensor_at("k_all", [64, NGC, H, 8], BF16, offset=base + 16 * 1024)
        v_all = nc.alloc_sbuf_tensor_at("v_all", [64, NGC, H, 8], BF16, offset=base + 32 * 1024)
        xfT = nc.alloc_sbuf_tensor_at("xfT", [128, 8, 1024], BF16, offset=base)
        h_sb = nc.alloc_sbuf_tensor_at("h_sb", [128, 8, CAP], BF16, offset=base)
        xg = nc.alloc_sbuf_tensor_at("xg", [128, 4, DM], BF16, offset=base + 8 * 1024)
        xgT = nc.alloc_sbuf_tensor_at("xgT", [128, 8, CAP], BF16, offset=base + 16 * 1024)
        y_stage = nc.alloc_sbuf_tensor_at("y_stage", [128, 4, DM], F32, offset=base + 24 * 1024)
        maskT = nc.alloc_sbuf_tensor_at("maskT", [8, 2 * HT], F16,
                                        offset=nc.lookup_mloc(x_sb).addr)
        csum = nc.alloc_sbuf_tensor_at("csum", [8, 2 * HT], F16,
                                       offset=nc.lookup_mloc(x_sb).addr + 4096)
        mp_f16 = nc.alloc_sbuf_tensor_at("mp_f16", [8, 2 * HT], F16,
                                         offset=nc.lookup_mloc(xc_sb).addr)
        zrow = nc.alloc_sbuf_tensor_at("zrow", [1, DM], BF16, offset=base + 40 * 1024)

        PP = [pst("pp0", [128, 512], F32), pst("pp1", [128, 512], F32)]
        SB_ps = [pst("S0", [128, 512], F32), pst("S1", [128, 512], F32)]
        tp_ps = pst("tp", [128, 256], BF16)
        PT_ps = pst("PT0", [128, 512], BF16)
        VG_ps = pst("VG0", [128, 4 * D], BF16)
        OT_ps = pst("OT0", [64, 256], F32)

        ev_names = [
            "loads", "wload", "store", "init", "bq8",
            "xload", "xresload", "negsum", "negmean", "ssq", "std", "rstd",
            "xln", "xcw", "tp", "tpevac", "qkvfill", "qkvevac",
            "sqS", "sqVG", "sqPT", "sqOT", "Tadd", "Tmax", "Tsub", "P",
            "Psum", "Prec", "Pnm", "sqcopies", "OTevac", "wofill", "woevac", "boadd",
            "rfill", "rlogv", "rmax", "rexp", "rsum_", "rgate", "rmask",
            "rsg", "rsbf", "stT", "stTevac", "b2f", "b2acc",
            "hfill", "hevac", "yfill", "yadd", "ysc", "yacc",
            "xw", "scand", "csm1", "gmm", "gevac", "gtr", "gsp", "gpe", "gidx",
            "tfill", "scat", "tbl", "traw", "ttr", "xgdma", "xgtr", "xgevac",
            "yscat", "yrld", "mld", "fmem", "mTw", "mst", "mldT", "xgini",
        ]
        DMA_EV = ("loads", "wload", "store", "xload", "xresload",
                  "xw", "tfill", "scat", "traw", "xgdma", "yscat", "yrld", "gtr", "mld",
                  "mst", "mldT", "gsp")
        EV = {n: Evt(st, nc, n, 16 if n in DMA_EV else 1) for n in ev_names}

        rg_gather = st.enter_context(nc.gpsimd.register(name="rg_gather"))
        rg_scat = st.enter_context(nc.gpsimd.register(name="rg_scat"))
        bc = {}
        block = st.enter_context(nc.Block())

        @block.sync
        def _(sync):
            EV["loads"].inc(sync.dma_start(out=ident[:], in_=ident_hbm))
            EV["loads"].inc(sync.dma_start(out=mask8[:], in_=mask_hbm))
            EV["loads"].inc(sync.dma_start(out=bo_rep[:], in_=bo[None, :].broadcast_to([128, DM])))
            EV["loads"].inc(sync.dma_start(out=br_rep[:], in_=br[None, :].broadcast_to([128, NE])))
            for i, bsrc in enumerate((bq, bk, bv)):
                EV["loads"].inc(sync.dma_start(
                    out=bqkv_sb[:, i, :], in_=bsrc.rearrange("(h d) -> d h", d=64),
                    allow_slow_non_contiguous=True))
            EV["loads"].inc(sync.dma_start(
                out=b1_sb[:], in_=b1.rearrange("e (mq p) -> p e mq", p=128),
                allow_slow_non_contiguous=True))
            EV["loads"].inc(sync.dma_start(out=idf32[:], in_=idf32_hbm))
            EV["loads"].inc(sync.dma_start(out=evc_sb[:], in_=evc_hbm))
            EV["loads"].inc(sync.dma_start(out=iota_sb[:], in_=iota_hbm))
        N_CONST_LOADS = EV["loads"].n

        @block.gpsimd
        def _(g):
            g.reg_mov(rg_gather, NTOK)
            bc["g"] = g.snap(rg_gather)
            g.reg_mov(rg_scat, NTOK - 1)
            bc["s"] = g.snap(rg_scat)
            EV["wload"].inc(g.dma_start(out=Wr_sb[:], in_=Wr.rearrange("(fc p) e -> p fc e", p=128)))
            EV["wload"].inc(g.dma_start(out=b2_sb[:], in_=b2))
        N_CONST_WLOADS = EV["wload"].n

        @block.vector
        def _(v):
            v.memset(T_sb[:], 0.0)
            v.memset(zer81[:], 0.0)
            v.memset(zrow[:], 0.0)
            v.memset(fill_i[:], 4096.0)
            EV["fmem"].inc(v.memset(fill_f[:], 0.0))
            EV["init"].inc(v.memset(eps_t[:], 1e-5))

        @block.scalar
        def _(s):
            EV["loads"].wait(s, N_CONST_LOADS - 1)
            EV["init"].wait(s, 0)
            EV["bq8"].inc(s.activation(bqkv_sb[:, 0, :], bqkv_sb[:, 0, :],
                                       AF.Identity, scale=0.125))

        @block.sync
        def _(sync):
            EV["fmem"].wait(sync, 0)
            EV["xw"].inc(sync.dma_start(out=xln2_hbm[2 * HT:2 * HT + 1, :], in_=zrow[:]))

        WQKVO = (Wq, Wk, Wv, Wo)
        W_re = lambda W: W.rearrange("(fc p) o -> p fc o", p=128)

        ln_count = [0]
        tp_count = [0]
        xcw_of_xtile = {}
        xw_of_ln = {}
        xg_free = {}

        def emit_ln_tile(src_ap):
            i = ln_count[0]
            ln_count[0] += 1
            p = i % 2

            @block.vector
            def _(v):
                EV["negsum"].inc(v.tensor_reduce(
                    ln_s[:, p, 0:1], src_ap, axis=AX.X, op=ALU.add, negate=True))
                EV["negsum"].wait(v, i)
                EV["negmean"].inc(v.tensor_scalar_mul(
                    ln_s[:, p, 1:2], ln_s[:, p, 0:1], 1.0 / DM))
                EV["negmean"].wait(v, i)
                EV["xcw"].inc(v.tensor_scalar(
                    out=xc_sb[:, p], in0=src_ap, scalar1=ln_s[:, p, 1:2],
                    scalar2=None, op0=ALU.add))

            @block.scalar
            def _(s):
                EV["xcw"].wait(s, i)
                EV["ssq"].inc(s.activation(sqd[:], xc_sb[:, p], AF.Square,
                                           accum_out=ln_s[:, p, 2:3]))
                EV["ssq"].wait(s, i)
                EV["std"].inc(s.activation(ln_s[:, p, 3:4], ln_s[:, p, 2:3],
                                           AF.Sqrt, scale=1.0 / DM, bias=eps_t[:]))

            @block.vector
            def _(v):
                EV["std"].wait(v, i)
                EV["rstd"].inc(v.reciprocal(ln_s[:, p, 1:2], ln_s[:, p, 3:4]))
                EV["rstd"].wait(v, i)
                if tp_count[0] >= 8:
                    EV["tp"].wait(v, tp_count[0] - 5)
                if (i - 2) in xw_of_ln:
                    EV["xw"].wait(v, xw_of_ln[i - 2])
                EV["xln"].inc(v.tensor_scalar_mul(
                    xln_sb[:, p], xc_sb[:, p], ln_s[:, p, 1:2]))
            return i

        def emit_transposes(dstT, dst_col0, xln_i):
            p = xln_i % 2
            for half4 in range(4):
                m = tp_count[0]
                tp_count[0] += 1

                @block.tensor
                def _(t, half4=half4, m=m):
                    if half4 == 0:
                        EV["xln"].wait(t, xln_i)
                    if m > 0:
                        EV["tpevac"].wait(t, m - 1)
                    for i2 in range(2):
                        fc = half4 * 2 + i2
                        ins = t.matmul(tp_ps[:, i2 * 128:(i2 + 1) * 128],
                                       xln_sb[:, p, fc * 128:(fc + 1) * 128],
                                       ident[:], is_transpose=True,
                                       start=(i2 == 0), stop=(i2 == 1))
                    EV["tp"].inc(ins)

                @block.vector
                def _(v, half4=half4, m=m):
                    EV["tp"].wait(v, m)
                    EV["tpevac"].inc(v.tensor_copy(
                        dstT[:, half4 * 2:(half4 + 1) * 2, dst_col0:dst_col0 + 128],
                        tp_ps[:]))

        grp_count = [0]
        qkv_count = [0]
        wo_count = [0]
        xl_count = [0]
        it_count = [0]
        h_count = [0]
        y_count = [0]
        rt_count = [0]

        def gslice(t_all, g):
            return t_all[:, g].rearrange("d h e -> d (h e)")

        for hf in range(NHALF):
            @block.sync
            def _(sync, hf=hf):
                EV["xresload"].inc(sync.dma_start(
                    out=acc[:],
                    in_=x_res.rearrange("j p d -> p j d")[:, hf * 8:(hf + 1) * 8, :]))

            @block.vector
            def _(v, hf=hf):
                EV["loads"].wait(v, N_CONST_LOADS - 1)
                EV["xresload"].wait(v, hf)
                for j in range(JT_HALF):
                    ins = v.tensor_tensor(acc[:, j], acc[:, j], bo_rep[:], ALU.add)
                EV["boadd"].inc(ins)

            @block.gpsimd
            def _(g, hf=hf):
                if hf > 0:
                    EV["qkvfill"].wait(g, qkv_count[0] - 1)
                    EV["wofill"].wait(g, wo_count[0] - 1)
                for i, W in enumerate(WQKVO):
                    EV["wload"].inc(g.dma_start(out=WA[i][:], in_=W_re(W)))
            wl_attn_last = EV["wload"].n - 1

            for ch in range(2):
                for n in range(4):
                    tile = (hf * 2 + ch) * 4 + n
                    xi = xl_count[0]
                    xl_count[0] += 1

                    @block.sync
                    def _(sync, tile=tile, xi=xi):
                        if xi == 8:
                            EV["mst"].wait(sync, 0)
                        if xi >= 2:
                            EV["xcw"].wait(sync, xcw_of_xtile[xi - 2])
                        EV["xload"].inc(sync.dma_start(
                            out=x_sb[:, xi % 2],
                            in_=x_own.rearrange("(t p) d -> t p d", p=128)[tile]))

                    @block.vector
                    def _(v, xi=xi):
                        EV["xload"].wait(v, xi)

                    ln_i = emit_ln_tile(x_sb[:, xi % 2])
                    xcw_of_xtile[xi] = ln_i
                    emit_transposes(xlnT, n * 128, ln_i)

                @block.tensor
                def _(t, hf=hf):
                    EV["tpevac"].wait(t, tp_count[0] - 1)
                    EV["bq8"].wait(t, 0)
                    EV["wload"].wait(t, wl_attn_last)
                    if hf > 0 and qkv_count[0] == hf * 96:
                        EV["b2acc"].wait(t, 2 * rt_count[0] - 1)
                    if wo_count[0] > 0:
                        EV["woevac"].wait(t, wo_count[0] - 1)
                    for idx in range(3 * H):
                        pi, h = divmod(idx, H)
                        gi = qkv_count[0] + idx
                        if gi >= 2:
                            EV["qkvevac"].wait(t, gi - 2)
                        ps = PP[gi % 2]
                        for fc in range(8):
                            ins = t.matmul(ps[0:64, :], WA[pi][:, fc, h * 64:(h + 1) * 64],
                                           xlnT[:, fc, :], start=(fc == 0), stop=(fc == 7))
                        EV["qkvfill"].inc(ins)

                @block.scalar
                def _(s, hf=hf):
                    dst_all = (q_all, k_all, v_all)
                    for idx in range(3 * H):
                        pi, h = divmod(idx, H)
                        gi = qkv_count[0] + idx
                        EV["qkvfill"].wait(s, gi)
                        EV["qkvevac"].inc(s.activation(
                            dst_all[pi][:, :, h, :], PP[gi % 2][0:64, :], AF.Identity,
                            bias=bqkv_sb[:, pi, h:h + 1],
                            scale=(0.125 if pi == 0 else 1.0)))
                qkv_count[0] += 3 * H

                g0 = grp_count[0]

                @block.tensor
                def _(t, g0=g0):
                    EV["qkvevac"].wait(t, qkv_count[0] - 1)
                    sq0 = g0 // 4
                    for sl in range(NGC // 4):
                        sq = sq0 + sl
                        pr = sq % 2
                        if sq >= 2:
                            EV["Tadd"].wait(t, sq - 2)
                        for i4 in range(4):
                            ins = t.matmul(SB_ps[pr][:, i4 * 128:(i4 + 1) * 128],
                                           gslice(q_all, sl * 4 + i4),
                                           gslice(k_all, sl * 4 + i4),
                                           start=(i4 == 0), stop=(i4 == 3))
                        EV["sqS"].inc(ins)
                        if sq >= 1:
                            EV["sqcopies"].wait(t, sq - 1)
                        for i4 in range(4):
                            ins = t.matmul(VG_ps[:, i4 * 64:(i4 + 1) * 64],
                                           gslice(v_all, sl * 4 + i4),
                                           ident[0:64, 0:64], is_transpose=True,
                                           start=(i4 == 0), stop=(i4 == 3))
                        EV["sqVG"].inc(ins)
                        EV["Pnm"].wait(t, sq)
                        for i4 in range(4):
                            ins = t.matmul(PT_ps[:, i4 * 128:(i4 + 1) * 128],
                                           Pn_sb[:, pr, i4], ident[:],
                                           is_transpose=True,
                                           start=(i4 == 0), stop=(i4 == 3))
                        EV["sqPT"].inc(ins)
                        EV["sqcopies"].wait(t, sq)
                        for pp2 in range(2):
                            pair = sq * 2 + pp2
                            if pair >= 1:
                                EV["OTevac"].wait(t, pair - 1)
                            for i2 in range(2):
                                i4 = pp2 * 2 + i2
                                ins = t.matmul(OT_ps[:, i2 * 128:(i2 + 1) * 128],
                                               Vg_sb[:, pr, i4], PT_sb[:, pr, i4],
                                               start=(i2 == 0), stop=(i2 == 1))
                            EV["sqOT"].inc(ins)

                @block.vector
                def _(v, g0=g0):
                    sq0 = g0 // 4

                    def emit_pair_evac(pair):
                        gl = (pair - sq0 * 2) * 2
                        EV["sqOT"].wait(v, pair)
                        j = gl // 16
                        rr = (gl % 16) // 2
                        ins = None
                        for sp in range(2):
                            srcap = (OT_ps.ap()
                                     .rearrange("d (g2 h u) -> d g2 h u", g2=2, h=H)
                                     [:, :, :, sp::2]
                                     .rearrange("d g2 h u -> d g2 u h"))
                            dst = (preT2.ap()
                                   .rearrange("p j c (h r) -> p j c h r", h=H)
                                   [sp * 64:(sp + 1) * 64, j, :, :, rr])
                            ins = v.tensor_copy(dst, srcap)
                        EV["OTevac"].inc(ins)

                    for sl in range(NGC // 4):
                        sq = sq0 + sl
                        pr = sq % 2
                        if sl > 0:
                            emit_pair_evac(sq * 2 - 2)
                            emit_pair_evac(sq * 2 - 1)
                        EV["sqS"].wait(v, sq)
                        mrep = mask8.ap()[:, None, :].broadcast_to([128, 4, 128])
                        EV["Tadd"].inc(v.tensor_tensor(
                            T_sb[:, pr], SB_ps[pr].ap().rearrange("p (a b) -> p a b", a=4),
                            mrep, ALU.add))
                        EV["Tadd"].wait(v, sq)
                        EV["Tmax"].inc(v.tensor_reduce(
                            smax[:, pr], T_sb[:, pr], axis=AX.X, op=ALU.max, negate=True))
                        EV["Tmax"].wait(v, sq)
                        EV["Tsub"].inc(v.tensor_tensor(
                            T_sb[:, pr], T_sb[:, pr],
                            smax[:, pr][:, :, None].broadcast_to([128, 4, 128]),
                            ALU.add))
                        EV["P"].wait(v, sq)
                        EV["Psum"].inc(v.tensor_reduce(
                            ssum[:, pr], P_sb[:, pr], axis=AX.X, op=ALU.add))
                        EV["Psum"].wait(v, sq)
                        EV["Prec"].inc(v.reciprocal(srec[:, pr], ssum[:, pr]))
                        EV["Prec"].wait(v, sq)
                        EV["Pnm"].inc(v.tensor_tensor(
                            Pn_sb[:, pr], P_sb[:, pr],
                            srec[:, pr][:, :, None].broadcast_to([128, 4, 128]),
                            ALU.mult))
                        EV["sqPT"].wait(v, sq)
                        v.tensor_copy(PT_sb[:, pr], PT_ps[:])
                        EV["sqcopies"].inc(v.tensor_copy(Vg_sb[:, pr], VG_ps[:]))
                    emit_pair_evac((sq0 + NGC // 4) * 2 - 2)
                    emit_pair_evac((sq0 + NGC // 4) * 2 - 1)

                @block.scalar
                def _(s, g0=g0):
                    sq0 = g0 // 4
                    for sl in range(NGC // 4):
                        sq = sq0 + sl
                        pr = sq % 2
                        EV["Tsub"].wait(s, sq)
                        EV["P"].inc(s.activation(P_sb[:, pr], T_sb[:, pr], AF.Exp))
                grp_count[0] += NGC

                @block.tensor
                def _(t):
                    EV["OTevac"].wait(t, grp_count[0] // 2 - 1)
                    for m in range(8):
                        jj, seg = divmod(m, 2)
                        wi = wo_count[0] + m
                        if wi >= 2:
                            EV["woevac"].wait(t, wi - 2)
                        ps = PP[wi % 2]
                        for cc in range(8):
                            ins = t.matmul(ps[:], preT2[:, jj, cc, :],
                                           WA[3][:, cc, seg * 512:(seg + 1) * 512],
                                           start=(cc == 0), stop=(cc == 7))
                        EV["wofill"].inc(ins)

                @block.vector
                def _(v, ch=ch, hf=hf):
                    EV["boadd"].wait(v, hf)
                    for m in range(8):
                        jj, seg = divmod(m, 2)
                        wi = wo_count[0] + m
                        EV["wofill"].wait(v, wi)
                        jl = ch * 4 + jj
                        sl = slice(seg * 512, (seg + 1) * 512)
                        EV["woevac"].inc(v.tensor_tensor(
                            acc[:, jl, sl], acc[:, jl, sl], PP[wi % 2][:], ALU.add))
                wo_count[0] += 8

            for jl in range(JT_HALF):
                ln_i = emit_ln_tile(acc[:, jl])
                emit_transposes(xfT, jl * 128, ln_i)

                @block.sync
                def _(sync, jl=jl, ln_i=ln_i, hf=hf):
                    EV["xln"].wait(sync, ln_i)
                    row0 = hf * HT + jl * 128
                    EV["xw"].inc(sync.dma_start(
                        out=xln2_hbm[row0:row0 + 128, :],
                        in_=xln_sb[:, ln_i % 2]))
                xw_of_ln[ln_i] = EV["xw"].n - 1

            for jl in range(JT_HALF):
                ri = rt_count[0]
                rt_count[0] += 1

                @block.tensor
                def _(t, jl=jl, ri=ri):
                    EV["tpevac"].wait(t, tp_count[0] - (JT_HALF - jl - 1) * 4 - 1)
                    if ri > 0:
                        EV["rlogv"].wait(t, ri - 1)
                    EV["wload"].wait(t, N_CONST_WLOADS - 1)
                    for fc in range(8):
                        ins = t.matmul(SB_ps[0][:, 256:256 + NE],
                                       xfT[:, fc, jl * 128:(jl + 1) * 128],
                                       Wr_sb[:, fc, :], start=(fc == 0), stop=(fc == 7))
                    EV["rfill"].inc(ins)

                @block.vector
                def _(v, ri=ri):
                    EV["rfill"].wait(v, ri)
                    EV["rlogv"].inc(v.tensor_tensor(rlog[:], SB_ps[0][:, 256:256 + NE],
                                                    br_rep[:], ALU.add))
                    EV["rlogv"].wait(v, ri)
                    EV["rmax"].inc(v.tensor_reduce(rneg[:], rlog[:], axis=AX.X,
                                                   op=ALU.max, negate=True))

                @block.scalar
                def _(s, ri=ri):
                    EV["rmax"].wait(s, ri)
                    EV["rexp"].inc(s.activation(rmsk[:], rlog[:], AF.Exp, bias=rneg[:]))

                @block.vector
                def _(v, jl=jl, ri=ri):
                    EV["rexp"].wait(v, ri)
                    EV["rsum_"].inc(v.tensor_reduce(rsum[:], rmsk[:], axis=AX.X, op=ALU.add))
                    EV["rsum_"].wait(v, ri)
                    EV["rgate"].inc(v.reciprocal(rgat[:], rsum[:]))
                    EV["rmask"].inc(v.tensor_scalar(out=rmsk[:], in0=rlog[:],
                                                    scalar1=rneg[:], scalar2=0.0,
                                                    op0=ALU.add, op1=ALU.is_ge))
                    EV["rgate"].wait(v, ri)
                    EV["rmask"].wait(v, ri)
                    EV["rsg"].inc(v.tensor_scalar(out=S_gates[:, jl, :], in0=rmsk[:],
                                                  scalar1=rgat[:], scalar2=None,
                                                  op0=ALU.mult))
                    EV["rsg"].wait(v, ri)
                    EV["rsbf"].inc(v.tensor_copy(S_bf[:], S_gates[:, jl, :]))

                @block.tensor
                def _(t, ri=ri):
                    EV["rsbf"].wait(t, ri)
                    if ri > 0:
                        EV["stTevac"].wait(t, ri - 1)
                    EV["stT"].inc(t.matmul(PT_ps[0:NE, 0:128], S_bf[:], ident[:],
                                           is_transpose=True, start=True, stop=True))

                @block.vector
                def _(v, ri=ri, jl=jl, hf=hf):
                    EV["stT"].wait(v, ri)
                    EV["stTevac"].inc(v.tensor_copy(ST_sb[:], PT_ps[0:NE, 0:128]))
                    gc = hf * JT_HALF + jl
                    EV["mTw"].inc(v.tensor_scalar(
                        out=maskT[:, gc * 128:(gc + 1) * 128],
                        in0=ST_sb[:], scalar1=0.0, scalar2=None, op0=ALU.is_gt))
                    v.tensor_copy(gate_tok[:, hf * 8 + jl:hf * 8 + jl + 1], rgat[:])

                @block.tensor
                def _(t, ri=ri):
                    EV["stTevac"].wait(t, ri)
                    if ri > 0:
                        EV["b2acc"].wait(t, 2 * ri - 1)
                    for seg in range(2):
                        EV["b2f"].inc(t.matmul(PP[seg][:], ST_sb[:],
                                               b2_sb[:, seg * 512:(seg + 1) * 512],
                                               start=True, stop=True))

                @block.vector
                def _(v, jl=jl, ri=ri):
                    for seg in range(2):
                        EV["b2f"].wait(v, 2 * ri + seg)
                        sl = slice(seg * 512, (seg + 1) * 512)
                        EV["b2acc"].inc(v.tensor_tensor(
                            acc[:, jl, sl], acc[:, jl, sl], PP[seg][:], ALU.add))

            if hf == 0:
                @block.sync
                def _(sync):
                    EV["mTw"].wait(sync, JT_HALF - 1)
                    EV["mst"].inc(sync.dma_start(out=maskT_hbm, in_=maskT[:, 0:HT]))

            @block.sync
            def _(sync, hf=hf):
                EV["b2acc"].wait(sync, 2 * rt_count[0] - 1)
                EV["store"].inc(sync.dma_start(
                    out=out.rearrange("j p d -> p j d")[:, hf * 8:(hf + 1) * 8, :],
                    in_=acc[:]))

        @block.sync
        def _(sync):
            EV["xcw"].wait(sync, xcw_of_xtile[15])
            EV["mldT"].inc(sync.dma_start(out=maskT[:, 0:HT], in_=maskT_hbm))

        @block.vector
        def _(v):
            EV["mldT"].wait(v, 0)
            EV["scand"].inc(v.tensor_tensor_scan(
                csum[:], maskT[:], zer81.ap().broadcast_to([8, 2 * HT]), 0.0,
                op0=ALU.add, op1=ALU.add))
            EV["scand"].wait(v, 0)
            v.tensor_scalar(out=csum[:], in0=csum[:], scalar1=-1.0,
                            scalar2=None, op0=ALU.add)
            EV["csm1"].inc(v.tensor_tensor(mp_f16[:], csum[:], maskT[:], ALU.mult))

        @block.tensor
        def _(t):
            EV["csm1"].wait(t, 0)
            EV["rlogv"].wait(t, rt_count[0] - 1)
            for ck in range(4):
                if ck >= 2:
                    EV["gevac"].wait(t, ck - 2)
                t.matmul(SB_ps[ck % 2][0:1, 0:512], evc_sb[:, 1:2],
                         maskT[:, ck * 512:(ck + 1) * 512],
                         start=True, stop=False)
                EV["gmm"].inc(t.matmul(SB_ps[ck % 2][0:1, 0:512], evc_sb[:, 0:1],
                                       mp_f16[:, ck * 512:(ck + 1) * 512],
                                       start=False, stop=True))

        for ck in range(4):
            @block.vector
            def _(v, ck=ck):
                EV["gmm"].wait(v, ck)
                EV["gevac"].inc(v.tensor_copy(g_stage[:], SB_ps[ck % 2][0:1, 0:512]))

            @block.sync
            def _(sync, ck=ck):
                EV["gevac"].wait(sync, ck)
                EV["gtr"].inc(sync.dma_start(
                    out=g_hbm[ck * 512:(ck + 1) * 512, :]
                        .rearrange("(o f) o2 -> o (f o2)", o=1),
                    in_=g_stage[:]))
            @block.vector
            def _(v, ck=ck):
                EV["gtr"].wait(v, ck)

        @block.gpsimd
        def _(g):
            EV["gtr"].wait(g, 3)
            EV["gsp"].inc(g.dma_start(
                out=g_f[:],
                in_=g_hbm.rearrange("(j p) o -> p (j o)", p=128),
                allow_slow_non_contiguous=True))

        @block.vector
        def _(v):
            EV["gsp"].wait(v, 0)
            EV["gidx"].inc(v.tensor_copy(g_i[:], g_f[:]))
            EV["gpe"].inc(v.engine_nop())

        @block.gpsimd
        def _(g):
            EV["fmem"].wait(g, 0)
            EV["tfill"].inc(g.dma_start(
                out=tok_tbl.rearrange("(p c) o -> p (c o)", c=32), in_=fill_i[:]))
            EV["tfill"].inc(g.dma_start(
                out=gate_tbl.rearrange("(p c) o -> p (c o)", c=32), in_=fill_f[:]))
            EV["tfill"].wait(g, 1)
            EV["gidx"].wait(g, 0)
            for c in range(16):
                EV["scat"].inc(g.indirect_dma_start(
                    out=tok_tbl,
                    out_offset=IndirectOffsetOnAxis(ap=g_i[:, c:c + 1], axis=0),
                    in_=iota_sb[:, c:c + 1], in_offset=None))
                EV["scat"].inc(g.indirect_dma_start(
                    out=gate_tbl,
                    out_offset=IndirectOffsetOnAxis(ap=g_i[:, c:c + 1], axis=0),
                    in_=gate_tok[:, c:c + 1], in_offset=None))

        @block.sync
        def _(sync):
            EV["scat"].wait(sync, 31)
            EV["traw"].inc(sync.dma_start(
                out=tok_raw[:], in_=tok_tbl.rearrange("(c p) o -> c (p o)", p=128)))
            EV["traw"].inc(sync.dma_start(
                out=gate_raw[:], in_=gate_tbl.rearrange("(c p) o -> c (p o)", p=128)))

        @block.tensor
        def _(t):
            EV["traw"].wait(t, 1)
            EV["b2acc"].wait(t, 2 * rt_count[0] - 1)
            for k, raw in enumerate((tok_raw, gate_raw)):
                ins = t.matmul(PP[0][:, k * 32:(k + 1) * 32], raw[:],
                               idf32[:], is_transpose=True,
                               start=(k == 0), stop=(k == 1))
            EV["ttr"].inc(ins)

        @block.vector
        def _(v):
            EV["ttr"].wait(v, 0)
            v.tensor_copy(tok_f[:], PP[0][:, 0:32])
            v.tensor_copy(gate_slots[:], PP[0][:, 32:64])
            EV["tbl"].inc(v.tensor_copy(tok_i[:], tok_f[:]))

        @block.vector
        def _(v):
            EV["sqS"].wait(v, EV["sqS"].n - 1)
            EV["sqVG"].wait(v, EV["sqVG"].n - 1)
            EV["xgini"].inc(v.memset(xg[:], 0.0))

        wl_ffn_base = EV["wload"].n
        for e in range(NE):
            @block.gpsimd
            def _(g, e=e):
                EV["tbl"].wait(g, 0)
                EV["xw"].wait(g, EV["xw"].n - 1)
                if e == 0:
                    EV["xgini"].wait(g, 0)
                if (e - 1) in xg_free:
                    EV["tp"].wait(g, xg_free[e - 1])
                for tile in range(4):
                    EV["xgdma"].inc(g.indirect_dma_start(
                        out=xg[:, tile], out_offset=None,
                        in_=xln2_hbm,
                        in_offset=IndirectOffsetOnAxis(
                            ap=tok_i[:, e * 4 + tile:e * 4 + tile + 1], axis=0),
                        bounds_check=bc["g"], oob_is_err=False))

            @block.tensor
            def _(t, e=e):
                if e >= 1:
                    EV["hfill"].wait(t, (e * 4) * 8 - 1)
                for tile in range(4):
                    EV["xgdma"].wait(t, e * 4 + tile)
                    for half4 in range(4):
                        m = tp_count[0]
                        tp_count[0] += 1
                        if m > 0:
                            EV["tpevac"].wait(t, m - 1)
                        for i2 in range(2):
                            fc = half4 * 2 + i2
                            ins = t.matmul(tp_ps[:, i2 * 128:(i2 + 1) * 128],
                                           xg[:, tile, fc * 128:(fc + 1) * 128],
                                           ident[:], is_transpose=True,
                                           start=(i2 == 0), stop=(i2 == 1))
                        EV["tp"].inc(ins)
            xg_free[e] = tp_count[0] - 1

            tpe_hold = tp_count[0]

            @block.vector
            def _(v, e=e, tpe_hold=tpe_hold):
                for k in range(16):
                    m = tpe_hold - 16 + k
                    tile, half4 = divmod(k, 4)
                    EV["tp"].wait(v, m)
                    EV["tpevac"].inc(v.tensor_copy(
                        xgT[:, half4 * 2:(half4 + 1) * 2,
                            tile * 128:(tile + 1) * 128],
                        tp_ps[:]))

            for qt in range(NQT):
                it = it_count[0]
                it_count[0] += 1
                buf = it % 2
                W1q = WA[buf * 2]
                W2q = WA[buf * 2 + 1]
                w1_ord = wl_ffn_base + it * 2
                w2_ord = w1_ord + 1

                @block.gpsimd
                def _(g, e=e, qt=qt, it=it, W1q=W1q, W2q=W2q):
                    if it == 0:
                        EV["qkvfill"].wait(g, qkv_count[0] - 1)
                        EV["wofill"].wait(g, wo_count[0] - 1)
                    if it >= 2:
                        EV["yfill"].wait(g, (it - 1) * 8 - 1)
                    EV["wload"].inc(g.dma_start(
                        out=W1q[:],
                        in_=W1[e, :, qt * 1024:(qt + 1) * 1024]
                            .rearrange("(fc p) o -> p fc o", p=128)))
                    EV["wload"].inc(g.dma_start(
                        out=W2q[:],
                        in_=W2[e, qt * 1024:(qt + 1) * 1024, :]
                            .rearrange("(fc p) o -> p fc o", p=128)))

                @block.tensor
                def _(t, e=e, qt=qt, it=it, W1q=W1q, W2q=W2q,
                      w1_ord=w1_ord, w2_ord=w2_ord, tpe_hold=tpe_hold):
                    EV["wload"].wait(t, w1_ord)
                    if it == 0:
                        EV["gidx"].wait(t, 0)
                    if qt == 0:
                        EV["tpevac"].wait(t, tpe_hold - 1)
                    for m in range(8):
                        hi = h_count[0]
                        h_count[0] += 1
                        if hi >= 2:
                            EV["hevac"].wait(t, hi - 2)
                        hb = SB_ps[hi % 2]
                        for fc in range(8):
                            ins = t.matmul(hb[:, 0:CAP],
                                           W1q[:, fc, m * 128:(m + 1) * 128],
                                           xgT[:, fc, :],
                                           start=(fc == 0), stop=(fc == 7))
                        EV["hfill"].inc(ins)
                    EV["wload"].wait(t, w2_ord)
                    if it == 0:
                        EV["tbl"].wait(t, 0)
                    for tile in range(4):
                        for dseg in range(2):
                            yi = y_count[0]
                            y_count[0] += 1
                            if yi >= 2:
                                EV["yadd"].wait(t, yi - 2)
                            yb = PP[yi % 2]
                            EV["hevac"].wait(t, it * 8 + 7)
                            for Fc in range(8):
                                ins = t.matmul(
                                    yb[:],
                                    h_sb[:, Fc, tile * 128:(tile + 1) * 128],
                                    W2q[:, Fc, dseg * 512:(dseg + 1) * 512],
                                    start=(Fc == 0), stop=(Fc == 7))
                            EV["yfill"].inc(ins)

                @block.scalar
                def _(s, e=e, qt=qt, it=it):
                    if it > 0:
                        EV["yfill"].wait(s, it * 8 - 1)
                    for m in range(8):
                        hi = it * 8 + m
                        EV["hfill"].wait(s, hi)
                        EV["hevac"].inc(s.activation(
                            h_sb[:, m, :], SB_ps[hi % 2][:, 0:CAP], AF.Gelu,
                            bias=b1_sb[:, e, qt * 8 + m:qt * 8 + m + 1]))

                @block.vector
                def _(v, e=e, qt=qt, it=it):
                    for tile in range(4):
                        for dseg in range(2):
                            yi = it * 8 + tile * 2 + dseg
                            if qt == 0 and tile == 0 and dseg == 0 and e >= 1:
                                EV["yscat"].wait(v, e * 4 - 1)
                            EV["yfill"].wait(v, yi)
                            sl = slice(dseg * 512, (dseg + 1) * 512)
                            if qt == 0:
                                ins = v.tensor_copy(y_stage[:, tile, sl],
                                                    PP[yi % 2][:])
                            else:
                                ins = v.tensor_tensor(y_stage[:, tile, sl],
                                                      y_stage[:, tile, sl],
                                                      PP[yi % 2][:], ALU.add)
                            EV["yadd"].inc(ins)

            @block.vector
            def _(v, e=e):
                for tile in range(4):
                    EV["ysc"].inc(v.tensor_scalar_mul(
                        y_stage[:, tile], y_stage[:, tile],
                        gate_slots[:, e * 4 + tile:e * 4 + tile + 1]))

            @block.gpsimd
            def _(g, e=e):
                EV["ysc"].wait(g, e * 4 + 3)
                if e == 0:
                    EV["store"].wait(g, 1)
                for tile in range(4):
                    EV["yscat"].inc(g.indirect_dma_start(
                        out=out.rearrange("j p d -> (j p) d"),
                        out_offset=IndirectOffsetOnAxis(
                            ap=tok_i[:, e * 4 + tile:e * 4 + tile + 1], axis=0),
                        in_=y_stage[:, tile], in_offset=None,
                        compute_op=ALU.add, bounds_check=bc["s"],
                        oob_is_err=False))

        @block.sync
        def _(sync):
            for name in DMA_EV:
                if EV[name].n:
                    sync.wait_ge(EV[name].h, EV[name].step * EV[name].n)
    return nc


_NC_CACHE = {}


def _get_nc():
    if "nc" not in _NC_CACHE:
        _NC_CACHE["nc"] = build()
    return _NC_CACHE["nc"]


def kernel(L0_attn_hidden_state, L0_attn_Wq, L0_attn_bq, L0_attn_Wk, L0_attn_bk,
           L0_attn_Wv, L0_attn_bv, L0_attn_Wo, L0_attn_bo,
           L0_ffn_W1, L0_ffn_b1, L0_ffn_W2, L0_ffn_b2, L0_ffn_Wr, L0_ffn_br):
    x = np.ascontiguousarray(np.asarray(L0_attn_hidden_state, dtype=np.float32))
    B, S, _ = x.shape
    nc = _get_nc()

    ident = np.eye(128, dtype=bfd)
    ii, jj = np.meshgrid(np.arange(128), np.arange(128), indexing="ij")
    mask8 = np.where(ii % 8 == jj % 8, 0.0, -1e30).astype(np.float32)
    idf32 = np.eye(32, dtype=np.float32)
    evc = np.stack([np.ones(8), np.arange(8) * 512.0], axis=1).astype(np.float16)
    iota = (np.arange(16)[None, :] * 128 + np.arange(128)[:, None]).astype(np.float32)

    common = dict(
        Wq=np.ascontiguousarray(L0_attn_Wq, np.float32),
        Wk=np.ascontiguousarray(L0_attn_Wk, np.float32),
        Wv=np.ascontiguousarray(L0_attn_Wv, np.float32),
        Wo=np.ascontiguousarray(L0_attn_Wo, np.float32),
        bq=np.ascontiguousarray(L0_attn_bq, np.float32),
        bk=np.ascontiguousarray(L0_attn_bk, np.float32),
        bv=np.ascontiguousarray(L0_attn_bv, np.float32),
        bo=np.ascontiguousarray(L0_attn_bo, np.float32),
        W1=np.ascontiguousarray(L0_ffn_W1, np.float32),
        b1=np.ascontiguousarray(L0_ffn_b1, np.float32),
        W2=np.ascontiguousarray(L0_ffn_W2, np.float32),
        b2=np.ascontiguousarray(L0_ffn_b2, np.float32),
        Wr=np.ascontiguousarray(L0_ffn_Wr, np.float32),
        br=np.ascontiguousarray(L0_ffn_br, np.float32),
        ident_hbm=ident, mask_hbm=mask8,
        idf32_hbm=idf32, evc_hbm=evc, iota_hbm=iota,
    )

    in_maps = []
    for c in range(8):
        b, c2 = divmod(c, 2)
        x_own = np.ascontiguousarray(x[b, 2048 * c2: 2048 * (c2 + 1)])
        xr = x[b].reshape(16, 2, 16, 8, DM)[:, c2]
        x_res = np.ascontiguousarray(
            xr.transpose(1, 0, 2, 3).reshape(16, 128, DM))
        im = dict(common)
        im["x_own"] = x_own
        im["x_res"] = x_res
        in_maps.append(im)

    res = run_bass_kernel_spmd(nc, in_maps, core_ids=list(range(8)))

    out = np.empty((B, S, DM), dtype=np.float32)
    for c in range(8):
        b, c2 = divmod(c, 2)
        oc = res.results[c]["out"]
        ov = oc.reshape(16, 16, 8, DM).transpose(1, 0, 2, 3)
        out[b].reshape(16, 2, 16, 8, DM)[:, c2] = ov
    return out
